# version 01
# baseline (speedup 1.0000x reference)
import numpy as np
import ml_dtypes
import concourse.bass as bass
import concourse.mybir as mybir
from concourse.bass import IndirectOffsetOnAxis
from concourse.bass_utils import run_bass_kernel_spmd
from contextlib import ExitStack

F32 = mybir.dt.float32
BF16 = mybir.dt.bfloat16
I32 = mybir.dt.int32
U32 = mybir.dt.uint32
AF = mybir.ActivationFunctionType
ALU = mybir.AluOpType
AX = mybir.AxisListType

B, S, D = 2, 8192, 1024
NT = B * S
H, DH = 8, 128
EPS_LN = 1e-5
EPS_GN = 1e-6
E0, K0, F0, C0 = 64, 8, 768, 512
SF = 768
D_INNER, D_STATE, D_CONV = 2048, 16, 4
E1, K1, H1 = 8, 2, 4096
OWN = S
QT = 2048

GAMMAS = [1.0 - 2.0 ** (-5.0 - h) for h in range(H)]


class Script:
    def __init__(self):
        self.items = []
        self.setups = {}

    def setup(self, eng, fn):
        self.setups.setdefault(eng, []).append(fn)

    def op(self, eng, fn, inc=1):
        self.items.append((eng, fn, inc))

    def dma(self, fn, q="sync"):
        self.op(q, fn, 16)

    EPOCH_LIMIT = 24000

    def plan(self):
        self.sched = []
        bounds = []
        ep, v = 0, 0
        for e, fn, inc in self.items:
            if v + inc > self.EPOCH_LIMIT:
                bounds.append(v)
                ep += 1
                v = 0
            self.sched.append((ep, v))
            v += inc
        bounds.append(v)
        return bounds

    def emit(self, block, sems):
        bounds = self.plan()
        assert len(bounds) <= len(sems), f"need {len(bounds)} ticket sems"
        engines = ["sync", "tensor", "vector", "scalar", "gpsimd"]
        for name in engines:
            def make(name):
                def body(eng):
                    for f in self.setups.get(name, []):
                        f(eng)
                    cur_ep = 0
                    for (e, fn, inc), (ep, v) in zip(self.items, self.sched):
                        if e != name:
                            continue
                        while cur_ep < ep:
                            eng.wait_ge(sems[cur_ep], bounds[cur_ep])
                            cur_ep += 1
                        eng.wait_ge(sems[ep], v)
                        inst = fn(eng)
                        inst.then_inc(sems[ep], inc)
                    while cur_ep < len(bounds) - 1:
                        eng.wait_ge(sems[cur_ep], bounds[cur_ep])
                        cur_ep += 1
                    eng.wait_ge(sems[cur_ep], bounds[cur_ep])
                return body
            getattr(block, name)(make(name))


class Arena:

    def __init__(self, tile, cols):
        self.tile = tile
        self.cols = cols
        self.off = 0
        self.marks = []

    ALIGN = 32

    def alloc(self, n):
        self.off = (self.off + self.ALIGN - 1) // self.ALIGN * self.ALIGN
        assert self.off + n <= self.cols, f"arena overflow {self.off}+{n}>{self.cols}"
        v = self.tile[:, self.off:self.off + n]
        self.off += n
        return v

    def push(self):
        self.marks.append(self.off)

    def pop(self):
        self.off = self.marks.pop()


def ceil_div(a, b):
    return (a + b - 1) // b


def emit_stage3(sc, ps, AF32, ABF, xsb, d, ln_helper):
    IOA = IndirectOffsetOnAxis
    REG = sc.REG
    AF32.push()
    ABF.push()

    acol = AF32.alloc(256)
    cw = AF32.alloc(64)
    cb = AF32.alloc(16)
    bdt = AF32.alloc(16)
    dpar = AF32.alloc(16)
    qoff = AF32.alloc(1)
    iotaf2 = AF32.alloc(1)
    idxf2 = AF32.alloc(1)
    onesc = AF32.alloc(1)
    ones16 = AF32.alloc(1)
    states = AF32.alloc(256)
    wdt = ABF.alloc(2048)
    onesr = ABF.alloc(128)
    onescol = ABF.alloc(1)
    alog_s = AF32.alloc(256)
    idxi2 = xsb("s3idxi", [128, 1], I32)
    qoffi = xsb("s3qoffi", [128, 1], I32)

    sc.dma(lambda e: e.dma_start(out=wdt[0:1, :], in_=d["W_dt"]), q="gpsimd")
    sc.dma(lambda e: e.dma_start(
        out=cw.rearrange("p (c o) -> p c o", c=16),
        in_=d["conv_w"].rearrange("(c p) o -> p c o", p=128)))
    sc.dma(lambda e: e.dma_start(
        out=cb,
        in_=d["conv_b"].rearrange("(c p) -> p c", p=128),
        allow_slow_non_contiguous=True))
    sc.dma(lambda e: e.dma_start(
        out=bdt,
        in_=d["b_dt"].rearrange("(c p) -> p c", p=128),
        allow_slow_non_contiguous=True))
    sc.dma(lambda e: e.dma_start(
        out=alog_s.rearrange("p (c n) -> p c n", c=16),
        in_=d["A_log"].rearrange("(c p) n -> p c n", p=128)))
    sc.dma(lambda e: e.dma_start(
        out=dpar,
        in_=d["D_param"].rearrange("(c p) -> p c", p=128),
        allow_slow_non_contiguous=True))
    sc.dma(lambda e: e.dma_start(out=qoff[:, 0:1], in_=d["qoff_f"]))
    sc.op("scalar", lambda e: e.activation(out=acol, in_=alog_s, func=AF.Exp))
    sc.op("vector", lambda e: e.tensor_scalar(out=acol, in0=acol,
                                              scalar1=-1.0, scalar2=None, op0=ALU.mult))
    sc.op("vector", lambda e: e.memset(states, 0.0))
    sc.op("vector", lambda e: e.memset(onesc, 1.0))
    sc.op("vector", lambda e: e.memset(ones16, 1.0))
    sc.op("vector", lambda e: e.memset(onesr[0:1, :], 1.0))
    sc.op("vector", lambda e: e.memset(onescol, 1.0))
    sc.op("gpsimd", lambda e: e.iota(iotaf2, [[1, 1]], base=0, channel_multiplier=1,
                                     allow_small_or_imprecise_dtypes=True))

    AF32.push(); ABF.push()
    win = ABF.alloc(8 * 4096)
    lnfm = ABF.alloc(8 * 512)
    ln3blk = ABF.alloc(1024)
    xzt = ABF.alloc(4096)
    y1blk = AF32.alloc(1024)
    sc.dma(lambda e: e.dma_start(
        out=win.rearrange("p (c o) -> p c o", c=8),
        in_=d["W_in"].rearrange("(c p) o -> p c o", p=128)), q="gpsimd")
    for bo in range(OWN // 512):
        o0 = bo * 512
        for tt in range(4):
            sc.dma(lambda e, o0=o0, tt=tt: e.dma_start(
                out=y1blk, in_=d["y1_dram"][o0 + tt * 128:o0 + (tt + 1) * 128, :]))
            ln_helper(y1blk, ln3blk, D)
            sc.dma(lambda e, o0=o0, tt=tt: e.dma_start(
                out=d["ln_dram"][o0 + tt * 128:o0 + (tt + 1) * 128, :], in_=ln3blk))
        for dc in range(8):
            sc.dma(lambda e, o0=o0, dc=dc: e.dma_start(
                out=lnfm[:, dc * 512:(dc + 1) * 512],
                in_=d["ln_dram"][o0:o0 + 512, dc * 128:(dc + 1) * 128],
                transpose=True))
        for tt in range(4):
            for ob in range(8):
                p0 = ps[0]
                for di in range(8):
                    lhs = lnfm[:, di * 512 + tt * 128: di * 512 + tt * 128 + 128]
                    sc.op("tensor", lambda e, lhs=lhs, di=di, ob=ob, p0=p0, win=win: e.matmul(
                        p0[:, 0:512], lhs,
                        win[:, di * 4096 + ob * 512: di * 4096 + (ob + 1) * 512],
                        start=(di == 0), stop=(di == 7)))
                sc.op("vector", lambda e, ob=ob, p0=p0, xzt=xzt: e.tensor_copy(
                    xzt[:, ob * 512:(ob + 1) * 512], p0[:, 0:512]))
            sc.dma(lambda e, o0=o0, tt=tt, xzt=xzt: e.dma_start(
                out=d["xz_dram"][o0 + tt * 128:o0 + (tt + 1) * 128, :], in_=xzt))
    AF32.pop(); ABF.pop()

    AF32.push(); ABF.push()
    xcbuf = ABF.alloc(4 + OWN)
    xcs = ABF.alloc(OWN)
    xcv = AF32.alloc(0)
    for dc in range(16):
        sc.op("vector", lambda e, xcbuf=xcbuf: e.memset(xcbuf[:, 0:4], 0.0))
        for tb in range(16):
            sc.dma(lambda e, dc=dc, tb=tb, xcbuf=xcbuf: e.dma_start(
                out=xcbuf[:, 4 + tb * 512: 4 + (tb + 1) * 512],
                in_=d["xz_dram"][tb * 512:(tb + 1) * 512, dc * 128:(dc + 1) * 128],
                transpose=True))
        sc.op("vector", lambda e, dc=dc, xcbuf=xcbuf, xcs=xcs: e.tensor_scalar(
            out=xcs, in0=xcbuf[:, 4:4 + OWN],
            scalar1=cw[:, dc * 4 + 3:dc * 4 + 4],
            scalar2=cb[:, dc:dc + 1], op0=ALU.mult, op1=ALU.add))
        for k in range(3):
            sc.op("vector", lambda e, dc=dc, k=k, xcbuf=xcbuf, xcs=xcs: e.scalar_tensor_tensor(
                out=xcs, in0=xcbuf[:, 1 + k:1 + k + OWN],
                scalar=cw[:, dc * 4 + k:dc * 4 + k + 1],
                in1=xcs, op0=ALU.mult, op1=ALU.add))
        sc.op("scalar", lambda e, xcs=xcs: e.activation(out=xcs, in_=xcs, func=AF.Silu))
        sc.dma(lambda e, dc=dc, xcs=xcs: e.dma_start(
            out=d["xc_dram"][dc * 128:(dc + 1) * 128, :], in_=xcs))
    AF32.pop(); ABF.pop()

    AF32.push(); ABF.push()
    wx = ABF.alloc(16 * 33)
    xctile = ABF.alloc(16 * 512)
    bsb = ABF.alloc(512)
    csb = AF32.alloc(512)
    eu = ABF.alloc(512)
    dtt = ABF.alloc(512)
    csr = AF32.alloc(512)
    dtr = ABF.alloc(512)
    sc.dma(lambda e: e.dma_start(
        out=wx.rearrange("p (c o) -> p c o", c=16),
        in_=d["W_x"].rearrange("(c p) o -> p c o", p=128)), q="gpsimd")
    for tb in range(16):
        t0 = tb * 512
        for dc in range(16):
            sc.dma(lambda e, dc=dc, t0=t0, xctile=xctile: e.dma_start(
                out=xctile[:, dc * 512:(dc + 1) * 512],
                in_=d["xc_dram"][dc * 128:(dc + 1) * 128, t0:t0 + 512]))
        pB, pC, pD = ps[0], ps[1], ps[2]
        for dc in range(16):
            lhs = xctile[:, dc * 512:(dc + 1) * 512]
            sc.op("tensor", lambda e, lhs=lhs, dc=dc, pB=pB, wx=wx: e.matmul(
                pB[0:16, 0:512], wx[:, dc * 33: dc * 33 + 16], lhs,
                start=(dc == 0), stop=(dc == 15)))
            sc.op("tensor", lambda e, lhs=lhs, dc=dc, pC=pC, wx=wx: e.matmul(
                pC[0:16, 0:512], wx[:, dc * 33 + 16: dc * 33 + 32], lhs,
                start=(dc == 0), stop=(dc == 15)))
            sc.op("tensor", lambda e, lhs=lhs, dc=dc, pD=pD, wx=wx: e.matmul(
                pD[0:1, 0:512], wx[:, dc * 33 + 32: dc * 33 + 33], lhs,
                start=(dc == 0), stop=(dc == 15)))
        sc.op("vector", lambda e, pB=pB, bsb=bsb: e.tensor_copy(
            bsb[0:16, 0:512], pB[0:16, 0:512]))
        sc.dma(lambda e, t0=t0, bsb=bsb: e.dma_start(
            out=d["b_dram"][:, t0:t0 + 512], in_=bsb[0:16, 0:512]))
        sc.op("vector", lambda e, pC=pC, csb=csb: e.tensor_copy(
            csb[0:16, 0:512], pC[0:16, 0:512]))
        p3 = ps[3]
        sc.op("tensor", lambda e, p3=p3, csb=csb: e.matmul(
            p3[0:1, 0:512], ones16[0:16, 0:1], csb[0:16, 0:512],
            start=True, stop=True))
        sc.op("vector", lambda e, p3=p3, csr=csr: e.tensor_copy(
            csr[0:1, 0:512], p3[0:1, 0:512]))
        sc.dma(lambda e, t0=t0, csr=csr: e.dma_start(
            out=d["csum_dram"][0:1, t0:t0 + 512], in_=csr[0:1, 0:512]))
        sc.op("vector", lambda e, pD=pD, dtr=dtr: e.tensor_copy(
            dtr[0:1, 0:512], pD[0:1, 0:512]))
        for dc in range(16):
            p4 = ps[4]
            sc.op("tensor", lambda e, dc=dc, p4=p4, dtr=dtr: e.matmul(
                p4[:, 0:512], wdt[0:1, dc * 128:(dc + 1) * 128],
                dtr[0:1, 0:512], start=True, stop=True))
            sc.op("scalar", lambda e, dc=dc, p4=p4, eu=eu: e.activation(
                out=eu, in_=p4[:, 0:512], func=AF.Exp, bias=bdt[:, dc:dc + 1]))
            sc.op("scalar", lambda e, eu=eu, dtt=dtt: e.activation(
                out=dtt, in_=eu, func=AF.Ln, bias=onesc[:, 0:1]))
            sc.dma(lambda e, dc=dc, t0=t0, dtt=dtt: e.dma_start(
                out=d["dt_dram"][dc * 128:(dc + 1) * 128, t0:t0 + 512], in_=dtt))
    AF32.pop(); ABF.pop()

    AF32.push(); ABF.push()
    TQ = 1024
    dtall = ABF.alloc(16 * TQ)
    uball = ABF.alloc(16 * TQ)
    bb = ABF.alloc(TQ)
    bu = ABF.alloc(TQ)
    scano = ABF.alloc(TQ)
    brow = ABF.alloc(TQ)
    ab = AF32.alloc(TQ)
    ysr = AF32.alloc(512)
    for q in range(OWN // TQ):
        t0 = q * TQ
        for dc in range(16):
            sc.dma(lambda e, dc=dc, t0=t0: e.dma_start(
                out=dtall[:, dc * TQ:(dc + 1) * TQ],
                in_=d["dt_dram"][dc * 128:(dc + 1) * 128, t0:t0 + TQ]))
            sc.dma(lambda e, dc=dc, t0=t0: e.dma_start(
                out=uball[:, dc * TQ:(dc + 1) * TQ],
                in_=d["xc_dram"][dc * 128:(dc + 1) * 128, t0:t0 + TQ]))
            sc.op("vector", lambda e, dc=dc: e.tensor_tensor(
                out=uball[:, dc * TQ:(dc + 1) * TQ],
                in0=uball[:, dc * TQ:(dc + 1) * TQ],
                in1=dtall[:, dc * TQ:(dc + 1) * TQ], op=ALU.mult))
        for n in range(D_STATE):
            sc.dma(lambda e, n=n, t0=t0: e.dma_start(
                out=brow[0:1, 0:TQ], in_=d["b_dram"][n:n + 1, t0:t0 + TQ]))
            pb = ps[5]
            for j in range(TQ // 512):
                sc.op("tensor", lambda e, j=j, pb=pb: e.matmul(
                    pb[:, 0:512], onesr[0:1, 0:128],
                    brow[0:1, j * 512:(j + 1) * 512], start=True, stop=True))
                sc.op("vector", lambda e, j=j, pb=pb: e.tensor_copy(
                    bb[:, j * 512:(j + 1) * 512], pb[:, 0:512]))
            for dc in range(16):
                sci = dc * D_STATE + n
                last = (n == D_STATE - 1 and dc == 15)
                sc.op("scalar", lambda e, dc=dc, n=n: e.activation(
                    out=ab, in_=dtall[:, dc * TQ:(dc + 1) * TQ], func=AF.Exp,
                    scale=acol[:, dc * D_STATE + n: dc * D_STATE + n + 1]))
                sc.op("vector", lambda e, dc=dc: e.tensor_tensor(
                    out=bu, in0=uball[:, dc * TQ:(dc + 1) * TQ], in1=bb, op=ALU.mult))
                sc.op("vector", lambda e, sci=sci: e.tensor_tensor_scan(
                    out=scano, data0=ab, data1=bu,
                    initial=states[:, sci:sci + 1], op0=ALU.mult, op1=ALU.add))
                sc.op("vector", lambda e, sci=sci: e.tensor_copy(
                    states[:, sci:sci + 1], scano[:, TQ - 1:TQ]))
                first = (n == 0 and dc == 0)
                for j in range(TQ // 512):
                    pyj = ps[6] if j == 0 else ps[7]
                    sc.op("tensor", lambda e, j=j, first=first, last=last, pyj=pyj: e.matmul(
                        pyj[0:1, 0:512], onescol[:, 0:1],
                        scano[:, j * 512:(j + 1) * 512],
                        start=first, stop=last))
        for j in range(TQ // 512):
            pyj = ps[6] if j == 0 else ps[7]
            sc.op("vector", lambda e, j=j, pyj=pyj, ysr=ysr: e.tensor_copy(
                ysr[0:1, 0:512], pyj[0:1, 0:512]))
            sc.dma(lambda e, j=j, t0=t0, ysr=ysr: e.dma_start(
                out=d["ysum_dram"][0:1, t0 + j * 512:t0 + (j + 1) * 512],
                in_=ysr[0:1, 0:512]))
    AF32.pop(); ABF.pop()

    AF32.push(); ABF.push()
    yrow = AF32.alloc(64)
    crow = AF32.alloc(64)
    ybb = ABF.alloc(64)
    sc.dma(lambda e: e.dma_start(
        out=yrow, in_=d["ysum_dram"].rearrange("a (p c) -> (a p) c", p=128)))
    sc.dma(lambda e: e.dma_start(
        out=crow, in_=d["csum_dram"].rearrange("a (p c) -> (a p) c", p=128)))
    sc.op("vector", lambda e: e.tensor_tensor(out=yrow, in0=yrow, in1=crow, op=ALU.mult))
    sc.op("vector", lambda e: e.tensor_scalar(out=ybb, in0=yrow,
                                              scalar1=1.0 / D_INNER, scalar2=None,
                                              op0=ALU.mult))
    sc.dma(lambda e: e.dma_start(
        out=d["ybc_dram"].rearrange("a (p c) -> (a p) c", p=128), in_=ybb))
    AF32.pop(); ABF.pop()

    AF32.push(); ABF.push()
    wout = ABF.alloc(16 * 1024)
    xtq = ABF.alloc(16 * QT)
    xzgath = ABF.alloc(4096)
    zfm = ABF.alloc(128)
    zs = ABF.alloc(128)
    yq2 = ABF.alloc(128)
    ybcq = ABF.alloc(QT)
    ybcq16 = ABF.alloc(16)
    y1q = AF32.alloc(1024)
    sc.dma(lambda e: e.dma_start(
        out=wout.rearrange("p (c o) -> p c o", c=16),
        in_=d["W_out"].rearrange("(c p) o -> p c o", p=128)), q="gpsimd")
    sc.op("vector", lambda e: e.tensor_scalar(
        out=idxf2, in0=iotaf2, scalar1=16.0, scalar2=None, op0=ALU.mult))
    sc.op("vector", lambda e: e.tensor_tensor(
        out=idxf2, in0=idxf2, in1=qoff[:, 0:1], op=ALU.add))
    sc.op("vector", lambda e: e.tensor_copy(qoffi[:, 0:1], idxf2))
    sc.dma(lambda e: e.indirect_dma_start(
        out=ybcq16, out_offset=None,
        in_=d["ybc_dram"].rearrange("a (b c) -> (a b) c", c=1),
        in_offset=IOA(ap=qoffi[:, 0:1], axis=0),
        bounds_check=REG["b_own"], oob_is_err=False), q="gpsimd")
    sc.dma(lambda e: e.dma_start(
        out=d["ybcq_dram"].rearrange("a (p c) -> (a p) c", p=128), in_=ybcq16))
    sc.dma(lambda e: e.dma_start(out=ybcq[0:1, 0:QT], in_=d["ybcq_dram"]))
    for i in range(QT // 128):
        sc.op("vector", lambda e, i=i: e.tensor_scalar(
            out=idxf2, in0=iotaf2, scalar1=float(i * 128), scalar2=None, op0=ALU.add))
        sc.op("vector", lambda e: e.tensor_tensor(
            out=idxf2, in0=idxf2, in1=qoff[:, 0:1], op=ALU.add))
        sc.op("vector", lambda e: e.tensor_copy(idxi2[:, 0:1], idxf2))
        sc.dma(lambda e: e.indirect_dma_start(
            out=xzgath, out_offset=None, in_=d["xz_dram"],
            in_offset=IOA(ap=idxi2[:, 0:1], axis=0),
            bounds_check=REG["b_own"], oob_is_err=False), q="gpsimd")
        sc.dma(lambda e, i=i: e.dma_start(
            out=d["zq_dram"][i * 128:(i + 1) * 128, :], in_=xzgath[:, 2048:4096]))
        sc.dma(lambda e: e.indirect_dma_start(
            out=y1q, out_offset=None, in_=d["y1_dram"],
            in_offset=IOA(ap=idxi2[:, 0:1], axis=0),
            bounds_check=REG["b_own"], oob_is_err=False), q="gpsimd")
        sc.dma(lambda e, i=i: e.dma_start(
            out=d["y1q_dram"][i * 128:(i + 1) * 128, :], in_=y1q))
    for dc in range(16):
        sc.op("vector", lambda e, dc=dc: e.tensor_scalar(
            out=idxf2, in0=iotaf2, scalar1=float(dc * 128), scalar2=None, op0=ALU.add))
        sc.op("vector", lambda e: e.tensor_scalar(
            out=idxf2, in0=idxf2, scalar1=float(OWN), scalar2=None, op0=ALU.mult))
        sc.op("vector", lambda e: e.tensor_tensor(
            out=idxf2, in0=idxf2, in1=qoff[:, 0:1], op=ALU.add))
        sc.op("vector", lambda e: e.tensor_copy(idxi2[:, 0:1], idxf2))
        sc.dma(lambda e, dc=dc: e.indirect_dma_start(
            out=xtq[:, dc * QT:(dc + 1) * QT], out_offset=None,
            in_=d["xc_dram"].rearrange("a (b c) -> (a b) c", c=1),
            in_offset=IOA(ap=idxi2[:, 0:1], axis=0),
            bounds_check=REG["b_flat"], oob_is_err=False), q="gpsimd")
    for tq in range(QT // 128):
        pacc0, pacc1 = ps[0], ps[1]
        for dc in range(16):
            pbc = ps[2]
            sc.op("tensor", lambda e, tq=tq, pbc=pbc: e.matmul(
                pbc[:, 0:128], onesr[0:1, 0:128],
                ybcq[0:1, tq * 128:(tq + 1) * 128], start=True, stop=True))
            sc.dma(lambda e, tq=tq, dc=dc: e.dma_start(
                out=zfm, in_=d["zq_dram"][tq * 128:(tq + 1) * 128,
                                          dc * 128:(dc + 1) * 128],
                transpose=True))
            sc.op("scalar", lambda e: e.activation(out=zs, in_=zfm, func=AF.Silu))
            sc.op("vector", lambda e, dc=dc, tq=tq, pbc=pbc: e.scalar_tensor_tensor(
                out=yq2, in0=xtq[:, dc * QT + tq * 128: dc * QT + (tq + 1) * 128],
                scalar=dpar[:, dc:dc + 1], in1=pbc[:, 0:128],
                op0=ALU.mult, op1=ALU.add))
            sc.op("vector", lambda e: e.tensor_tensor(
                out=yq2, in0=yq2, in1=zs, op=ALU.mult))
            for ob in range(2):
                pacc = pacc0 if ob == 0 else pacc1
                sc.op("tensor", lambda e, dc=dc, ob=ob, pacc=pacc: e.matmul(
                    pacc[:, 0:512], yq2,
                    wout[:, dc * 1024 + ob * 512: dc * 1024 + (ob + 1) * 512],
                    start=(dc == 0), stop=(dc == 15)))
        sc.dma(lambda e, tq=tq: e.dma_start(
            out=y1q, in_=d["y1q_dram"][tq * 128:(tq + 1) * 128, :]))
        for ob in range(2):
            pacc = pacc0 if ob == 0 else pacc1
            ysl = y1q[:, ob * 512:(ob + 1) * 512]
            sc.op("vector", lambda e, ysl=ysl, pacc=pacc: e.tensor_tensor(
                out=ysl, in0=ysl, in1=pacc[:, 0:512], op=ALU.add))
        sc.dma(lambda e, tq=tq: e.dma_start(
            out=d["y2_dram"][tq * 128:(tq + 1) * 128, :], in_=y1q))
    AF32.pop(); ABF.pop()
    AF32.pop()
    ABF.pop()


def emit_stage4(sc, ps, AF32, ABF, d, ln_helper):
    AF32.push()
    ABF.push()
    wg2 = ABF.alloc(8 * 8)
    bg2b = AF32.alloc(8)
    b2sb = ABF.alloc(1024)
    lgt = AF32.alloc(8)
    m8 = AF32.alloc(8)
    e8 = AF32.alloc(8)
    oh0 = AF32.alloc(8)
    oh1 = AF32.alloc(8)
    nm1 = AF32.alloc(1)
    den = AF32.alloc(1)
    w1c = AF32.alloc(1)
    w2c = AF32.alloc(1)
    me_all = AF32.alloc(16 * 8)
    me_em = ABF.alloc(2048)
    ln4blk = ABF.alloc(1024)
    y2blk = AF32.alloc(1024)
    lnfm = ABF.alloc(8 * 512)
    w1sb = ABF.alloc(8 * 4096)
    w2ring = ABF.alloc(1024)
    ghall = ABF.alloc(32 * 512)
    b1col = AF32.alloc(32)
    accsl = AF32.alloc(512)
    fin = AF32.alloc(1024)

    sc.dma(lambda e: e.dma_start(
        out=wg2.rearrange("p (c o) -> p c o", c=8),
        in_=d["Wg2"].rearrange("(c p) o -> p c o", p=128)), q="gpsimd")
    sc.dma(lambda e: e.dma_start(out=bg2b, in_=d["bg2_b"]))
    sc.dma(lambda e: e.dma_start(out=b2sb[0:8, :], in_=d["b2"]), q="gpsimd")

    for ex in range(E1):
        sc.dma(lambda e, ex=ex: e.dma_start(
            out=w1sb.rearrange("p (a o) -> p a o", a=32),
            in_=d["W2"][ex].rearrange("(a p) o -> p a o", p=128)), q="gpsimd")
        sc.dma(lambda e, ex=ex: e.dma_start(
            out=d["w2bf_dram"][ex].rearrange("(a p) o -> p a o", p=128),
            in_=w1sb.rearrange("p (a o) -> p a o", a=32)))

    for i in range(QT // 128):
        sc.dma(lambda e, i=i: e.dma_start(
            out=y2blk, in_=d["y2_dram"][i * 128:(i + 1) * 128, :]))
        ln_helper(y2blk, ln4blk, D)
        sc.dma(lambda e, i=i: e.dma_start(
            out=d["ln_dram"][i * 128:(i + 1) * 128, :], in_=ln4blk))
    for bo in range(QT // 512):
        for dc in range(8):
            sc.dma(lambda e, bo=bo, dc=dc: e.dma_start(
                out=lnfm[:, dc * 512:(dc + 1) * 512],
                in_=d["ln_dram"][bo * 512:(bo + 1) * 512, dc * 128:(dc + 1) * 128],
                transpose=True))
        for tt in range(4):
            i = bo * 4 + tt
            p0 = ps[0]
            for di in range(8):
                lhs = lnfm[:, di * 512 + tt * 128: di * 512 + tt * 128 + 128]
                sc.op("tensor", lambda e, lhs=lhs, di=di, p0=p0: e.matmul(
                    p0[:, 0:8], lhs, wg2[:, di * 8:(di + 1) * 8],
                    start=(di == 0), stop=(di == 7)))
            sc.op("vector", lambda e, p0=p0: e.tensor_copy(lgt, p0[:, 0:8]))
            sc.op("vector", lambda e: e.tensor_tensor(out=lgt, in0=lgt, in1=bg2b,
                                                      op=ALU.add))
            sc.op("vector", lambda e: e.max(out=m8, in_=lgt))
            sc.op("vector", lambda e: e.tensor_scalar(
                out=nm1, in0=m8[:, 0:1], scalar1=-1.0, scalar2=None, op0=ALU.mult))
            sc.op("scalar", lambda e: e.activation(out=e8, in_=m8, func=AF.Exp,
                                                   bias=nm1[:, 0:1]))
            sc.op("vector", lambda e: e.reduce_sum(out=den, in_=e8, axis=AX.X))
            sc.op("vector", lambda e: e.reciprocal(out=den, in_=den))
            sc.op("vector", lambda e: e.tensor_scalar(
                out=w1c, in0=e8[:, 0:1], scalar1=den[:, 0:1], scalar2=None,
                op0=ALU.mult))
            sc.op("vector", lambda e: e.tensor_scalar(
                out=w2c, in0=e8[:, 1:2], scalar1=den[:, 0:1], scalar2=None,
                op0=ALU.mult))
            sc.op("vector", lambda e: e.tensor_scalar(
                out=oh0, in0=lgt, scalar1=m8[:, 0:1], scalar2=None, op0=ALU.is_equal))
            sc.op("vector", lambda e: e.tensor_scalar(
                out=oh1, in0=lgt, scalar1=m8[:, 1:2], scalar2=None, op0=ALU.is_equal))
            mei = me_all[:, i * 8:(i + 1) * 8]
            sc.op("vector", lambda e, mei=mei: e.tensor_scalar(
                out=mei, in0=oh0, scalar1=w1c[:, 0:1], scalar2=None, op0=ALU.mult))
            sc.op("vector", lambda e, mei=mei: e.scalar_tensor_tensor(
                out=mei, in0=oh1, scalar=w2c[:, 0:1], in1=mei,
                op0=ALU.mult, op1=ALU.add))
            p1 = ps[1]
            sc.op("tensor", lambda e, mei=mei, p1=p1: e.transpose(
                p1[0:8, 0:128], mei, d["idf_sb"]))
            sc.op("vector", lambda e, i=i, p1=p1: e.tensor_copy(
                me_em[0:8, i * 128:(i + 1) * 128], p1[0:8, 0:128]))

    for ex in range(E1):
        sc.dma(lambda e, ex=ex: e.dma_start(
            out=w1sb.rearrange("p (c o) -> p c o", c=8),
            in_=d["W1"][ex].rearrange("(c p) o -> p c o", p=128)), q="gpsimd")
        for hc in range(32):
            sc.dma(lambda e, ex=ex, hc=hc: e.dma_start(
                out=b1col[:, hc:hc + 1],
                in_=d["b1"][ex:ex + 1, hc * 128:(hc + 1) * 128]))
        for bo in range(QT // 512):
            for dc in range(8):
                sc.dma(lambda e, bo=bo, dc=dc: e.dma_start(
                    out=lnfm[:, dc * 512:(dc + 1) * 512],
                    in_=d["ln_dram"][bo * 512:(bo + 1) * 512, dc * 128:(dc + 1) * 128],
                    transpose=True))
            for hc in range(32):
                p0 = ps[0]
                for di in range(8):
                    lhs = w1sb[:, di * 4096 + hc * 128: di * 4096 + (hc + 1) * 128]
                    sc.op("tensor", lambda e, lhs=lhs, di=di, bo=bo, p0=p0: e.matmul(
                        p0[:, 0:512], lhs, lnfm[:, di * 512:(di + 1) * 512],
                        start=(di == 0), stop=(di == 7)))
                sc.op("scalar", lambda e, hc=hc, p0=p0: e.activation(
                    out=ghall[:, hc * 512:(hc + 1) * 512], in_=p0[:, 0:512],
                    func=AF.Gelu, bias=b1col[:, hc:hc + 1]))
            for hc in range(32):
                sc.dma(lambda e, ex=ex, hc=hc: e.dma_start(
                    out=w2ring, in_=d["w2bf_dram"][ex, hc * 128:(hc + 1) * 128, :]))
                for tt in range(4):
                    for obk in range(2):
                        pacc = ps[tt * 2 + obk]
                        lhs = ghall[:, hc * 512 + tt * 128: hc * 512 + tt * 128 + 128]
                        sc.op("tensor", lambda e, lhs=lhs, hc=hc, obk=obk, pacc=pacc: e.matmul(
                            pacc[:, 0:512], lhs, w2ring[:, obk * 512:(obk + 1) * 512],
                            start=(hc == 0), stop=(hc == 31)))
            for tt in range(4):
                i = bo * 4 + tt
                for obk in range(2):
                    pacc = ps[tt * 2 + obk]
                    if ex > 0:
                        sc.dma(lambda e, i=i, obk=obk: e.dma_start(
                            out=accsl,
                            in_=d["acc_dram"][i * 128:(i + 1) * 128,
                                              obk * 512:(obk + 1) * 512]))
                        sc.op("vector", lambda e, i=i, ex=ex, pacc=pacc: e.scalar_tensor_tensor(
                            out=accsl, in0=pacc[:, 0:512],
                            scalar=me_all[:, i * 8 + ex:i * 8 + ex + 1],
                            in1=accsl, op0=ALU.mult, op1=ALU.add))
                    else:
                        sc.op("vector", lambda e, i=i, ex=ex, pacc=pacc: e.tensor_scalar(
                            out=accsl, in0=pacc[:, 0:512],
                            scalar1=me_all[:, i * 8 + ex:i * 8 + ex + 1],
                            scalar2=None, op0=ALU.mult))
                    sc.dma(lambda e, i=i, obk=obk: e.dma_start(
                        out=d["acc_dram"][i * 128:(i + 1) * 128,
                                          obk * 512:(obk + 1) * 512], in_=accsl))

    for i in range(QT // 128):
        sc.dma(lambda e, i=i: e.dma_start(
            out=fin, in_=d["acc_dram"][i * 128:(i + 1) * 128, :]))
        sc.dma(lambda e, i=i: e.dma_start(
            out=y2blk, in_=d["y2_dram"][i * 128:(i + 1) * 128, :]))
        for obk in range(2):
            p2 = ps[0]
            sc.op("tensor", lambda e, i=i, obk=obk, p2=p2: e.matmul(
                p2[:, 0:512], me_em[0:8, i * 128:(i + 1) * 128],
                b2sb[0:8, obk * 512:(obk + 1) * 512], start=True, stop=True))
            fs = fin[:, obk * 512:(obk + 1) * 512]
            sc.op("vector", lambda e, fs=fs, p2=p2: e.tensor_tensor(
                out=fs, in0=fs, in1=p2[:, 0:512], op=ALU.add))
            ys = y2blk[:, obk * 512:(obk + 1) * 512]
            sc.op("vector", lambda e, fs=fs, ys=ys: e.tensor_tensor(
                out=fs, in0=fs, in1=ys, op=ALU.add))
        sc.dma(lambda e, i=i: e.dma_start(
            out=d["out_final"][i * 128:(i + 1) * 128, :], in_=fin))
    AF32.pop()
    ABF.pop()


def emit_stage2(sc, ps, s2, d, ln_helper, stat, eps_ln_c):
    NBLK = NT // 512
    IOA = IndirectOffsetOnAxis

    sc.op("vector", lambda e: e.memset(s2["ones64"], 1.0))
    sc.op("vector", lambda e: e.memset(s2["big64"], 1e9))
    sc.op("vector", lambda e: e.memset(s2["cnt"], 0.0))
    sc.op("vector", lambda e: e.memset(s2["ones1"][0:1, :], 1.0))
    sc.op("vector", lambda e: e.memset(s2["zero8k"], 0.0))
    sc.op("gpsimd", lambda e: e.iota(s2["e512"], [[1, 1]], base=0, channel_multiplier=512,
                                     allow_small_or_imprecise_dtypes=True))
    sc.op("gpsimd", lambda e: e.iota(s2["iotaf"], [[1, 1]], base=0, channel_multiplier=1,
                                     allow_small_or_imprecise_dtypes=True))
    ownb = s2["ownb"][:, 0:1]
    sc.dma(lambda e: e.dma_start(out=ownb, in_=d["own_base_f"]))
    sc.dma(lambda e: e.dma_start(
        out=s2["rw_sb"].rearrange("p (c o) -> p c o", c=8),
        in_=d["rwT"].rearrange("(c p) o -> p c o", p=128)), q="gpsimd")
    for a in range(128):
        sc.dma(lambda e, a=a: e.dma_start(
            out=d["xbuf_dram"][a * 256:(a + 1) * 256, :].rearrange(
                "(p q) d -> p q d", p=128),
            in_=s2["zero8k"].rearrange("p (q d) -> p q d", q=2)))
    sc.dma(lambda e: e.dma_start(
        out=s2["wsgu"].rearrange("p (c o) -> p c o", c=8),
        in_=d["wsgu"].rearrange("(c p) o -> p c o", p=128)), q="gpsimd")
    sc.dma(lambda e: e.dma_start(
        out=s2["wsd"].rearrange("p (c o) -> p c o", c=6),
        in_=d["wsd"].rearrange("(c p) o -> p c o", p=128)), q="gpsimd")

    for blk in range(NBLK):
        t0 = blk * 512
        for tt in range(4):
            sc.dma(lambda e, t0=t0, tt=tt: e.dma_start(
                out=s2["y0blk"],
                in_=d["y0_dram"][t0 + tt * 128:t0 + (tt + 1) * 128, :]))
            ln_helper(s2["y0blk"], s2["ln2blk"], D)
            sc.dma(lambda e, t0=t0, tt=tt: e.dma_start(
                out=d["ln_dram"][t0 + tt * 128:t0 + (tt + 1) * 128, :],
                in_=s2["ln2blk"]))
        for dc in range(8):
            sc.dma(lambda e, t0=t0, dc=dc: e.dma_start(
                out=s2["lnfm"][:, dc * 512:(dc + 1) * 512],
                in_=d["ln_dram"][t0:t0 + 512, dc * 128:(dc + 1) * 128],
                transpose=True))
        for tt in range(4):
            p0 = ps[0]
            for di in range(8):
                lhs = s2["lnfm"][:, di * 512 + tt * 128: di * 512 + tt * 128 + 128]
                sc.op("tensor", lambda e, lhs=lhs, di=di, p0=p0: e.matmul(
                    p0[:, 0:64], lhs, s2["rw_sb"][:, di * 64:(di + 1) * 64],
                    start=(di == 0), stop=(di == 7)))
            lg = s2["logits"][:, tt * 64:(tt + 1) * 64]
            sc.op("vector", lambda e, lg=lg, p0=p0: e.tensor_copy(lg, p0[:, 0:64]))
            sc.dma(lambda e, lg=lg, t0=t0, tt=tt: e.dma_start(
                out=d["logits_dram"][t0 + tt * 128:t0 + (tt + 1) * 128, :], in_=lg))
            sc.op("vector", lambda e, lg=lg: e.max(out=s2["max8"], in_=lg))
            sc.dma(lambda e, t0=t0, tt=tt: e.dma_start(
                out=d["max8_dram"][t0 + tt * 128:t0 + (tt + 1) * 128, :], in_=s2["max8"]))
            sc.dma(lambda e, t0=t0, tt=tt: e.dma_start(
                out=d["t8_dram"][t0 + tt * 128:t0 + (tt + 1) * 128, :],
                in_=s2["max8"][:, 7:8]))
            sc.op("vector", lambda e: e.tensor_scalar(
                out=s2["nm1"], in0=s2["max8"][:, 0:1], scalar1=-1.0, scalar2=None,
                op0=ALU.mult))
            sc.op("scalar", lambda e, lg=lg: e.activation(
                out=s2["expw"], in_=lg, func=AF.Exp, bias=s2["nm1"][:, 0:1]))
            sc.op("vector", lambda e, lg=lg: e.tensor_scalar(
                out=s2["oh"], in0=lg, scalar1=s2["max8"][:, 7:8], scalar2=None,
                op0=ALU.is_ge))
            sc.op("vector", lambda e: e.tensor_tensor(
                out=s2["expw"], in0=s2["expw"], in1=s2["oh"], op=ALU.mult))
            sc.op("vector", lambda e: e.reduce_sum(out=s2["den"], in_=s2["expw"],
                                                   axis=AX.X))
            sc.op("vector", lambda e: e.reciprocal(out=s2["den"], in_=s2["den"]))
            sc.op("vector", lambda e: e.tensor_scalar(
                out=s2["wv"], in0=s2["expw"], scalar1=s2["den"][:, 0:1], scalar2=None,
                op0=ALU.mult))
            sc.dma(lambda e, t0=t0, tt=tt: e.dma_start(
                out=d["wtm_dram"][t0 + tt * 128:t0 + (tt + 1) * 128, :], in_=s2["wv"]))
            p1 = ps[1]
            sc.op("tensor", lambda e, lg=lg, p1=p1: e.transpose(
                p1[0:64, 0:128], lg, d["idf_sb"]))
            sc.op("vector", lambda e, tt=tt, p1=p1: e.tensor_copy(
                s2["em"][0:64, tt * 128:(tt + 1) * 128], p1[0:64, 0:128]))
        sc.dma(lambda e, t0=t0: e.dma_start(
            out=s2["t8row"][0:1, 0:512], in_=d["t8_dram"][t0:t0 + 512, :]))
        p2 = ps[2]
        sc.op("tensor", lambda e, p2=p2: e.matmul(
            p2[0:64, 0:512], s2["ones1"][0:1, 0:64], s2["t8row"][0:1, 0:512],
            start=True, stop=True))
        sc.op("vector", lambda e, p2=p2: e.tensor_tensor(
            out=s2["selem"][0:64, :], in0=s2["em"][0:64, :], in1=p2[0:64, 0:512],
            op=ALU.is_ge))
        sc.op("vector", lambda e: e.tensor_tensor_scan(
            out=s2["scan"][0:64, :], data0=s2["ones64"][0:64, :],
            data1=s2["selem"][0:64, :], initial=s2["cnt"][0:64, 0:1],
            op0=ALU.mult, op1=ALU.add))
        sc.op("vector", lambda e: e.tensor_copy(s2["cnt"][0:64, 0:1],
                                                s2["scan"][0:64, 511:512]))
        sc.op("vector", lambda e: e.tensor_tensor(
            out=s2["pos"][0:64, :], in0=s2["scan"][0:64, :], in1=s2["selem"][0:64, :],
            op=ALU.subtract))
        sc.op("vector", lambda e: e.scalar_tensor_tensor(
            out=s2["kept"][0:64, :], in0=s2["pos"][0:64, :], scalar=float(C0),
            in1=s2["selem"][0:64, :], op0=ALU.is_lt, op1=ALU.mult))
        sc.op("vector", lambda e: e.tensor_copy(s2["kepti"][0:64, :], s2["kept"][0:64, :]))
        sc.op("vector", lambda e: e.tensor_scalar(
            out=s2["scan"][0:64, :], in0=s2["pos"][0:64, :],
            scalar1=s2["e512"][0:64, 0:1], scalar2=None, op0=ALU.add))
        sc.op("vector", lambda e: e.select(
            out=s2["slotm"][0:64, :], mask=s2["kepti"][0:64, :],
            on_true=s2["scan"][0:64, :], on_false=s2["big64"][0:64, :]))
        for tt in range(4):
            p3 = ps[3]
            sc.op("tensor", lambda e, tt=tt, p3=p3: e.transpose(
                p3[:, 0:64], s2["slotm"][0:64, tt * 128:(tt + 1) * 128], d["idf_sb"][0:64, 0:64]))
            sc.op("vector", lambda e, p3=p3: e.tensor_copy(s2["lgown"], p3[:, 0:64]))
            sc.dma(lambda e, t0=t0, tt=tt: e.dma_start(
                out=d["slotm_dram"][t0 + tt * 128:t0 + (tt + 1) * 128, :],
                in_=s2["lgown"]))

    for i in range(OWN // 128):
        sc.op("vector", lambda e, i=i: e.tensor_scalar(
            out=s2["idxf"], in0=s2["iotaf"], scalar1=float(i * 128), scalar2=None,
            op0=ALU.add))
        sc.op("vector", lambda e: e.tensor_tensor(
            out=s2["idxf"], in0=s2["idxf"], in1=ownb, op=ALU.add))
        sc.op("vector", lambda e: e.tensor_copy(s2["idxi"][:, 0:1], s2["idxf"]))
        for name, dst, src in (
            ("ln", s2["lnown"], d["ln_dram"]),
            ("lg", s2["lgown"], d["logits_dram"]),
            ("m8", s2["m8own"], d["max8_dram"]),
            ("w", s2["wown"], d["wtm_dram"]),
            ("sm", s2["smown"], d["slotm_dram"]),
        ):
            sc.dma(lambda e, dst=dst, src=src: e.indirect_dma_start(
                out=dst, out_offset=None, in_=src,
                in_offset=IOA(ap=s2["idxi"][:, 0:1], axis=0),
                bounds_check=sc.REG["b_nt"], oob_is_err=False), q="gpsimd")
        sc.dma(lambda e, i=i: e.dma_start(
            out=d["ln2own_dram"][i * 128:(i + 1) * 128, :], in_=s2["lnown"]))
        for k in range(8):
            sc.op("vector", lambda e, k=k: e.tensor_scalar(
                out=s2["oh"], in0=s2["lgown"], scalar1=s2["m8own"][:, k:k + 1],
                scalar2=None, op0=ALU.is_equal))
            sc.op("vector", lambda e: e.tensor_tensor(
                out=s2["ohs"], in0=s2["oh"], in1=s2["smown"], op=ALU.mult))
            sc.op("vector", lambda e, i=i, k=k: e.reduce_sum(
                out=s2["slot8"][:, i * 8 + k:i * 8 + k + 1], in_=s2["ohs"], axis=AX.X))
            sc.op("vector", lambda e: e.tensor_tensor(
                out=s2["ohs"], in0=s2["oh"], in1=s2["wown"], op=ALU.mult))
            sc.op("vector", lambda e, i=i, k=k: e.reduce_sum(
                out=s2["w8"][:, i * 8 + k:i * 8 + k + 1], in_=s2["ohs"], axis=AX.X))
            sc.op("vector", lambda e, i=i, k=k: e.tensor_copy(
                s2["idxi"][:, 1:2], s2["slot8"][:, i * 8 + k:i * 8 + k + 1]))
            sc.dma(lambda e: e.indirect_dma_start(
                out=d["xbuf_dram"], out_offset=IOA(ap=s2["idxi"][:, 1:2], axis=0),
                in_=s2["lnown"], in_offset=None,
                bounds_check=sc.REG["b_slots"], oob_is_err=False), q="gpsimd")

    for ex in range(E0):
        sc.dma(lambda e, ex=ex: e.dma_start(
            out=s2["wg"].rearrange("p (c f) -> p c f", c=8),
            in_=d["wgate"][ex].rearrange("(c p) f -> p c f", p=128)), q="gpsimd")
        sc.dma(lambda e, ex=ex: e.dma_start(
            out=s2["wu"].rearrange("p (c f) -> p c f", c=8),
            in_=d["wup"][ex].rearrange("(c p) f -> p c f", p=128)), q="gpsimd")
        sc.dma(lambda e, ex=ex: e.dma_start(
            out=s2["wd"].rearrange("p (c o) -> p c o", c=6),
            in_=d["wdown"][ex].rearrange("(c p) o -> p c o", p=128)), q="gpsimd")
        for dc in range(8):
            sc.dma(lambda e, ex=ex, dc=dc: e.dma_start(
                out=s2["xfm"][:, dc * 512:(dc + 1) * 512],
                in_=d["xbuf_dram"][ex * C0:(ex + 1) * C0, dc * 128:(dc + 1) * 128],
                transpose=True))
        for st in range(4):
            for fb0, fbn in ((0, 512), (512, 256)):
                pg, pu = ps[0], ps[1]
                for di in range(8):
                    lhs = s2["xfm"][:, di * 512 + st * 128: di * 512 + st * 128 + 128]
                    sc.op("tensor", lambda e, lhs=lhs, di=di, fb0=fb0, fbn=fbn, pg=pg: e.matmul(
                        pg[:, 0:fbn], lhs, s2["wg"][:, di * F0 + fb0: di * F0 + fb0 + fbn],
                        start=(di == 0), stop=(di == 7)))
                    sc.op("tensor", lambda e, lhs=lhs, di=di, fb0=fb0, fbn=fbn, pu=pu: e.matmul(
                        pu[:, 0:fbn], lhs, s2["wu"][:, di * F0 + fb0: di * F0 + fb0 + fbn],
                        start=(di == 0), stop=(di == 7)))
                sc.op("scalar", lambda e, fbn=fbn, pg=pg: e.activation(
                    out=s2["siluscr"][:, 0:fbn], in_=pg[:, 0:fbn], func=AF.Silu))
                sc.op("vector", lambda e, st=st, fb0=fb0, fbn=fbn, pu=pu: e.tensor_tensor(
                    out=s2["hexp"][:, st * F0 + fb0: st * F0 + fb0 + fbn],
                    in0=s2["siluscr"][:, 0:fbn], in1=pu[:, 0:fbn], op=ALU.mult))
        for st in range(4):
            sc.dma(lambda e, st=st: e.dma_start(
                out=d["hexp_dram"][st * 128:(st + 1) * 128, :],
                in_=s2["hexp"][:, st * F0:(st + 1) * F0]))
        for fc in range(6):
            sc.dma(lambda e, fc=fc: e.dma_start(
                out=s2["hfm"][:, fc * 512:(fc + 1) * 512],
                in_=d["hexp_dram"][:, fc * 128:(fc + 1) * 128],
                transpose=True))
        for st in range(4):
            for obk in range(2):
                p2 = ps[2]
                for fc in range(6):
                    lhs = s2["hfm"][:, fc * 512 + st * 128: fc * 512 + st * 128 + 128]
                    sc.op("tensor", lambda e, lhs=lhs, fc=fc, obk=obk, p2=p2: e.matmul(
                        p2[:, 0:512], lhs, s2["wd"][:, fc * 1024 + obk * 512: fc * 1024 + (obk + 1) * 512],
                        start=(fc == 0), stop=(fc == 5)))
                sc.op("vector", lambda e, obk=obk, p2=p2: e.tensor_copy(
                    s2["ytile"][:, obk * 512:(obk + 1) * 512], p2[:, 0:512]))
            sc.dma(lambda e, ex=ex, st=st: e.dma_start(
                out=d["ybuf_dram"][ex * C0 + st * 128: ex * C0 + (st + 1) * 128, :],
                in_=s2["ytile"]))

    for bo in range(OWN // 512):
        o0 = bo * 512
        for dc in range(8):
            sc.dma(lambda e, o0=o0, dc=dc: e.dma_start(
                out=s2["lnfm"][:, dc * 512:(dc + 1) * 512],
                in_=d["ln2own_dram"][o0:o0 + 512, dc * 128:(dc + 1) * 128],
                transpose=True))
        for tt in range(4):
            for gb in range(3):
                p0 = ps[0]
                for di in range(8):
                    lhs = s2["lnfm"][:, di * 512 + tt * 128: di * 512 + tt * 128 + 128]
                    sc.op("tensor", lambda e, lhs=lhs, di=di, gb=gb, p0=p0: e.matmul(
                        p0[:, 0:512], lhs, s2["wsgu"][:, di * 1536 + gb * 512: di * 1536 + (gb + 1) * 512],
                        start=(di == 0), stop=(di == 7)))
                sc.op("vector", lambda e, gb=gb, p0=p0: e.tensor_copy(
                    s2["gu"][:, gb * 512:(gb + 1) * 512], p0[:, 0:512]))
            sc.op("scalar", lambda e: e.activation(
                out=s2["sgs"], in_=s2["gu"][:, 0:SF], func=AF.Silu))
            sc.op("vector", lambda e, tt=tt: e.tensor_tensor(
                out=s2["hs"][:, tt * SF:(tt + 1) * SF], in0=s2["sgs"],
                in1=s2["gu"][:, SF:2 * SF], op=ALU.mult))
            sc.dma(lambda e, o0=o0, tt=tt: e.dma_start(
                out=d["hs_dram"][o0 + tt * 128:o0 + (tt + 1) * 128, :],
                in_=s2["hs"][:, tt * SF:(tt + 1) * SF]))
        for fc in range(6):
            sc.dma(lambda e, o0=o0, fc=fc: e.dma_start(
                out=s2["hsfm"][:, fc * 512:(fc + 1) * 512],
                in_=d["hs_dram"][o0:o0 + 512, fc * 128:(fc + 1) * 128],
                transpose=True))
        for tt in range(4):
            i = bo * 4 + tt
            sc.op("vector", lambda e: e.memset(s2["accv"], 0.0))
            for k in range(8):
                sc.op("vector", lambda e, i=i, k=k: e.tensor_copy(
                    s2["idxi"][:, 1:2], s2["slot8"][:, i * 8 + k:i * 8 + k + 1]))
                sc.op("vector", lambda e: e.memset(s2["ygath"], 0.0))
                sc.dma(lambda e: e.indirect_dma_start(
                    out=s2["ygath"], out_offset=None, in_=d["ybuf_dram"],
                    in_offset=IOA(ap=s2["idxi"][:, 1:2], axis=0),
                    bounds_check=sc.REG["b_slots"], oob_is_err=False), q="gpsimd")
                sc.op("vector", lambda e, i=i, k=k: e.scalar_tensor_tensor(
                    out=s2["accv"], in0=s2["ygath"],
                    scalar=s2["w8"][:, i * 8 + k:i * 8 + k + 1],
                    in1=s2["accv"], op0=ALU.mult, op1=ALU.add))
            sc.op("vector", lambda e, i=i: e.tensor_scalar(
                out=s2["idxf"], in0=s2["iotaf"], scalar1=float(i * 128), scalar2=None,
                op0=ALU.add))
            sc.op("vector", lambda e: e.tensor_tensor(
                out=s2["idxf"], in0=s2["idxf"], in1=ownb, op=ALU.add))
            sc.op("vector", lambda e: e.tensor_copy(s2["idxi"][:, 0:1], s2["idxf"]))
            sc.dma(lambda e: e.indirect_dma_start(
                out=s2["y0own"], out_offset=None, in_=d["y0_dram"],
                in_offset=IOA(ap=s2["idxi"][:, 0:1], axis=0),
                bounds_check=sc.REG["b_nt"], oob_is_err=False), q="gpsimd")
            for obk in range(2):
                p1 = ps[1]
                for fc in range(6):
                    lhs = s2["hsfm"][:, fc * 512 + tt * 128: fc * 512 + tt * 128 + 128]
                    sc.op("tensor", lambda e, lhs=lhs, fc=fc, obk=obk, p1=p1: e.matmul(
                        p1[:, 0:512], lhs, s2["wsd"][:, fc * 1024 + obk * 512: fc * 1024 + (obk + 1) * 512],
                        start=(fc == 0), stop=(fc == 5)))
                asl = s2["accv"][:, obk * 512:(obk + 1) * 512]
                sc.op("vector", lambda e, asl=asl, p1=p1: e.tensor_tensor(
                    out=asl, in0=asl, in1=p1[:, 0:512], op=ALU.add))
                ysl = s2["y0own"][:, obk * 512:(obk + 1) * 512]
                sc.op("vector", lambda e, asl=asl, ysl=ysl: e.tensor_tensor(
                    out=ysl, in0=ysl, in1=asl, op=ALU.add))
            sc.dma(lambda e, i=i: e.dma_start(
                out=d["y1_dram"][i * 128:(i + 1) * 128, :], in_=s2["y0own"]))


def build_program(dbg=(), s3only=False):
    nc = bass.Bass("TRN2", target_bir_lowering=False, detect_race_conditions=False,
                   dynamic_dma_scratch_size=32768, num_swdge_queues=4)

    def dram_in(name, shape, dt):
        return nc.dram_tensor(name, shape, dt, kind="ExternalInput").ap()

    taps = []

    def dram_tmp(name, shape, dt):
        t = nc.dram_tensor(name, shape, dt, kind="Internal").ap()
        if name in dbg:
            o = nc.dram_tensor("tap_" + name, shape, dt, kind="ExternalOutput").ap()
            taps.append((o, t))
        return t

    x_tm = dram_in("x_tm", [NT, D], F32)
    Wq = dram_in("Wq", [D, D], F32)
    Wk = dram_in("Wk", [D, D], F32)
    Wv = dram_in("Wv", [D, D], F32)
    Wg = dram_in("Wg", [D, D], F32)
    Wo = dram_in("Wo", [D, D], F32)
    rwT = dram_in("rwT", [D, E0], F32)
    wgate = dram_in("wgate", [E0, D, F0], F32)
    wup = dram_in("wup", [E0, D, F0], F32)
    wdown = dram_in("wdown", [E0, F0, D], F32)
    wsgu = dram_in("wsgu", [D, 2 * SF], F32)
    wsd = dram_in("wsd", [SF, D], F32)
    own_base_f = dram_in("own_base_f", [128, 1], F32)
    W_in = dram_in("W_in", [D, 2 * D_INNER], F32)
    conv_w = dram_in("conv_w", [D_INNER, D_CONV], F32)
    conv_b = dram_in("conv_b", [D_INNER], F32)
    W_x = dram_in("W_x", [D_INNER, 2 * D_STATE + 1], F32)
    W_dt = dram_in("W_dt", [1, D_INNER], F32)
    b_dt = dram_in("b_dt", [D_INNER], F32)
    A_log = dram_in("A_log", [D_INNER, D_STATE], F32)
    D_param = dram_in("D_param", [D_INNER], F32)
    W_out = dram_in("W_out", [D_INNER, D], F32)
    qoff_f = dram_in("qoff_f", [128, 1], F32)
    Wg2 = dram_in("Wg2", [D, E1], F32)
    bg2_b = dram_in("bg2_b", [128, E1], F32)
    W1 = dram_in("W1", [E1, D, H1], F32)
    b1 = dram_in("b1", [E1, H1], F32)
    W2 = dram_in("W2", [E1, H1, D], F32)
    b2 = dram_in("b2", [E1, D], F32)
    masks_c = dram_in("masks_c", [128, H * 128], F32)
    gw_c = dram_in("gw_c", [128, H], F32)
    ident_bf = dram_in("ident_bf", [128, 128], BF16)
    ident_f32 = dram_in("ident_f32", [128, 128], F32)

    y0_dram = dram_tmp("y0_tm", [NT, D], F32)

    out_final = nc.dram_tensor("out", [QT, D], F32, kind="ExternalOutput").ap()

    ln_dram = dram_tmp("ln_dram", [NT, D], BF16)
    gated_dram = dram_tmp("gated_dram", [512, D], BF16)
    logits_dram = dram_tmp("logits_dram", [NT, E0], F32)
    max8_dram = dram_tmp("max8_dram", [NT, 8], F32)
    wtm_dram = dram_tmp("wtm_dram", [NT, E0], F32)
    t8_dram = dram_tmp("t8_dram", [NT, 1], F32)
    slotm_dram = dram_tmp("slotm_dram", [NT, E0], F32)
    ln2own_dram = dram_tmp("ln2own_dram", [OWN, D], BF16)
    xbuf_dram = dram_tmp("xbuf_dram", [E0 * C0, D], BF16)
    ybuf_dram = dram_tmp("ybuf_dram", [E0 * C0, D], BF16)
    hexp_dram = dram_tmp("hexp_dram", [C0, F0], BF16)
    hs_dram = dram_tmp("hs_dram", [OWN, SF], BF16)
    y1_dram = (dram_in("y1_in", [OWN, D], F32) if s3only
               else dram_tmp("y1_tm", [OWN, D], F32))
    xz_dram = dram_tmp("xz_dram", [OWN, 2 * D_INNER], BF16)
    xc_dram = dram_tmp("xc_dram", [D_INNER, OWN], BF16)
    dt_dram = dram_tmp("dt_dram", [D_INNER, OWN], BF16)
    b_dram = dram_tmp("b_dram", [D_STATE, OWN], BF16)
    csum_dram = dram_tmp("csum_dram", [1, OWN], F32)
    ysum_dram = dram_tmp("ysum_dram", [1, OWN], F32)
    ybc_dram = dram_tmp("ybc_dram", [1, OWN], BF16)
    ybcq_dram = dram_tmp("ybcq_dram", [1, QT], BF16)
    zq_dram = dram_tmp("zq_dram", [QT, D_INNER], BF16)
    y1q_dram = dram_tmp("y1q_dram", [QT, D], F32)
    y2_dram = dram_tmp("y2_tm", [QT, D], F32)
    w2bf_dram = dram_tmp("w2bf_dram", [E1, H1, D], BF16)
    acc_dram = dram_tmp("acc_dram", [QT, D], F32)

    sc = Script()
    REG = {}
    sc.REG = REG

    def reg_setup(eng):
        for name, val in (("b_nt", NT - 1), ("b_slots", E0 * C0 - 1),
                          ("b_own", OWN - 1), ("b_qt", QT - 1),
                          ("b_flat", D_INNER * OWN - 1)):
            r = eng.alloc_register(name)
            eng.reg_mov(r, val)
            REG[name] = r


    sc.setup("gpsimd", reg_setup)

    with ExitStack() as stack:
        def sb(name, shape, dt):
            return stack.enter_context(nc.sbuf_tensor(name, shape, dt))

        AF32 = Arena(sb("AF32", [128, 10000], F32), 10000)
        ABF = Arena(sb("ABF", [128, 68000], BF16), 68000)

        ps = [stack.enter_context(nc.psum_tensor(f"ps{i}", [128, 512], F32))
              for i in range(8)]

        ticks = [stack.enter_context(nc.semaphore(f"tick{i}")) for i in range(24)]
        block = stack.enter_context(nc.Block())

        masks_sb = AF32.alloc(H * 128)
        gw_sb = AF32.alloc(H)
        idbf_sb = ABF.alloc(128)
        idf_sb = AF32.alloc(128)

        sc.dma(lambda e: e.dma_start(out=masks_sb, in_=masks_c))
        sc.dma(lambda e: e.dma_start(out=gw_sb, in_=gw_c))
        sc.dma(lambda e: e.dma_start(out=idbf_sb, in_=ident_bf))
        sc.dma(lambda e: e.dma_start(out=idf_sb, in_=ident_f32))

        sq_scr = ABF.alloc(1024)
        stat = AF32.alloc(16)

        AF32.push()
        ABF.push()
        w_sb = {}
        for name, w in [("Wq", Wq), ("Wk", Wk), ("Wv", Wv), ("Wg", Wg), ("Wo", Wo)]:
            t = ABF.alloc(8 * 1024)
            w_sb[name] = t
            sc.dma(lambda e, t=t, w=w: e.dma_start(
                out=t.rearrange("p (c o) -> p c o", c=8),
                in_=w.rearrange("(c p) o -> p c o", p=128)), q="gpsimd")

        def wslice(name, cin, lo, n):
            return w_sb[name][:, cin * 1024 + lo: cin * 1024 + lo + n]

        xblk = AF32.alloc(4 * 1024)
        lnblk = ABF.alloc(1024)
        lnfm = ABF.alloc(8 * 512)
        qfm = ABF.alloc(8 * 512)
        kfm = ABF.alloc(8 * 512)
        vtm = ABF.alloc(1024)
        kwtm = ABF.alloc(1024)
        gstm = ABF.alloc(1024)
        retn = ABF.alloc(1024)
        gatedb = retn
        gatedfm = lnfm
        sT_bf = ABF.alloc(128)
        S_f32 = AF32.alloc(8 * 128)
        S_bf = ABF.alloc(8 * 128)

        eps_ln_c = stat[:, 10:11]
        eps_gn_c = stat[:, 11:12]
        sc.op("vector", lambda e: e.memset(eps_ln_c, EPS_LN))
        sc.op("vector", lambda e: e.memset(eps_gn_c, EPS_GN))

        s_col = stat[:, 0:1]
        sq_col = stat[:, 1:2]
        negm = stat[:, 2:3]
        var_c = stat[:, 3:4]
        rstd = stat[:, 4:5]
        m2 = stat[:, 5:6]
        sq2 = stat[:, 6:7]
        negm2 = stat[:, 7:8]
        var2 = stat[:, 8:9]
        rstd2 = stat[:, 9:10]

        def ln_stats_and_norm(xtile, lntile, dcols):
            sc.op("vector", lambda e: e.reduce_sum(out=s_col, in_=xtile, axis=AX.X))
            sc.op("scalar", lambda e: e.activation(out=sq_scr[:, :dcols], in_=xtile,
                                                   func=AF.Square, accum_out=sq_col))
            sc.op("vector", lambda e: e.tensor_scalar(out=negm, in0=s_col,
                                                      scalar1=-1.0 / dcols, scalar2=None,
                                                      op0=ALU.mult))
            sc.op("vector", lambda e: e.tensor_scalar(out=sq2, in0=sq_col,
                                                      scalar1=1.0 / dcols, scalar2=None,
                                                      op0=ALU.mult))
            sc.op("vector", lambda e: e.scalar_tensor_tensor(
                out=var_c, in0=negm, scalar=negm, in1=sq2,
                op0=ALU.mult, op1=ALU.subtract))
            sc.op("vector", lambda e: e.tensor_scalar(out=var_c, in0=var_c,
                                                      scalar1=-1.0, scalar2=None,
                                                      op0=ALU.mult))
            sc.op("scalar", lambda e: e.activation(out=rstd, in_=var_c, func=AF.Sqrt,
                                                   bias=eps_ln_c))
            sc.op("vector", lambda e: e.reciprocal(out=rstd, in_=rstd))
            sc.op("vector", lambda e: e.tensor_scalar(out=lntile, in0=xtile,
                                                      scalar1=negm, scalar2=rstd,
                                                      op0=ALU.add, op1=ALU.mult))

        NBLK = NT // 512
        for blk in range([] if s3only else range(NBLK)) if False else (range(0) if s3only else range(NBLK)):
            t0 = blk * 512
            for tt in range(4):
                sc.dma(lambda e, t0=t0, tt=tt: e.dma_start(
                    out=xblk[:, tt * 1024:(tt + 1) * 1024],
                    in_=x_tm[t0 + tt * 128:t0 + (tt + 1) * 128, :]))
                ln_stats_and_norm(xblk[:, tt * 1024:(tt + 1) * 1024], lnblk, D)
                sc.dma(lambda e, t0=t0, tt=tt: e.dma_start(
                    out=ln_dram[t0 + tt * 128:t0 + (tt + 1) * 128, :], in_=lnblk))
            for dc in range(8):
                sc.dma(lambda e, t0=t0, dc=dc: e.dma_start(
                    out=lnfm[:, dc * 512:(dc + 1) * 512],
                    in_=ln_dram[t0:t0 + 512, dc * 128:(dc + 1) * 128],
                    transpose=True))
            for which, dst in (("Wq", qfm), ("Wk", kfm)):
                for do in range(8):
                    pb = ps[0]
                    for di in range(8):
                        sc.op("tensor", lambda e, which=which, do=do, di=di, pb=pb: e.matmul(
                            pb[:, 0:512], wslice(which, di, do * 128, 128),
                            lnfm[:, di * 512:(di + 1) * 512],
                            start=(di == 0), stop=(di == 7)))
                    sc.op("vector", lambda e, do=do, dst=dst, pb=pb: e.tensor_copy(
                        dst[:, do * 512:(do + 1) * 512], pb[:, 0:512]))
            for tt in range(4):
                for dvb in range(2):
                    pv, pk, pg = ps[1], ps[2], ps[3]
                    for di in range(8):
                        lhs = lnfm[:, di * 512 + tt * 128: di * 512 + tt * 128 + 128]
                        sc.op("tensor", lambda e, lhs=lhs, di=di, dvb=dvb, pv=pv: e.matmul(
                            pv[:, 0:512], lhs, wslice("Wv", di, dvb * 512, 512),
                            start=(di == 0), stop=(di == 7)))
                        sc.op("tensor", lambda e, lhs=lhs, di=di, dvb=dvb, pk=pk: e.matmul(
                            pk[:, 0:512], lhs, wslice("Wk", di, dvb * 512, 512),
                            start=(di == 0), stop=(di == 7)))
                        sc.op("tensor", lambda e, lhs=lhs, di=di, dvb=dvb, pg=pg: e.matmul(
                            pg[:, 0:512], lhs, wslice("Wg", di, dvb * 512, 512),
                            start=(di == 0), stop=(di == 7)))
                    vdst = vtm[:, dvb * 512:(dvb + 1) * 512]
                    sc.op("vector", lambda e, vdst=vdst, pv=pv: e.tensor_copy(vdst, pv[:, 0:512]))
                    for hh in range(4):
                        h = dvb * 4 + hh
                        kdst = kwtm[:, h * 128:(h + 1) * 128]
                        sc.op("vector", lambda e, kdst=kdst, pk=pk, hh=hh, h=h: e.tensor_scalar(
                            out=kdst, in0=pk[:, hh * 128:(hh + 1) * 128],
                            scalar1=gw_sb[:, h:h + 1], scalar2=None, op0=ALU.mult))
                    gdst = gstm[:, dvb * 512:(dvb + 1) * 512]
                    sc.op("scalar", lambda e, gdst=gdst, pg=pg: e.activation(
                        out=gdst, in_=pg[:, 0:512], func=AF.Silu))
                c = blk * 4 + tt
                batch_start = (c % (S // 128) == 0)
                for h in range(H):
                    qsl = qfm[:, h * 512 + tt * 128: h * 512 + tt * 128 + 128]
                    ksl = kfm[:, h * 512 + tt * 128: h * 512 + tt * 128 + 128]
                    vsl = vtm[:, h * 128:(h + 1) * 128]
                    kwsl = kwtm[:, h * 128:(h + 1) * 128]
                    Sf = S_f32[:, h * 128:(h + 1) * 128]
                    Sb = S_bf[:, h * 128:(h + 1) * 128]
                    p_sc, p_ret, p_upd = ps[4], ps[5], ps[6]
                    sc.op("tensor", lambda e, ksl=ksl, qsl=qsl, p_sc=p_sc: e.matmul(
                        p_sc[:, 0:128], ksl, qsl, start=True, stop=True))
                    sc.op("vector", lambda e, h=h, p_sc=p_sc: e.tensor_tensor(
                        out=sT_bf, in0=p_sc[:, 0:128],
                        in1=masks_sb[:, h * 128:(h + 1) * 128], op=ALU.mult))
                    if not batch_start:
                        sc.op("tensor", lambda e, qsl=qsl, Sb=Sb, p_ret=p_ret: e.matmul(
                            p_ret[:, 0:128], qsl, Sb, start=True, stop=False))
                    sc.op("tensor", lambda e, vsl=vsl, p_ret=p_ret, bs=batch_start: e.matmul(
                        p_ret[:, 0:128], sT_bf, vsl, start=bs, stop=True))
                    sc.op("vector", lambda e, p_ret=p_ret: e.reduce_sum(
                        out=m2, in_=p_ret[:, 0:128], axis=AX.X))
                    sc.op("scalar", lambda e, p_ret=p_ret: e.activation(
                        out=sq_scr[:, 0:128], in_=p_ret[:, 0:128], func=AF.Square,
                        accum_out=sq2))
                    sc.op("vector", lambda e: e.tensor_scalar(
                        out=negm2, in0=m2, scalar1=-1.0 / DH, scalar2=None, op0=ALU.mult))
                    sc.op("vector", lambda e: e.tensor_scalar(
                        out=var2, in0=sq2, scalar1=1.0 / DH, scalar2=None, op0=ALU.mult))
                    sc.op("vector", lambda e: e.scalar_tensor_tensor(
                        out=var2, in0=negm2, scalar=negm2, in1=var2,
                        op0=ALU.mult, op1=ALU.subtract))
                    sc.op("vector", lambda e: e.tensor_scalar(
                        out=var2, in0=var2, scalar1=-1.0, scalar2=None, op0=ALU.mult))
                    sc.op("scalar", lambda e: e.activation(out=rstd2, in_=var2,
                                                           func=AF.Sqrt, bias=eps_gn_c))
                    sc.op("vector", lambda e: e.reciprocal(out=rstd2, in_=rstd2))
                    rdst = retn[:, h * 128:(h + 1) * 128]
                    sc.op("vector", lambda e, rdst=rdst, p_ret=p_ret: e.tensor_scalar(
                        out=rdst, in0=p_ret[:, 0:128], scalar1=negm2, scalar2=rstd2,
                        op0=ALU.add, op1=ALU.mult))
                    sc.op("tensor", lambda e, kwsl=kwsl, vsl=vsl, p_upd=p_upd: e.matmul(
                        p_upd[:, 0:128], kwsl, vsl, start=True, stop=True))
                    g128 = GAMMAS[h] ** 128
                    if batch_start:
                        sc.op("vector", lambda e, Sf=Sf, p_upd=p_upd: e.tensor_copy(
                            Sf, p_upd[:, 0:128]))
                    else:
                        sc.op("vector", lambda e, Sf=Sf, p_upd=p_upd, g128=g128:
                              e.scalar_tensor_tensor(out=Sf, in0=Sf, scalar=g128,
                                                     in1=p_upd[:, 0:128],
                                                     op0=ALU.mult, op1=ALU.add))
                    sc.op("vector", lambda e, Sb=Sb, Sf=Sf: e.tensor_copy(Sb, Sf))
                sc.op("vector", lambda e: e.tensor_tensor(
                    out=gatedb, in0=retn, in1=gstm, op=ALU.mult))
                sc.dma(lambda e, tt=tt: e.dma_start(
                    out=gated_dram[tt * 128:(tt + 1) * 128, :], in_=gatedb))
            for dc in range(8):
                sc.dma(lambda e, dc=dc: e.dma_start(
                    out=gatedfm[:, dc * 512:(dc + 1) * 512],
                    in_=gated_dram[:, dc * 128:(dc + 1) * 128],
                    transpose=True))
            for tt in range(4):
                for dvb in range(2):
                    pb = ps[7]
                    for di in range(8):
                        lhs = gatedfm[:, di * 512 + tt * 128: di * 512 + tt * 128 + 128]
                        sc.op("tensor", lambda e, lhs=lhs, di=di, dvb=dvb, pb=pb: e.matmul(
                            pb[:, 0:512], lhs, wslice("Wo", di, dvb * 512, 512),
                            start=(di == 0), stop=(di == 7)))
                    dst = xblk[:, tt * 1024 + dvb * 512: tt * 1024 + (dvb + 1) * 512]
                    sc.op("vector", lambda e, dst=dst, pb=pb: e.tensor_tensor(
                        out=dst, in0=dst, in1=pb[:, 0:512], op=ALU.add))
            for tt in range(4):
                sc.dma(lambda e, t0=t0, tt=tt: e.dma_start(
                    out=y0_dram[t0 + tt * 128:t0 + (tt + 1) * 128, :],
                    in_=xblk[:, tt * 1024:(tt + 1) * 1024]))

        AF32.pop()
        ABF.pop()

        AF32.push()
        ABF.push()
        s2 = {}
        s2["y0blk"] = AF32.alloc(1024)
        s2["ln2blk"] = ABF.alloc(1024)
        s2["lnfm"] = ABF.alloc(8 * 512)
        s2["logits"] = AF32.alloc(4 * 64)
        s2["max8"] = AF32.alloc(8)
        s2["nm1"] = AF32.alloc(1)
        s2["expw"] = AF32.alloc(64)
        s2["den"] = AF32.alloc(1)
        s2["wv"] = AF32.alloc(64)
        s2["em"] = AF32.alloc(512)
        s2["selem"] = AF32.alloc(512)
        s2["scan"] = AF32.alloc(512)
        s2["pos"] = AF32.alloc(512)
        s2["slotm"] = AF32.alloc(512)
        s2["ones64"] = AF32.alloc(512)
        s2["big64"] = AF32.alloc(512)
        s2["kept"] = AF32.alloc(512)
        s2["kepti"] = stack.enter_context(nc.sbuf_tensor("kepti", [128, 512], I32))
        s2["e512"] = AF32.alloc(1)
        s2["cnt"] = AF32.alloc(1)
        s2["t8row"] = AF32.alloc(512)
        s2["ones1"] = AF32.alloc(64)
        s2["iotaf"] = AF32.alloc(1)
        s2["idxf"] = AF32.alloc(1)
        s2["idxi"] = stack.enter_context(nc.sbuf_tensor("idxi", [128, 8], I32))
        s2["lnown"] = ABF.alloc(1024)
        s2["lgown"] = AF32.alloc(64)
        s2["m8own"] = AF32.alloc(8)
        s2["wown"] = AF32.alloc(64)
        s2["smown"] = AF32.alloc(64)
        s2["oh"] = AF32.alloc(64)
        s2["ohs"] = AF32.alloc(64)
        s2["slot8"] = AF32.alloc(64 * 8)
        s2["w8"] = AF32.alloc(64 * 8)
        s2["zero8k"] = ABF.alloc(2048)
        s2["accv"] = AF32.alloc(1024)
        s2["ownb"] = AF32.alloc(1)
        s2["ygath"] = ABF.alloc(1024)
        s2["y0own"] = s2["y0blk"]
        s2["rw_sb"] = ABF.alloc(8 * 64)
        s2["wg"] = ABF.alloc(8 * F0)
        s2["wu"] = ABF.alloc(8 * F0)
        s2["wd"] = ABF.alloc(6 * 1024)
        s2["xfm"] = ABF.alloc(8 * 512)
        s2["hexp"] = ABF.alloc(4 * F0)
        s2["hfm"] = ABF.alloc(6 * 512)
        s2["ytile"] = ABF.alloc(1024)
        s2["siluscr"] = ABF.alloc(512)
        s2["wsgu"] = ABF.alloc(8 * 2 * SF)
        s2["wsd"] = ABF.alloc(6 * 1024)
        s2["gu"] = ABF.alloc(1536)
        s2["sgs"] = ABF.alloc(SF)
        s2["hs"] = ABF.alloc(4 * SF)
        s2["hsfm"] = ABF.alloc(6 * 512)

        if not s3only:
            emit_stage2(sc, ps, s2, dict(
                ln_dram=ln_dram, y0_dram=y0_dram, rwT=rwT, wgate=wgate, wup=wup,
                wdown=wdown, wsgu=wsgu, wsd=wsd, own_base_f=own_base_f,
                logits_dram=logits_dram, max8_dram=max8_dram, wtm_dram=wtm_dram,
                t8_dram=t8_dram, slotm_dram=slotm_dram, ln2own_dram=ln2own_dram,
                xbuf_dram=xbuf_dram, ybuf_dram=ybuf_dram, hexp_dram=hexp_dram,
                hs_dram=hs_dram, y1_dram=y1_dram, idf_sb=idf_sb,
            ), ln_stats_and_norm, stat, eps_ln_c)

        AF32.pop()
        ABF.pop()

        def xsb(name, shape, dt):
            return stack.enter_context(nc.sbuf_tensor(name, shape, dt))

        emit_stage3(sc, ps, AF32, ABF, xsb, dict(
            y1_dram=y1_dram, ln_dram=ln_dram, W_in=W_in, conv_w=conv_w,
            conv_b=conv_b, W_x=W_x, W_dt=W_dt, b_dt=b_dt, A_log=A_log,
            D_param=D_param, W_out=W_out, qoff_f=qoff_f,
            xz_dram=xz_dram, xc_dram=xc_dram, dt_dram=dt_dram, b_dram=b_dram,
            csum_dram=csum_dram, ysum_dram=ysum_dram, ybc_dram=ybc_dram,
            zq_dram=zq_dram, y1q_dram=y1q_dram, y2_dram=y2_dram,
            ybcq_dram=ybcq_dram,
        ), ln_stats_and_norm)

        emit_stage4(sc, ps, AF32, ABF, dict(
            y2_dram=y2_dram, ln_dram=ln_dram, Wg2=Wg2, bg2_b=bg2_b, W1=W1, b1=b1,
            W2=W2, b2=b2, w2bf_dram=w2bf_dram, acc_dram=acc_dram,
            out_final=out_final, idf_sb=idf_sb,
        ), ln_stats_and_norm)

        if taps:
            tap_scr_bf = ABF.alloc(8192)
            tap_scr_f = AF32.alloc(2048)
            for o_, t_ in taps:
                if len(t_.shape) == 3:
                    a_, b_, c_ = t_.shape
                    t_ = t_.rearrange("a b c -> (a b) c")
                    o_ = o_.rearrange("a b c -> (a b) c")
                rows, width = t_.shape
                scr = tap_scr_bf if t_.dtype == BF16 else tap_scr_f
                if rows % 128 == 0 and width <= scr.shape[1]:
                    for rr in range(rows // 128):
                        sc.dma(lambda e, t_=t_, rr=rr, width=width, scr=scr: e.dma_start(
                            out=scr[:, 0:width], in_=t_[rr * 128:(rr + 1) * 128, :]))
                        sc.dma(lambda e, o_=o_, rr=rr, width=width, scr=scr: e.dma_start(
                            out=o_[rr * 128:(rr + 1) * 128, :], in_=scr[:, 0:width]))
                else:
                    sc.dma(lambda e, o_=o_, t_=t_: e.dma_start(out=o_, in_=t_))

        if "y0_tm" in dbg or "y1_tm" in dbg:
            sc.dma(lambda e: e.dma_start(
                out=out_final[0:128, :], in_=s2["y0blk"][:, 0:1024]))

        sc.emit(block, ticks)

    return nc


def make_consts():
    t = np.arange(128, dtype=np.float64)
    masks = np.zeros((128, H * 128), dtype=np.float32)
    gw = np.zeros((128, H), dtype=np.float32)
    for h in range(H):
        g = GAMMAS[h]
        colv = np.power(g, -(t + 1.0)) / np.sqrt(DH)
        m = np.where(t[:, None] <= t[None, :], colv[:, None], 0.0)
        masks[:, h * 128:(h + 1) * 128] = m.astype(np.float32)
        gw[:, h] = (np.power(g, 127.0 - t) / np.sqrt(DH)).astype(np.float32)
    ident = np.eye(128, dtype=np.float32)
    return dict(masks_c=masks, gw_c=gw,
                ident_bf=ident.astype(ml_dtypes.bfloat16), ident_f32=ident)


def kernel(**inputs):
    z = inputs
    x_tm = np.ascontiguousarray(z["L0_attn_X"].reshape(-1, D).astype(np.float32))
    consts = make_consts()
    common = dict(
        x_tm=x_tm,
        Wq=z["L0_attn_Wq"], Wk=z["L0_attn_Wk"], Wv=z["L0_attn_Wv"],
        Wg=z["L0_attn_W_G"], Wo=z["L0_attn_W_O"],
        rwT=np.ascontiguousarray(z["L0_ffn_router_w"].T),
        wgate=z["L0_ffn_w_gate"], wup=z["L0_ffn_w_up"], wdown=z["L0_ffn_w_down"],
        wsgu=z["L0_ffn_ws_gate_up"], wsd=z["L0_ffn_ws_down"],
        W_in=z["L1_attn_W_in"], conv_w=z["L1_attn_conv_w"],
        conv_b=z["L1_attn_conv_b"], W_x=z["L1_attn_W_x"], W_dt=z["L1_attn_W_dt"],
        b_dt=z["L1_attn_b_dt"], A_log=z["L1_attn_A_log"],
        D_param=z["L1_attn_D_param"], W_out=z["L1_attn_W_out"],
        Wg2=z["L1_ffn_Wg"],
        bg2_b=np.ascontiguousarray(np.broadcast_to(z["L1_ffn_bg"], (128, E1))),
        W1=z["L1_ffn_W1"], b1=z["L1_ffn_b1"], W2=z["L1_ffn_W2"], b2=z["L1_ffn_b2"],
        **consts)
    in_maps = []
    for c in range(8):
        b, q = c // 4, c % 4
        m = dict(common)
        m["own_base_f"] = np.full((128, 1), float(b * S), np.float32)
        m["qoff_f"] = np.full((128, 1), float(q * QT), np.float32)
        in_maps.append(m)
    nc = build_program()
    res = run_bass_kernel_spmd(nc, in_maps, core_ids=list(range(8)))
    out = np.empty((B, S, D), np.float32)
    for c in range(8):
        b, q = c // 4, c % 4
        out[b, q * QT:(q + 1) * QT, :] = np.asarray(res.results[c]["out"])
    return out


if __name__ == "__main__":
    pass


# version 06 (final)
# speedup vs baseline: 1.2595x; 1.2595x over previous
import numpy as np
import ml_dtypes
import concourse.bass as bass
import concourse.mybir as mybir
from concourse.bass import IndirectOffsetOnAxis
from concourse.bass_utils import run_bass_kernel_spmd
from contextlib import ExitStack

F32 = mybir.dt.float32
BF16 = mybir.dt.bfloat16
I32 = mybir.dt.int32
U32 = mybir.dt.uint32
AF = mybir.ActivationFunctionType
ALU = mybir.AluOpType
AX = mybir.AxisListType

B, S, D = 2, 8192, 1024
NT = B * S
H, DH = 8, 128
EPS_LN = 1e-5
EPS_GN = 1e-6
E0, K0, F0, C0 = 64, 8, 768, 512
SF = 768
D_INNER, D_STATE, D_CONV = 2048, 16, 4
E1, K1, H1 = 8, 2, 4096
OWN = S
QT = 2048

GAMMAS = [1.0 - 2.0 ** (-5.0 - h) for h in range(H)]


class Script:
    def __init__(self):
        self.items = []
        self.setups = {}

    def setup(self, eng, fn):
        self.setups.setdefault(eng, []).append(fn)

    def op(self, eng, fn, inc=1):
        self.items.append((eng, fn, inc))

    def dma(self, fn, q="sync"):
        self.op(q, fn, 16)

    EPOCH_LIMIT = 24000

    def plan(self):
        self.sched = []
        bounds = []
        ep, v = 0, 0
        for e, fn, inc in self.items:
            if v + inc > self.EPOCH_LIMIT:
                bounds.append(v)
                ep += 1
                v = 0
            self.sched.append((ep, v))
            v += inc
        bounds.append(v)
        return bounds

    def emit(self, block, sems, coalesce=("tensor",)):
        bounds = self.plan()
        assert len(bounds) <= len(sems), f"need {len(bounds)} ticket sems"
        n = len(self.items)
        is_first = [True] * n
        run_inc = [0] * n
        i = 0
        while i < n:
            e0 = self.items[i][0]
            ep0 = self.sched[i][0]
            j = i
            tot = 0
            kind0 = self.items[i][2] == 16
            while (j < n and self.items[j][0] == e0 and self.sched[j][0] == ep0
                   and (self.items[j][2] == 16) == kind0
                   and (e0 in coalesce or j == i)):
                tot += self.items[j][2]
                j += 1
                if e0 not in coalesce:
                    break
            for k2 in range(i, j):
                is_first[k2] = (k2 == i)
            run_inc[j - 1] = tot
            i = j
        engines = ["sync", "tensor", "vector", "scalar", "gpsimd"]
        for name in engines:
            def make(name):
                def body(eng):
                    for f in self.setups.get(name, []):
                        f(eng)
                    cur_ep = 0
                    for idx in range(n):
                        e, fn, inc = self.items[idx]
                        if e != name:
                            continue
                        ep, v = self.sched[idx]
                        while cur_ep < ep:
                            eng.wait_ge(sems[cur_ep], bounds[cur_ep])
                            cur_ep += 1
                        if is_first[idx]:
                            eng.wait_ge(sems[ep], v)
                        inst = fn(eng)
                        inst.then_inc(sems[ep], inc)
                    while cur_ep < len(bounds) - 1:
                        eng.wait_ge(sems[cur_ep], bounds[cur_ep])
                        cur_ep += 1
                    eng.wait_ge(sems[cur_ep], bounds[cur_ep])
                return body
            getattr(block, name)(make(name))


class Arena:

    def __init__(self, tile, cols):
        self.tile = tile
        self.cols = cols
        self.off = 0
        self.marks = []

    ALIGN = 32

    def alloc(self, n):
        self.off = (self.off + self.ALIGN - 1) // self.ALIGN * self.ALIGN
        assert self.off + n <= self.cols, f"arena overflow {self.off}+{n}>{self.cols}"
        v = self.tile[:, self.off:self.off + n]
        self.off += n
        return v

    def push(self):
        self.marks.append(self.off)

    def pop(self):
        self.off = self.marks.pop()


def ceil_div(a, b):
    return (a + b - 1) // b


def emit_stage3(sc, ps, AF32, ABF, xsb, d, ln_helper):
    IOA = IndirectOffsetOnAxis
    REG = sc.REG
    AF32.push()
    ABF.push()

    acol = AF32.alloc(256)
    cw = AF32.alloc(64)
    cb = AF32.alloc(16)
    bdt = AF32.alloc(16)
    dpar = AF32.alloc(16)
    qoff = AF32.alloc(1)
    iotaf2 = AF32.alloc(1)
    idxf2 = AF32.alloc(1)
    onesc = AF32.alloc(1)
    ones16 = AF32.alloc(1)
    states = AF32.alloc(256)
    wdt = ABF.alloc(2048)
    onesr = ABF.alloc(128)
    onescol = ABF.alloc(1)
    alog_s = AF32.alloc(256)
    idxi2 = xsb("s3idxi", [128, 1], I32)
    qoffi = xsb("s3qoffi", [128, 1], I32)

    sc.dma(lambda e: e.dma_start(out=wdt[0:1, :], in_=d["W_dt"]), q="gpsimd")
    sc.dma(lambda e: e.dma_start(
        out=cw.rearrange("p (c o) -> p c o", c=16),
        in_=d["conv_w"].rearrange("(c p) o -> p c o", p=128)))
    sc.dma(lambda e: e.dma_start(
        out=cb,
        in_=d["conv_b"].rearrange("(c p) -> p c", p=128),
        allow_slow_non_contiguous=True))
    sc.dma(lambda e: e.dma_start(
        out=bdt,
        in_=d["b_dt"].rearrange("(c p) -> p c", p=128),
        allow_slow_non_contiguous=True))
    sc.dma(lambda e: e.dma_start(
        out=alog_s.rearrange("p (c n) -> p c n", c=16),
        in_=d["A_log"].rearrange("(c p) n -> p c n", p=128)))
    sc.dma(lambda e: e.dma_start(
        out=dpar,
        in_=d["D_param"].rearrange("(c p) -> p c", p=128),
        allow_slow_non_contiguous=True))
    sc.dma(lambda e: e.dma_start(out=qoff[:, 0:1], in_=d["qoff_f"]))
    sc.op("scalar", lambda e: e.activation(out=acol, in_=alog_s, func=AF.Exp))
    sc.op("vector", lambda e: e.tensor_scalar(out=acol, in0=acol,
                                              scalar1=-1.0, scalar2=None, op0=ALU.mult))
    sc.op("vector", lambda e: e.memset(states, 0.0))
    sc.op("vector", lambda e: e.memset(onesc, 1.0))
    sc.op("vector", lambda e: e.memset(ones16, 1.0))
    sc.op("vector", lambda e: e.memset(onesr[0:1, :], 1.0))
    sc.op("vector", lambda e: e.memset(onescol, 1.0))
    sc.op("gpsimd", lambda e: e.iota(iotaf2, [[1, 1]], base=0, channel_multiplier=1,
                                     allow_small_or_imprecise_dtypes=True))

    AF32.push(); ABF.push()
    win = ABF.alloc(8 * 4096)
    lnfm = ABF.alloc(8 * 512)
    ln3blk = ABF.alloc(1024)
    xzt = ABF.alloc(4096)
    y1blk = AF32.alloc(1024)
    sc.dma(lambda e: e.dma_start(
        out=win.rearrange("p (c o) -> p c o", c=8),
        in_=d["W_in"].rearrange("(c p) o -> p c o", p=128)), q="gpsimd")
    for bo in range(OWN // 512):
        o0 = bo * 512
        for tt in range(4):
            sc.dma(lambda e, o0=o0, tt=tt: e.dma_start(
                out=y1blk, in_=d["y1_dram"][o0 + tt * 128:o0 + (tt + 1) * 128, :]))
            ln_helper(y1blk, ln3blk, D)
            sc.dma(lambda e, o0=o0, tt=tt: e.dma_start(
                out=d["ln_dram"][o0 + tt * 128:o0 + (tt + 1) * 128, :], in_=ln3blk))
        for dc in range(8):
            sc.dma(lambda e, o0=o0, dc=dc: e.dma_start(
                out=lnfm[:, dc * 512:(dc + 1) * 512],
                in_=d["ln_dram"][o0:o0 + 512, dc * 128:(dc + 1) * 128],
                transpose=True))
        for tt in range(4):
            for ob in range(8):
                p0 = ps[0]
                for di in range(8):
                    lhs = lnfm[:, di * 512 + tt * 128: di * 512 + tt * 128 + 128]
                    sc.op("tensor", lambda e, lhs=lhs, di=di, ob=ob, p0=p0, win=win: e.matmul(
                        p0[:, 0:512], lhs,
                        win[:, di * 4096 + ob * 512: di * 4096 + (ob + 1) * 512],
                        start=(di == 0), stop=(di == 7)))
                sc.op("vector", lambda e, ob=ob, p0=p0, xzt=xzt: e.tensor_copy(
                    xzt[:, ob * 512:(ob + 1) * 512], p0[:, 0:512]))
            sc.dma(lambda e, o0=o0, tt=tt, xzt=xzt: e.dma_start(
                out=d["xz_dram"][o0 + tt * 128:o0 + (tt + 1) * 128, :], in_=xzt))
    AF32.pop(); ABF.pop()

    AF32.push(); ABF.push()
    xcbuf = ABF.alloc(4 + OWN)
    xcs = ABF.alloc(OWN)
    xcv = AF32.alloc(0)
    for dc in range(16):
        sc.op("vector", lambda e, xcbuf=xcbuf: e.memset(xcbuf[:, 0:4], 0.0))
        for tb in range(16):
            sc.dma(lambda e, dc=dc, tb=tb, xcbuf=xcbuf: e.dma_start(
                out=xcbuf[:, 4 + tb * 512: 4 + (tb + 1) * 512],
                in_=d["xz_dram"][tb * 512:(tb + 1) * 512, dc * 128:(dc + 1) * 128],
                transpose=True))
        sc.op("vector", lambda e, dc=dc, xcbuf=xcbuf, xcs=xcs: e.tensor_scalar(
            out=xcs, in0=xcbuf[:, 4:4 + OWN],
            scalar1=cw[:, dc * 4 + 3:dc * 4 + 4],
            scalar2=cb[:, dc:dc + 1], op0=ALU.mult, op1=ALU.add))
        for k in range(3):
            sc.op("vector", lambda e, dc=dc, k=k, xcbuf=xcbuf, xcs=xcs: e.scalar_tensor_tensor(
                out=xcs, in0=xcbuf[:, 1 + k:1 + k + OWN],
                scalar=cw[:, dc * 4 + k:dc * 4 + k + 1],
                in1=xcs, op0=ALU.mult, op1=ALU.add))
        sc.op("scalar", lambda e, xcs=xcs: e.activation(out=xcs, in_=xcs, func=AF.Silu))
        sc.dma(lambda e, dc=dc, xcs=xcs: e.dma_start(
            out=d["xc_dram"][dc * 128:(dc + 1) * 128, :], in_=xcs))
    AF32.pop(); ABF.pop()

    AF32.push(); ABF.push()
    wx = ABF.alloc(16 * 33)
    xctile = ABF.alloc(16 * 512)
    bsb = ABF.alloc(512)
    csb = AF32.alloc(512)
    eu = ABF.alloc(512)
    dtt = ABF.alloc(512)
    csr = AF32.alloc(512)
    dtr = ABF.alloc(512)
    sc.dma(lambda e: e.dma_start(
        out=wx.rearrange("p (c o) -> p c o", c=16),
        in_=d["W_x"].rearrange("(c p) o -> p c o", p=128)), q="gpsimd")
    for tb in range(16):
        t0 = tb * 512
        for dc in range(16):
            sc.dma(lambda e, dc=dc, t0=t0, xctile=xctile: e.dma_start(
                out=xctile[:, dc * 512:(dc + 1) * 512],
                in_=d["xc_dram"][dc * 128:(dc + 1) * 128, t0:t0 + 512]))
        pB, pC, pD = ps[0], ps[1], ps[2]
        for dc in range(16):
            lhs = xctile[:, dc * 512:(dc + 1) * 512]
            sc.op("tensor", lambda e, lhs=lhs, dc=dc, pB=pB, wx=wx: e.matmul(
                pB[0:16, 0:512], wx[:, dc * 33: dc * 33 + 16], lhs,
                start=(dc == 0), stop=(dc == 15)))
            sc.op("tensor", lambda e, lhs=lhs, dc=dc, pC=pC, wx=wx: e.matmul(
                pC[0:16, 0:512], wx[:, dc * 33 + 16: dc * 33 + 32], lhs,
                start=(dc == 0), stop=(dc == 15)))
            sc.op("tensor", lambda e, lhs=lhs, dc=dc, pD=pD, wx=wx: e.matmul(
                pD[0:1, 0:512], wx[:, dc * 33 + 32: dc * 33 + 33], lhs,
                start=(dc == 0), stop=(dc == 15)))
        sc.op("vector", lambda e, pB=pB, bsb=bsb: e.tensor_copy(
            bsb[0:16, 0:512], pB[0:16, 0:512]))
        sc.dma(lambda e, t0=t0, bsb=bsb: e.dma_start(
            out=d["b_dram"][:, t0:t0 + 512], in_=bsb[0:16, 0:512]))
        sc.op("vector", lambda e, pC=pC, csb=csb: e.tensor_copy(
            csb[0:16, 0:512], pC[0:16, 0:512]))
        p3 = ps[3]
        sc.op("tensor", lambda e, p3=p3, csb=csb: e.matmul(
            p3[0:1, 0:512], ones16[0:16, 0:1], csb[0:16, 0:512],
            start=True, stop=True))
        sc.op("vector", lambda e, p3=p3, csr=csr: e.tensor_copy(
            csr[0:1, 0:512], p3[0:1, 0:512]))
        sc.dma(lambda e, t0=t0, csr=csr: e.dma_start(
            out=d["csum_dram"][0:1, t0:t0 + 512], in_=csr[0:1, 0:512]))
        sc.op("vector", lambda e, pD=pD, dtr=dtr: e.tensor_copy(
            dtr[0:1, 0:512], pD[0:1, 0:512]))
        for dc in range(16):
            p4 = ps[4]
            sc.op("tensor", lambda e, dc=dc, p4=p4, dtr=dtr: e.matmul(
                p4[:, 0:512], wdt[0:1, dc * 128:(dc + 1) * 128],
                dtr[0:1, 0:512], start=True, stop=True))
            sc.op("scalar", lambda e, dc=dc, p4=p4, eu=eu: e.activation(
                out=eu, in_=p4[:, 0:512], func=AF.Exp, bias=bdt[:, dc:dc + 1]))
            sc.op("scalar", lambda e, eu=eu, dtt=dtt: e.activation(
                out=dtt, in_=eu, func=AF.Ln, bias=onesc[:, 0:1]))
            sc.dma(lambda e, dc=dc, t0=t0, dtt=dtt: e.dma_start(
                out=d["dt_dram"][dc * 128:(dc + 1) * 128, t0:t0 + 512], in_=dtt))
    AF32.pop(); ABF.pop()

    AF32.push(); ABF.push()
    TQ = 1024
    dtall = ABF.alloc(16 * TQ)
    uball = ABF.alloc(16 * TQ)
    bb = ABF.alloc(TQ)
    bu = ABF.alloc(TQ)
    scano = ABF.alloc(TQ)
    brow = ABF.alloc(TQ)
    ab = AF32.alloc(TQ)
    ysr = AF32.alloc(512)
    for q in range(OWN // TQ):
        t0 = q * TQ
        for dc in range(16):
            sc.dma(lambda e, dc=dc, t0=t0: e.dma_start(
                out=dtall[:, dc * TQ:(dc + 1) * TQ],
                in_=d["dt_dram"][dc * 128:(dc + 1) * 128, t0:t0 + TQ]))
            sc.dma(lambda e, dc=dc, t0=t0: e.dma_start(
                out=uball[:, dc * TQ:(dc + 1) * TQ],
                in_=d["xc_dram"][dc * 128:(dc + 1) * 128, t0:t0 + TQ]))
            sc.op("vector", lambda e, dc=dc: e.tensor_tensor(
                out=uball[:, dc * TQ:(dc + 1) * TQ],
                in0=uball[:, dc * TQ:(dc + 1) * TQ],
                in1=dtall[:, dc * TQ:(dc + 1) * TQ], op=ALU.mult))
        for n in range(D_STATE):
            sc.dma(lambda e, n=n, t0=t0: e.dma_start(
                out=brow[0:1, 0:TQ], in_=d["b_dram"][n:n + 1, t0:t0 + TQ]))
            pb = ps[5]
            for j in range(TQ // 512):
                sc.op("tensor", lambda e, j=j, pb=pb: e.matmul(
                    pb[:, 0:512], onesr[0:1, 0:128],
                    brow[0:1, j * 512:(j + 1) * 512], start=True, stop=True))
                sc.op("vector", lambda e, j=j, pb=pb: e.tensor_copy(
                    bb[:, j * 512:(j + 1) * 512], pb[:, 0:512]))
            for dc in range(16):
                sci = dc * D_STATE + n
                last = (n == D_STATE - 1 and dc == 15)
                sc.op("scalar", lambda e, dc=dc, n=n: e.activation(
                    out=ab, in_=dtall[:, dc * TQ:(dc + 1) * TQ], func=AF.Exp,
                    scale=acol[:, dc * D_STATE + n: dc * D_STATE + n + 1]))
                sc.op("vector", lambda e, dc=dc: e.tensor_tensor(
                    out=bu, in0=uball[:, dc * TQ:(dc + 1) * TQ], in1=bb, op=ALU.mult))
                sc.op("vector", lambda e, sci=sci: e.tensor_tensor_scan(
                    out=scano, data0=ab, data1=bu,
                    initial=states[:, sci:sci + 1], op0=ALU.mult, op1=ALU.add))
                sc.op("vector", lambda e, sci=sci: e.tensor_copy(
                    states[:, sci:sci + 1], scano[:, TQ - 1:TQ]))
                first = (n == 0 and dc == 0)
                for j in range(TQ // 512):
                    pyj = ps[6] if j == 0 else ps[7]
                    sc.op("tensor", lambda e, j=j, first=first, last=last, pyj=pyj: e.matmul(
                        pyj[0:1, 0:512], onescol[:, 0:1],
                        scano[:, j * 512:(j + 1) * 512],
                        start=first, stop=last))
        for j in range(TQ // 512):
            pyj = ps[6] if j == 0 else ps[7]
            sc.op("vector", lambda e, j=j, pyj=pyj, ysr=ysr: e.tensor_copy(
                ysr[0:1, 0:512], pyj[0:1, 0:512]))
            sc.dma(lambda e, j=j, t0=t0, ysr=ysr: e.dma_start(
                out=d["ysum_dram"][0:1, t0 + j * 512:t0 + (j + 1) * 512],
                in_=ysr[0:1, 0:512]))
    AF32.pop(); ABF.pop()

    AF32.push(); ABF.push()
    yrow = AF32.alloc(64)
    crow = AF32.alloc(64)
    ybb = ABF.alloc(64)
    sc.dma(lambda e: e.dma_start(
        out=yrow, in_=d["ysum_dram"].rearrange("a (p c) -> (a p) c", p=128)))
    sc.dma(lambda e: e.dma_start(
        out=crow, in_=d["csum_dram"].rearrange("a (p c) -> (a p) c", p=128)))
    sc.op("vector", lambda e: e.tensor_tensor(out=yrow, in0=yrow, in1=crow, op=ALU.mult))
    sc.op("vector", lambda e: e.tensor_scalar(out=ybb, in0=yrow,
                                              scalar1=1.0 / D_INNER, scalar2=None,
                                              op0=ALU.mult))
    sc.dma(lambda e: e.dma_start(
        out=d["ybc_dram"].rearrange("a (p c) -> (a p) c", p=128), in_=ybb))
    AF32.pop(); ABF.pop()

    AF32.push(); ABF.push()
    wout = ABF.alloc(16 * 1024)
    xtq = ABF.alloc(16 * QT)
    xzgath = ABF.alloc(4096)
    zfm = ABF.alloc(128)
    zs = ABF.alloc(128)
    yq2 = ABF.alloc(128)
    ybcq = ABF.alloc(QT)
    ybcq16 = ABF.alloc(16)
    y1q = AF32.alloc(1024)
    sc.dma(lambda e: e.dma_start(
        out=wout.rearrange("p (c o) -> p c o", c=16),
        in_=d["W_out"].rearrange("(c p) o -> p c o", p=128)), q="gpsimd")
    sc.op("vector", lambda e: e.tensor_scalar(
        out=idxf2, in0=iotaf2, scalar1=16.0, scalar2=None, op0=ALU.mult))
    sc.op("vector", lambda e: e.tensor_tensor(
        out=idxf2, in0=idxf2, in1=qoff[:, 0:1], op=ALU.add))
    sc.op("vector", lambda e: e.tensor_copy(qoffi[:, 0:1], idxf2))
    sc.dma(lambda e: e.indirect_dma_start(
        out=ybcq16, out_offset=None,
        in_=d["ybc_dram"].rearrange("a (b c) -> (a b) c", c=1),
        in_offset=IOA(ap=qoffi[:, 0:1], axis=0),
        bounds_check=REG["b_own"], oob_is_err=False), q="gpsimd")
    sc.dma(lambda e: e.dma_start(
        out=d["ybcq_dram"].rearrange("a (p c) -> (a p) c", p=128), in_=ybcq16))
    sc.dma(lambda e: e.dma_start(out=ybcq[0:1, 0:QT], in_=d["ybcq_dram"]))
    for i in range(QT // 128):
        sc.op("vector", lambda e, i=i: e.tensor_scalar(
            out=idxf2, in0=iotaf2, scalar1=float(i * 128), scalar2=None, op0=ALU.add))
        sc.op("vector", lambda e: e.tensor_tensor(
            out=idxf2, in0=idxf2, in1=qoff[:, 0:1], op=ALU.add))
        sc.op("vector", lambda e: e.tensor_copy(idxi2[:, 0:1], idxf2))
        sc.dma(lambda e: e.indirect_dma_start(
            out=xzgath, out_offset=None, in_=d["xz_dram"],
            in_offset=IOA(ap=idxi2[:, 0:1], axis=0),
            bounds_check=REG["b_own"], oob_is_err=False), q="gpsimd")
        sc.dma(lambda e, i=i: e.dma_start(
            out=d["zq_dram"][i * 128:(i + 1) * 128, :], in_=xzgath[:, 2048:4096]))
        sc.dma(lambda e: e.indirect_dma_start(
            out=y1q, out_offset=None, in_=d["y1_dram"],
            in_offset=IOA(ap=idxi2[:, 0:1], axis=0),
            bounds_check=REG["b_own"], oob_is_err=False), q="gpsimd")
        sc.dma(lambda e, i=i: e.dma_start(
            out=d["y1q_dram"][i * 128:(i + 1) * 128, :], in_=y1q))
    for dc in range(16):
        sc.op("vector", lambda e, dc=dc: e.tensor_scalar(
            out=idxf2, in0=iotaf2, scalar1=float(dc * 128), scalar2=None, op0=ALU.add))
        sc.op("vector", lambda e: e.tensor_scalar(
            out=idxf2, in0=idxf2, scalar1=float(OWN), scalar2=None, op0=ALU.mult))
        sc.op("vector", lambda e: e.tensor_tensor(
            out=idxf2, in0=idxf2, in1=qoff[:, 0:1], op=ALU.add))
        sc.op("vector", lambda e: e.tensor_copy(idxi2[:, 0:1], idxf2))
        sc.dma(lambda e, dc=dc: e.indirect_dma_start(
            out=xtq[:, dc * QT:(dc + 1) * QT], out_offset=None,
            in_=d["xc_dram"].rearrange("a (b c) -> (a b) c", c=1),
            in_offset=IOA(ap=idxi2[:, 0:1], axis=0),
            bounds_check=REG["b_flat"], oob_is_err=False), q="gpsimd")
    for tq in range(QT // 128):
        pacc0, pacc1 = ps[0], ps[1]
        for dc in range(16):
            pbc = ps[2]
            sc.op("tensor", lambda e, tq=tq, pbc=pbc: e.matmul(
                pbc[:, 0:128], onesr[0:1, 0:128],
                ybcq[0:1, tq * 128:(tq + 1) * 128], start=True, stop=True))
            sc.dma(lambda e, tq=tq, dc=dc: e.dma_start(
                out=zfm, in_=d["zq_dram"][tq * 128:(tq + 1) * 128,
                                          dc * 128:(dc + 1) * 128],
                transpose=True))
            sc.op("scalar", lambda e: e.activation(out=zs, in_=zfm, func=AF.Silu))
            sc.op("vector", lambda e, dc=dc, tq=tq, pbc=pbc: e.scalar_tensor_tensor(
                out=yq2, in0=xtq[:, dc * QT + tq * 128: dc * QT + (tq + 1) * 128],
                scalar=dpar[:, dc:dc + 1], in1=pbc[:, 0:128],
                op0=ALU.mult, op1=ALU.add))
            sc.op("vector", lambda e: e.tensor_tensor(
                out=yq2, in0=yq2, in1=zs, op=ALU.mult))
            for ob in range(2):
                pacc = pacc0 if ob == 0 else pacc1
                sc.op("tensor", lambda e, dc=dc, ob=ob, pacc=pacc: e.matmul(
                    pacc[:, 0:512], yq2,
                    wout[:, dc * 1024 + ob * 512: dc * 1024 + (ob + 1) * 512],
                    start=(dc == 0), stop=(dc == 15)))
        sc.dma(lambda e, tq=tq: e.dma_start(
            out=y1q, in_=d["y1q_dram"][tq * 128:(tq + 1) * 128, :]))
        for ob in range(2):
            pacc = pacc0 if ob == 0 else pacc1
            ysl = y1q[:, ob * 512:(ob + 1) * 512]
            sc.op("vector", lambda e, ysl=ysl, pacc=pacc: e.tensor_tensor(
                out=ysl, in0=ysl, in1=pacc[:, 0:512], op=ALU.add))
        sc.dma(lambda e, tq=tq: e.dma_start(
            out=d["y2_dram"][tq * 128:(tq + 1) * 128, :], in_=y1q))
    AF32.pop(); ABF.pop()
    AF32.pop()
    ABF.pop()


def emit_stage4(sc, ps, AF32, ABF, d, ln_helper):
    AF32.push()
    ABF.push()
    wg2 = ABF.alloc(8 * 8)
    bg2b = AF32.alloc(8)
    b2sb = ABF.alloc(1024)
    lgt = AF32.alloc(8)
    m8 = AF32.alloc(8)
    e8 = AF32.alloc(8)
    oh0 = AF32.alloc(8)
    oh1 = AF32.alloc(8)
    nm1 = AF32.alloc(1)
    den = AF32.alloc(1)
    w1c = AF32.alloc(1)
    w2c = AF32.alloc(1)
    me_all = AF32.alloc(16 * 8)
    me_em = ABF.alloc(2048)
    ln4blk = ABF.alloc(1024)
    y2blk = AF32.alloc(1024)
    lnfm = ABF.alloc(8 * 512)
    w1sb = ABF.alloc(8 * 4096)
    w2ring = ABF.alloc(1024)
    ghall = ABF.alloc(32 * 512)
    b1col = AF32.alloc(32)
    accsl = AF32.alloc(512)
    fin = AF32.alloc(1024)

    sc.dma(lambda e: e.dma_start(
        out=wg2.rearrange("p (c o) -> p c o", c=8),
        in_=d["Wg2"].rearrange("(c p) o -> p c o", p=128)), q="gpsimd")
    sc.dma(lambda e: e.dma_start(out=bg2b, in_=d["bg2_b"]))
    sc.dma(lambda e: e.dma_start(out=b2sb[0:8, :], in_=d["b2"]), q="gpsimd")

    for ex in range(E1):
        sc.dma(lambda e, ex=ex: e.dma_start(
            out=w1sb.rearrange("p (a o) -> p a o", a=32),
            in_=d["W2"][ex].rearrange("(a p) o -> p a o", p=128)), q="gpsimd")
        sc.dma(lambda e, ex=ex: e.dma_start(
            out=d["w2bf_dram"][ex].rearrange("(a p) o -> p a o", p=128),
            in_=w1sb.rearrange("p (a o) -> p a o", a=32)))

    for i in range(QT // 128):
        sc.dma(lambda e, i=i: e.dma_start(
            out=y2blk, in_=d["y2_dram"][i * 128:(i + 1) * 128, :]))
        ln_helper(y2blk, ln4blk, D)
        sc.dma(lambda e, i=i: e.dma_start(
            out=d["ln_dram"][i * 128:(i + 1) * 128, :], in_=ln4blk))
    for bo in range(QT // 512):
        for dc in range(8):
            sc.dma(lambda e, bo=bo, dc=dc: e.dma_start(
                out=lnfm[:, dc * 512:(dc + 1) * 512],
                in_=d["ln_dram"][bo * 512:(bo + 1) * 512, dc * 128:(dc + 1) * 128],
                transpose=True))
        for tt in range(4):
            i = bo * 4 + tt
            p0 = ps[0]
            for di in range(8):
                lhs = lnfm[:, di * 512 + tt * 128: di * 512 + tt * 128 + 128]
                sc.op("tensor", lambda e, lhs=lhs, di=di, p0=p0: e.matmul(
                    p0[:, 0:8], lhs, wg2[:, di * 8:(di + 1) * 8],
                    start=(di == 0), stop=(di == 7)))
            sc.op("vector", lambda e, p0=p0: e.tensor_copy(lgt, p0[:, 0:8]))
            sc.op("vector", lambda e: e.tensor_tensor(out=lgt, in0=lgt, in1=bg2b,
                                                      op=ALU.add))
            sc.op("vector", lambda e: e.max(out=m8, in_=lgt))
            sc.op("vector", lambda e: e.tensor_scalar(
                out=nm1, in0=m8[:, 0:1], scalar1=-1.0, scalar2=None, op0=ALU.mult))
            sc.op("scalar", lambda e: e.activation(out=e8, in_=m8, func=AF.Exp,
                                                   bias=nm1[:, 0:1]))
            sc.op("vector", lambda e: e.reduce_sum(out=den, in_=e8, axis=AX.X))
            sc.op("vector", lambda e: e.reciprocal(out=den, in_=den))
            sc.op("vector", lambda e: e.tensor_scalar(
                out=w1c, in0=e8[:, 0:1], scalar1=den[:, 0:1], scalar2=None,
                op0=ALU.mult))
            sc.op("vector", lambda e: e.tensor_scalar(
                out=w2c, in0=e8[:, 1:2], scalar1=den[:, 0:1], scalar2=None,
                op0=ALU.mult))
            sc.op("vector", lambda e: e.tensor_scalar(
                out=oh0, in0=lgt, scalar1=m8[:, 0:1], scalar2=None, op0=ALU.is_equal))
            sc.op("vector", lambda e: e.tensor_scalar(
                out=oh1, in0=lgt, scalar1=m8[:, 1:2], scalar2=None, op0=ALU.is_equal))
            mei = me_all[:, i * 8:(i + 1) * 8]
            sc.op("vector", lambda e, mei=mei: e.tensor_scalar(
                out=mei, in0=oh0, scalar1=w1c[:, 0:1], scalar2=None, op0=ALU.mult))
            sc.op("vector", lambda e, mei=mei: e.scalar_tensor_tensor(
                out=mei, in0=oh1, scalar=w2c[:, 0:1], in1=mei,
                op0=ALU.mult, op1=ALU.add))
            p1 = ps[1]
            sc.op("tensor", lambda e, mei=mei, p1=p1: e.transpose(
                p1[0:8, 0:128], mei, d["idf_sb"]))
            sc.op("vector", lambda e, i=i, p1=p1: e.tensor_copy(
                me_em[0:8, i * 128:(i + 1) * 128], p1[0:8, 0:128]))

    for ex in range(E1):
        sc.dma(lambda e, ex=ex: e.dma_start(
            out=w1sb.rearrange("p (c o) -> p c o", c=8),
            in_=d["W1"][ex].rearrange("(c p) o -> p c o", p=128)), q="gpsimd")
        for hc in range(32):
            sc.dma(lambda e, ex=ex, hc=hc: e.dma_start(
                out=b1col[:, hc:hc + 1],
                in_=d["b1"][ex:ex + 1, hc * 128:(hc + 1) * 128]))
        for bo in range(QT // 512):
            for dc in range(8):
                sc.dma(lambda e, bo=bo, dc=dc: e.dma_start(
                    out=lnfm[:, dc * 512:(dc + 1) * 512],
                    in_=d["ln_dram"][bo * 512:(bo + 1) * 512, dc * 128:(dc + 1) * 128],
                    transpose=True))
            for hc in range(32):
                p0 = ps[0]
                for di in range(8):
                    lhs = w1sb[:, di * 4096 + hc * 128: di * 4096 + (hc + 1) * 128]
                    sc.op("tensor", lambda e, lhs=lhs, di=di, bo=bo, p0=p0: e.matmul(
                        p0[:, 0:512], lhs, lnfm[:, di * 512:(di + 1) * 512],
                        start=(di == 0), stop=(di == 7)))
                sc.op("scalar", lambda e, hc=hc, p0=p0: e.activation(
                    out=ghall[:, hc * 512:(hc + 1) * 512], in_=p0[:, 0:512],
                    func=AF.Gelu, bias=b1col[:, hc:hc + 1]))
            for hc in range(32):
                sc.dma(lambda e, ex=ex, hc=hc: e.dma_start(
                    out=w2ring, in_=d["w2bf_dram"][ex, hc * 128:(hc + 1) * 128, :]))
                for tt in range(4):
                    for obk in range(2):
                        pacc = ps[tt * 2 + obk]
                        lhs = ghall[:, hc * 512 + tt * 128: hc * 512 + tt * 128 + 128]
                        sc.op("tensor", lambda e, lhs=lhs, hc=hc, obk=obk, pacc=pacc: e.matmul(
                            pacc[:, 0:512], lhs, w2ring[:, obk * 512:(obk + 1) * 512],
                            start=(hc == 0), stop=(hc == 31)))
            for tt in range(4):
                i = bo * 4 + tt
                for obk in range(2):
                    pacc = ps[tt * 2 + obk]
                    if ex > 0:
                        sc.dma(lambda e, i=i, obk=obk: e.dma_start(
                            out=accsl,
                            in_=d["acc_dram"][i * 128:(i + 1) * 128,
                                              obk * 512:(obk + 1) * 512]))
                        sc.op("vector", lambda e, i=i, ex=ex, pacc=pacc: e.scalar_tensor_tensor(
                            out=accsl, in0=pacc[:, 0:512],
                            scalar=me_all[:, i * 8 + ex:i * 8 + ex + 1],
                            in1=accsl, op0=ALU.mult, op1=ALU.add))
                    else:
                        sc.op("vector", lambda e, i=i, ex=ex, pacc=pacc: e.tensor_scalar(
                            out=accsl, in0=pacc[:, 0:512],
                            scalar1=me_all[:, i * 8 + ex:i * 8 + ex + 1],
                            scalar2=None, op0=ALU.mult))
                    sc.dma(lambda e, i=i, obk=obk: e.dma_start(
                        out=d["acc_dram"][i * 128:(i + 1) * 128,
                                          obk * 512:(obk + 1) * 512], in_=accsl))

    for i in range(QT // 128):
        sc.dma(lambda e, i=i: e.dma_start(
            out=fin, in_=d["acc_dram"][i * 128:(i + 1) * 128, :]))
        sc.dma(lambda e, i=i: e.dma_start(
            out=y2blk, in_=d["y2_dram"][i * 128:(i + 1) * 128, :]))
        for obk in range(2):
            p2 = ps[0]
            sc.op("tensor", lambda e, i=i, obk=obk, p2=p2: e.matmul(
                p2[:, 0:512], me_em[0:8, i * 128:(i + 1) * 128],
                b2sb[0:8, obk * 512:(obk + 1) * 512], start=True, stop=True))
            fs = fin[:, obk * 512:(obk + 1) * 512]
            sc.op("vector", lambda e, fs=fs, p2=p2: e.tensor_tensor(
                out=fs, in0=fs, in1=p2[:, 0:512], op=ALU.add))
            ys = y2blk[:, obk * 512:(obk + 1) * 512]
            sc.op("vector", lambda e, fs=fs, ys=ys: e.tensor_tensor(
                out=fs, in0=fs, in1=ys, op=ALU.add))
        sc.dma(lambda e, i=i: e.dma_start(
            out=d["out_final"][i * 128:(i + 1) * 128, :], in_=fin))
    AF32.pop()
    ABF.pop()


def emit_stage2(sc, ps, s2, d, ln_helper, stat, eps_ln_c):
    NBLK = NT // 512
    IOA = IndirectOffsetOnAxis

    sc.op("vector", lambda e: e.memset(s2["ones64"], 1.0))
    sc.op("vector", lambda e: e.memset(s2["big64"], 1e9))
    sc.op("vector", lambda e: e.memset(s2["cnt"], 0.0))
    sc.op("vector", lambda e: e.memset(s2["ones1"][0:1, :], 1.0))
    sc.op("vector", lambda e: e.memset(s2["zero8k"], 0.0))
    sc.op("gpsimd", lambda e: e.iota(s2["e512"], [[1, 1]], base=0, channel_multiplier=512,
                                     allow_small_or_imprecise_dtypes=True))
    sc.op("gpsimd", lambda e: e.iota(s2["iotaf"], [[1, 1]], base=0, channel_multiplier=1,
                                     allow_small_or_imprecise_dtypes=True))
    ownb = s2["ownb"][:, 0:1]
    sc.dma(lambda e: e.dma_start(out=ownb, in_=d["own_base_f"]))
    sc.dma(lambda e: e.dma_start(
        out=s2["rw_sb"].rearrange("p (c o) -> p c o", c=8),
        in_=d["rwT"].rearrange("(c p) o -> p c o", p=128)), q="gpsimd")
    for a in range(128):
        sc.dma(lambda e, a=a: e.dma_start(
            out=d["xbuf_dram"][a * 256:(a + 1) * 256, :].rearrange(
                "(p q) d -> p q d", p=128),
            in_=s2["zero8k"].rearrange("p (q d) -> p q d", q=2)))
    sc.dma(lambda e: e.dma_start(
        out=s2["wsgu"].rearrange("p (c o) -> p c o", c=8),
        in_=d["wsgu"].rearrange("(c p) o -> p c o", p=128)), q="gpsimd")
    sc.dma(lambda e: e.dma_start(
        out=s2["wsd"].rearrange("p (c o) -> p c o", c=6),
        in_=d["wsd"].rearrange("(c p) o -> p c o", p=128)), q="gpsimd")

    for blk in range(NBLK):
        t0 = blk * 512
        for tt in range(4):
            sc.dma(lambda e, t0=t0, tt=tt: e.dma_start(
                out=s2["y0blk"],
                in_=d["y0_dram"][t0 + tt * 128:t0 + (tt + 1) * 128, :]))
            ln_helper(s2["y0blk"], s2["ln2blk"], D)
            sc.dma(lambda e, t0=t0, tt=tt: e.dma_start(
                out=d["ln_dram"][t0 + tt * 128:t0 + (tt + 1) * 128, :],
                in_=s2["ln2blk"]))
        for dc in range(8):
            sc.dma(lambda e, t0=t0, dc=dc: e.dma_start(
                out=s2["lnfm"][:, dc * 512:(dc + 1) * 512],
                in_=d["ln_dram"][t0:t0 + 512, dc * 128:(dc + 1) * 128],
                transpose=True))
        for tt in range(4):
            p0 = ps[0]
            for di in range(8):
                lhs = s2["lnfm"][:, di * 512 + tt * 128: di * 512 + tt * 128 + 128]
                sc.op("tensor", lambda e, lhs=lhs, di=di, p0=p0: e.matmul(
                    p0[:, 0:64], lhs, s2["rw_sb"][:, di * 64:(di + 1) * 64],
                    start=(di == 0), stop=(di == 7)))
            lg = s2["logits"][:, tt * 64:(tt + 1) * 64]
            sc.op("vector", lambda e, lg=lg, p0=p0: e.tensor_copy(lg, p0[:, 0:64]))
            sc.dma(lambda e, lg=lg, t0=t0, tt=tt: e.dma_start(
                out=d["logits_dram"][t0 + tt * 128:t0 + (tt + 1) * 128, :], in_=lg))
            sc.op("vector", lambda e, lg=lg: e.max(out=s2["max8"], in_=lg))
            sc.dma(lambda e, t0=t0, tt=tt: e.dma_start(
                out=d["max8_dram"][t0 + tt * 128:t0 + (tt + 1) * 128, :], in_=s2["max8"]))
            sc.dma(lambda e, t0=t0, tt=tt: e.dma_start(
                out=d["t8_dram"][t0 + tt * 128:t0 + (tt + 1) * 128, :],
                in_=s2["max8"][:, 7:8]))
            sc.op("vector", lambda e: e.tensor_scalar(
                out=s2["nm1"], in0=s2["max8"][:, 0:1], scalar1=-1.0, scalar2=None,
                op0=ALU.mult))
            sc.op("scalar", lambda e, lg=lg: e.activation(
                out=s2["expw"], in_=lg, func=AF.Exp, bias=s2["nm1"][:, 0:1]))
            sc.op("vector", lambda e, lg=lg: e.tensor_scalar(
                out=s2["oh"], in0=lg, scalar1=s2["max8"][:, 7:8], scalar2=None,
                op0=ALU.is_ge))
            sc.op("vector", lambda e: e.tensor_tensor(
                out=s2["expw"], in0=s2["expw"], in1=s2["oh"], op=ALU.mult))
            sc.op("vector", lambda e: e.reduce_sum(out=s2["den"], in_=s2["expw"],
                                                   axis=AX.X))
            sc.op("vector", lambda e: e.reciprocal(out=s2["den"], in_=s2["den"]))
            sc.op("vector", lambda e: e.tensor_scalar(
                out=s2["wv"], in0=s2["expw"], scalar1=s2["den"][:, 0:1], scalar2=None,
                op0=ALU.mult))
            sc.dma(lambda e, t0=t0, tt=tt: e.dma_start(
                out=d["wtm_dram"][t0 + tt * 128:t0 + (tt + 1) * 128, :], in_=s2["wv"]))
            p1 = ps[1]
            sc.op("tensor", lambda e, lg=lg, p1=p1: e.transpose(
                p1[0:64, 0:128], lg, d["idf_sb"]))
            sc.op("vector", lambda e, tt=tt, p1=p1: e.tensor_copy(
                s2["em"][0:64, tt * 128:(tt + 1) * 128], p1[0:64, 0:128]))
        sc.dma(lambda e, t0=t0: e.dma_start(
            out=s2["t8row"][0:1, 0:512], in_=d["t8_dram"][t0:t0 + 512, :]))
        p2 = ps[2]
        sc.op("tensor", lambda e, p2=p2: e.matmul(
            p2[0:64, 0:512], s2["ones1"][0:1, 0:64], s2["t8row"][0:1, 0:512],
            start=True, stop=True))
        sc.op("vector", lambda e, p2=p2: e.tensor_tensor(
            out=s2["selem"][0:64, :], in0=s2["em"][0:64, :], in1=p2[0:64, 0:512],
            op=ALU.is_ge))
        sc.op("vector", lambda e: e.tensor_tensor_scan(
            out=s2["scan"][0:64, :], data0=s2["ones64"][0:64, :],
            data1=s2["selem"][0:64, :], initial=s2["cnt"][0:64, 0:1],
            op0=ALU.mult, op1=ALU.add))
        sc.op("vector", lambda e: e.tensor_copy(s2["cnt"][0:64, 0:1],
                                                s2["scan"][0:64, 511:512]))
        sc.op("vector", lambda e: e.tensor_tensor(
            out=s2["pos"][0:64, :], in0=s2["scan"][0:64, :], in1=s2["selem"][0:64, :],
            op=ALU.subtract))
        sc.op("vector", lambda e: e.scalar_tensor_tensor(
            out=s2["kept"][0:64, :], in0=s2["pos"][0:64, :], scalar=float(C0),
            in1=s2["selem"][0:64, :], op0=ALU.is_lt, op1=ALU.mult))
        sc.op("vector", lambda e: e.tensor_copy(s2["kepti"][0:64, :], s2["kept"][0:64, :]))
        sc.op("vector", lambda e: e.tensor_scalar(
            out=s2["scan"][0:64, :], in0=s2["pos"][0:64, :],
            scalar1=s2["e512"][0:64, 0:1], scalar2=None, op0=ALU.add))
        sc.op("vector", lambda e: e.select(
            out=s2["slotm"][0:64, :], mask=s2["kepti"][0:64, :],
            on_true=s2["scan"][0:64, :], on_false=s2["big64"][0:64, :]))
        for tt in range(4):
            p3 = ps[3]
            sc.op("tensor", lambda e, tt=tt, p3=p3: e.transpose(
                p3[:, 0:64], s2["slotm"][0:64, tt * 128:(tt + 1) * 128], d["idf_sb"][0:64, 0:64]))
            sc.op("vector", lambda e, p3=p3: e.tensor_copy(s2["lgown"], p3[:, 0:64]))
            sc.dma(lambda e, t0=t0, tt=tt: e.dma_start(
                out=d["slotm_dram"][t0 + tt * 128:t0 + (tt + 1) * 128, :],
                in_=s2["lgown"]))

    for i in range(OWN // 128):
        sc.op("vector", lambda e, i=i: e.tensor_scalar(
            out=s2["idxf"], in0=s2["iotaf"], scalar1=float(i * 128), scalar2=None,
            op0=ALU.add))
        sc.op("vector", lambda e: e.tensor_tensor(
            out=s2["idxf"], in0=s2["idxf"], in1=ownb, op=ALU.add))
        sc.op("vector", lambda e: e.tensor_copy(s2["idxi"][:, 0:1], s2["idxf"]))
        for name, dst, src in (
            ("ln", s2["lnown"], d["ln_dram"]),
            ("lg", s2["lgown"], d["logits_dram"]),
            ("m8", s2["m8own"], d["max8_dram"]),
            ("w", s2["wown"], d["wtm_dram"]),
            ("sm", s2["smown"], d["slotm_dram"]),
        ):
            sc.dma(lambda e, dst=dst, src=src: e.indirect_dma_start(
                out=dst, out_offset=None, in_=src,
                in_offset=IOA(ap=s2["idxi"][:, 0:1], axis=0),
                bounds_check=sc.REG["b_nt"], oob_is_err=False), q="gpsimd")
        sc.dma(lambda e, i=i: e.dma_start(
            out=d["ln2own_dram"][i * 128:(i + 1) * 128, :], in_=s2["lnown"]))
        for k in range(8):
            sc.op("vector", lambda e, k=k: e.tensor_scalar(
                out=s2["oh"], in0=s2["lgown"], scalar1=s2["m8own"][:, k:k + 1],
                scalar2=None, op0=ALU.is_equal))
            sc.op("vector", lambda e: e.tensor_tensor(
                out=s2["ohs"], in0=s2["oh"], in1=s2["smown"], op=ALU.mult))
            sc.op("vector", lambda e, i=i, k=k: e.reduce_sum(
                out=s2["slot8"][:, i * 8 + k:i * 8 + k + 1], in_=s2["ohs"], axis=AX.X))
            sc.op("vector", lambda e: e.tensor_tensor(
                out=s2["ohs"], in0=s2["oh"], in1=s2["wown"], op=ALU.mult))
            sc.op("vector", lambda e, i=i, k=k: e.reduce_sum(
                out=s2["w8"][:, i * 8 + k:i * 8 + k + 1], in_=s2["ohs"], axis=AX.X))
            sc.op("vector", lambda e, i=i, k=k: e.tensor_copy(
                s2["idxi"][:, 1:2], s2["slot8"][:, i * 8 + k:i * 8 + k + 1]))
            sc.dma(lambda e: e.indirect_dma_start(
                out=d["xbuf_dram"], out_offset=IOA(ap=s2["idxi"][:, 1:2], axis=0),
                in_=s2["lnown"], in_offset=None,
                bounds_check=sc.REG["b_slots"], oob_is_err=False), q="gpsimd")

    for ex in range(E0):
        sc.dma(lambda e, ex=ex: e.dma_start(
            out=s2["wg"].rearrange("p (c f) -> p c f", c=8),
            in_=d["wgate"][ex].rearrange("(c p) f -> p c f", p=128)), q="gpsimd")
        sc.dma(lambda e, ex=ex: e.dma_start(
            out=s2["wu"].rearrange("p (c f) -> p c f", c=8),
            in_=d["wup"][ex].rearrange("(c p) f -> p c f", p=128)), q="gpsimd")
        sc.dma(lambda e, ex=ex: e.dma_start(
            out=s2["wd"].rearrange("p (c o) -> p c o", c=6),
            in_=d["wdown"][ex].rearrange("(c p) o -> p c o", p=128)), q="gpsimd")
        for dc in range(8):
            sc.dma(lambda e, ex=ex, dc=dc: e.dma_start(
                out=s2["xfm"][:, dc * 512:(dc + 1) * 512],
                in_=d["xbuf_dram"][ex * C0:(ex + 1) * C0, dc * 128:(dc + 1) * 128],
                transpose=True))
        for st in range(4):
            for fb0, fbn in ((0, 512), (512, 256)):
                pg, pu = ps[0], ps[1]
                for di in range(8):
                    lhs = s2["xfm"][:, di * 512 + st * 128: di * 512 + st * 128 + 128]
                    sc.op("tensor", lambda e, lhs=lhs, di=di, fb0=fb0, fbn=fbn, pg=pg: e.matmul(
                        pg[:, 0:fbn], lhs, s2["wg"][:, di * F0 + fb0: di * F0 + fb0 + fbn],
                        start=(di == 0), stop=(di == 7)))
                    sc.op("tensor", lambda e, lhs=lhs, di=di, fb0=fb0, fbn=fbn, pu=pu: e.matmul(
                        pu[:, 0:fbn], lhs, s2["wu"][:, di * F0 + fb0: di * F0 + fb0 + fbn],
                        start=(di == 0), stop=(di == 7)))
                sc.op("scalar", lambda e, fbn=fbn, pg=pg: e.activation(
                    out=s2["siluscr"][:, 0:fbn], in_=pg[:, 0:fbn], func=AF.Silu))
                sc.op("vector", lambda e, st=st, fb0=fb0, fbn=fbn, pu=pu: e.tensor_tensor(
                    out=s2["hexp"][:, st * F0 + fb0: st * F0 + fb0 + fbn],
                    in0=s2["siluscr"][:, 0:fbn], in1=pu[:, 0:fbn], op=ALU.mult))
        for st in range(4):
            sc.dma(lambda e, st=st: e.dma_start(
                out=d["hexp_dram"][st * 128:(st + 1) * 128, :],
                in_=s2["hexp"][:, st * F0:(st + 1) * F0]))
        for fc in range(6):
            sc.dma(lambda e, fc=fc: e.dma_start(
                out=s2["hfm"][:, fc * 512:(fc + 1) * 512],
                in_=d["hexp_dram"][:, fc * 128:(fc + 1) * 128],
                transpose=True))
        for st in range(4):
            for obk in range(2):
                p2 = ps[2]
                for fc in range(6):
                    lhs = s2["hfm"][:, fc * 512 + st * 128: fc * 512 + st * 128 + 128]
                    sc.op("tensor", lambda e, lhs=lhs, fc=fc, obk=obk, p2=p2: e.matmul(
                        p2[:, 0:512], lhs, s2["wd"][:, fc * 1024 + obk * 512: fc * 1024 + (obk + 1) * 512],
                        start=(fc == 0), stop=(fc == 5)))
                sc.op("vector", lambda e, obk=obk, p2=p2: e.tensor_copy(
                    s2["ytile"][:, obk * 512:(obk + 1) * 512], p2[:, 0:512]))
            sc.dma(lambda e, ex=ex, st=st: e.dma_start(
                out=d["ybuf_dram"][ex * C0 + st * 128: ex * C0 + (st + 1) * 128, :],
                in_=s2["ytile"]))

    for bo in range(OWN // 512):
        o0 = bo * 512
        for dc in range(8):
            sc.dma(lambda e, o0=o0, dc=dc: e.dma_start(
                out=s2["lnfm"][:, dc * 512:(dc + 1) * 512],
                in_=d["ln2own_dram"][o0:o0 + 512, dc * 128:(dc + 1) * 128],
                transpose=True))
        for tt in range(4):
            for gb in range(3):
                p0 = ps[0]
                for di in range(8):
                    lhs = s2["lnfm"][:, di * 512 + tt * 128: di * 512 + tt * 128 + 128]
                    sc.op("tensor", lambda e, lhs=lhs, di=di, gb=gb, p0=p0: e.matmul(
                        p0[:, 0:512], lhs, s2["wsgu"][:, di * 1536 + gb * 512: di * 1536 + (gb + 1) * 512],
                        start=(di == 0), stop=(di == 7)))
                sc.op("vector", lambda e, gb=gb, p0=p0: e.tensor_copy(
                    s2["gu"][:, gb * 512:(gb + 1) * 512], p0[:, 0:512]))
            sc.op("scalar", lambda e: e.activation(
                out=s2["sgs"], in_=s2["gu"][:, 0:SF], func=AF.Silu))
            sc.op("vector", lambda e, tt=tt: e.tensor_tensor(
                out=s2["hs"][:, tt * SF:(tt + 1) * SF], in0=s2["sgs"],
                in1=s2["gu"][:, SF:2 * SF], op=ALU.mult))
            sc.dma(lambda e, o0=o0, tt=tt: e.dma_start(
                out=d["hs_dram"][o0 + tt * 128:o0 + (tt + 1) * 128, :],
                in_=s2["hs"][:, tt * SF:(tt + 1) * SF]))
        for fc in range(6):
            sc.dma(lambda e, o0=o0, fc=fc: e.dma_start(
                out=s2["hsfm"][:, fc * 512:(fc + 1) * 512],
                in_=d["hs_dram"][o0:o0 + 512, fc * 128:(fc + 1) * 128],
                transpose=True))
        for tt in range(4):
            i = bo * 4 + tt
            sc.op("vector", lambda e: e.memset(s2["accv"], 0.0))
            for k in range(8):
                sc.op("vector", lambda e, i=i, k=k: e.tensor_copy(
                    s2["idxi"][:, 1:2], s2["slot8"][:, i * 8 + k:i * 8 + k + 1]))
                sc.op("vector", lambda e: e.memset(s2["ygath"], 0.0))
                sc.dma(lambda e: e.indirect_dma_start(
                    out=s2["ygath"], out_offset=None, in_=d["ybuf_dram"],
                    in_offset=IOA(ap=s2["idxi"][:, 1:2], axis=0),
                    bounds_check=sc.REG["b_slots"], oob_is_err=False), q="gpsimd")
                sc.op("vector", lambda e, i=i, k=k: e.scalar_tensor_tensor(
                    out=s2["accv"], in0=s2["ygath"],
                    scalar=s2["w8"][:, i * 8 + k:i * 8 + k + 1],
                    in1=s2["accv"], op0=ALU.mult, op1=ALU.add))
            sc.op("vector", lambda e, i=i: e.tensor_scalar(
                out=s2["idxf"], in0=s2["iotaf"], scalar1=float(i * 128), scalar2=None,
                op0=ALU.add))
            sc.op("vector", lambda e: e.tensor_tensor(
                out=s2["idxf"], in0=s2["idxf"], in1=ownb, op=ALU.add))
            sc.op("vector", lambda e: e.tensor_copy(s2["idxi"][:, 0:1], s2["idxf"]))
            sc.dma(lambda e: e.indirect_dma_start(
                out=s2["y0own"], out_offset=None, in_=d["y0_dram"],
                in_offset=IOA(ap=s2["idxi"][:, 0:1], axis=0),
                bounds_check=sc.REG["b_nt"], oob_is_err=False), q="gpsimd")
            for obk in range(2):
                p1 = ps[1]
                for fc in range(6):
                    lhs = s2["hsfm"][:, fc * 512 + tt * 128: fc * 512 + tt * 128 + 128]
                    sc.op("tensor", lambda e, lhs=lhs, fc=fc, obk=obk, p1=p1: e.matmul(
                        p1[:, 0:512], lhs, s2["wsd"][:, fc * 1024 + obk * 512: fc * 1024 + (obk + 1) * 512],
                        start=(fc == 0), stop=(fc == 5)))
                asl = s2["accv"][:, obk * 512:(obk + 1) * 512]
                sc.op("vector", lambda e, asl=asl, p1=p1: e.tensor_tensor(
                    out=asl, in0=asl, in1=p1[:, 0:512], op=ALU.add))
                ysl = s2["y0own"][:, obk * 512:(obk + 1) * 512]
                sc.op("vector", lambda e, asl=asl, ysl=ysl: e.tensor_tensor(
                    out=ysl, in0=ysl, in1=asl, op=ALU.add))
            sc.dma(lambda e, i=i: e.dma_start(
                out=d["y1_dram"][i * 128:(i + 1) * 128, :], in_=s2["y0own"]))


def build_program(dbg=(), s3only=False):
    nc = bass.Bass("TRN2", target_bir_lowering=False, detect_race_conditions=False,
                   dynamic_dma_scratch_size=32768, num_swdge_queues=4)

    def dram_in(name, shape, dt):
        return nc.dram_tensor(name, shape, dt, kind="ExternalInput").ap()

    taps = []

    def dram_tmp(name, shape, dt):
        t = nc.dram_tensor(name, shape, dt, kind="Internal").ap()
        if name in dbg:
            o = nc.dram_tensor("tap_" + name, shape, dt, kind="ExternalOutput").ap()
            taps.append((o, t))
        return t

    x_tm = dram_in("x_tm", [NT, D], F32)
    Wq = dram_in("Wq", [D, D], F32)
    Wk = dram_in("Wk", [D, D], F32)
    Wv = dram_in("Wv", [D, D], F32)
    Wg = dram_in("Wg", [D, D], F32)
    Wo = dram_in("Wo", [D, D], F32)
    rwT = dram_in("rwT", [D, E0], F32)
    wgate = dram_in("wgate", [E0, D, F0], F32)
    wup = dram_in("wup", [E0, D, F0], F32)
    wdown = dram_in("wdown", [E0, F0, D], F32)
    wsgu = dram_in("wsgu", [D, 2 * SF], F32)
    wsd = dram_in("wsd", [SF, D], F32)
    own_base_f = dram_in("own_base_f", [128, 1], F32)
    W_in = dram_in("W_in", [D, 2 * D_INNER], F32)
    conv_w = dram_in("conv_w", [D_INNER, D_CONV], F32)
    conv_b = dram_in("conv_b", [D_INNER], F32)
    W_x = dram_in("W_x", [D_INNER, 2 * D_STATE + 1], F32)
    W_dt = dram_in("W_dt", [1, D_INNER], F32)
    b_dt = dram_in("b_dt", [D_INNER], F32)
    A_log = dram_in("A_log", [D_INNER, D_STATE], F32)
    D_param = dram_in("D_param", [D_INNER], F32)
    W_out = dram_in("W_out", [D_INNER, D], F32)
    qoff_f = dram_in("qoff_f", [128, 1], F32)
    Wg2 = dram_in("Wg2", [D, E1], F32)
    bg2_b = dram_in("bg2_b", [128, E1], F32)
    W1 = dram_in("W1", [E1, D, H1], F32)
    b1 = dram_in("b1", [E1, H1], F32)
    W2 = dram_in("W2", [E1, H1, D], F32)
    b2 = dram_in("b2", [E1, D], F32)
    masks_c = dram_in("masks_c", [128, H * 128], F32)
    gw_c = dram_in("gw_c", [128, H], F32)
    ident_bf = dram_in("ident_bf", [128, 128], BF16)
    ident_f32 = dram_in("ident_f32", [128, 128], F32)

    y0_dram = dram_tmp("y0_tm", [NT, D], F32)

    out_final = nc.dram_tensor("out", [QT, D], F32, kind="ExternalOutput").ap()

    ln_dram = dram_tmp("ln_dram", [NT, D], BF16)
    gated_dram = dram_tmp("gated_dram", [512, D], BF16)
    logits_dram = dram_tmp("logits_dram", [NT, E0], F32)
    max8_dram = dram_tmp("max8_dram", [NT, 8], F32)
    wtm_dram = dram_tmp("wtm_dram", [NT, E0], F32)
    t8_dram = dram_tmp("t8_dram", [NT, 1], F32)
    slotm_dram = dram_tmp("slotm_dram", [NT, E0], F32)
    ln2own_dram = dram_tmp("ln2own_dram", [OWN, D], BF16)
    xbuf_dram = dram_tmp("xbuf_dram", [E0 * C0, D], BF16)
    ybuf_dram = dram_tmp("ybuf_dram", [E0 * C0, D], BF16)
    hexp_dram = dram_tmp("hexp_dram", [C0, F0], BF16)
    hs_dram = dram_tmp("hs_dram", [OWN, SF], BF16)
    y1_dram = (dram_in("y1_in", [OWN, D], F32) if s3only
               else dram_tmp("y1_tm", [OWN, D], F32))
    xz_dram = dram_tmp("xz_dram", [OWN, 2 * D_INNER], BF16)
    xc_dram = dram_tmp("xc_dram", [D_INNER, OWN], BF16)
    dt_dram = dram_tmp("dt_dram", [D_INNER, OWN], BF16)
    b_dram = dram_tmp("b_dram", [D_STATE, OWN], BF16)
    csum_dram = dram_tmp("csum_dram", [1, OWN], F32)
    ysum_dram = dram_tmp("ysum_dram", [1, OWN], F32)
    ybc_dram = dram_tmp("ybc_dram", [1, OWN], BF16)
    ybcq_dram = dram_tmp("ybcq_dram", [1, QT], BF16)
    zq_dram = dram_tmp("zq_dram", [QT, D_INNER], BF16)
    y1q_dram = dram_tmp("y1q_dram", [QT, D], F32)
    y2_dram = dram_tmp("y2_tm", [QT, D], F32)
    w2bf_dram = dram_tmp("w2bf_dram", [E1, H1, D], BF16)
    acc_dram = dram_tmp("acc_dram", [QT, D], F32)

    sc = Script()
    REG = {}
    sc.REG = REG

    def reg_setup(eng):
        for name, val in (("b_nt", NT - 1), ("b_slots", E0 * C0 - 1),
                          ("b_own", OWN - 1), ("b_qt", QT - 1),
                          ("b_flat", D_INNER * OWN - 1)):
            r = eng.alloc_register(name)
            eng.reg_mov(r, val)
            REG[name] = r


    sc.setup("gpsimd", reg_setup)

    with ExitStack() as stack:
        def sb(name, shape, dt):
            return stack.enter_context(nc.sbuf_tensor(name, shape, dt))

        AF32 = Arena(sb("AF32", [128, 10000], F32), 10000)
        ABF = Arena(sb("ABF", [128, 68000], BF16), 68000)

        ps = [stack.enter_context(nc.psum_tensor(f"ps{i}", [128, 512], F32))
              for i in range(8)]

        ticks = [stack.enter_context(nc.semaphore(f"tick{i}")) for i in range(24)]
        block = stack.enter_context(nc.Block())

        masks_sb = AF32.alloc(H * 128)
        gw_sb = AF32.alloc(H)
        idbf_sb = ABF.alloc(128)
        idf_sb = AF32.alloc(128)

        sc.dma(lambda e: e.dma_start(out=masks_sb, in_=masks_c))
        sc.dma(lambda e: e.dma_start(out=gw_sb, in_=gw_c))
        sc.dma(lambda e: e.dma_start(out=idbf_sb, in_=ident_bf))
        sc.dma(lambda e: e.dma_start(out=idf_sb, in_=ident_f32))

        sq_scr = ABF.alloc(1024)
        stat = AF32.alloc(16)

        AF32.push()
        ABF.push()
        w_sb = {}
        for name, w in [("Wq", Wq), ("Wk", Wk), ("Wv", Wv), ("Wg", Wg), ("Wo", Wo)]:
            t = ABF.alloc(8 * 1024)
            w_sb[name] = t
            sc.dma(lambda e, t=t, w=w: e.dma_start(
                out=t.rearrange("p (c o) -> p c o", c=8),
                in_=w.rearrange("(c p) o -> p c o", p=128)), q="gpsimd")

        def wslice(name, cin, lo, n):
            return w_sb[name][:, cin * 1024 + lo: cin * 1024 + lo + n]

        xblk = AF32.alloc(4 * 1024)
        lnblk = ABF.alloc(1024)
        lnfm = ABF.alloc(8 * 512)
        qfm = ABF.alloc(8 * 512)
        kfm = ABF.alloc(8 * 512)
        vtm = ABF.alloc(1024)
        kwtm = ABF.alloc(1024)
        gstm = ABF.alloc(1024)
        retn = ABF.alloc(1024)
        gatedb = retn
        gatedfm = lnfm
        sT_bf = ABF.alloc(128)
        S_f32 = AF32.alloc(8 * 128)
        S_bf = ABF.alloc(8 * 128)

        eps_ln_c = stat[:, 10:11]
        eps_gn_c = stat[:, 11:12]
        sc.op("vector", lambda e: e.memset(eps_ln_c, EPS_LN))
        sc.op("vector", lambda e: e.memset(eps_gn_c, EPS_GN))

        s_col = stat[:, 0:1]
        sq_col = stat[:, 1:2]
        negm = stat[:, 2:3]
        var_c = stat[:, 3:4]
        rstd = stat[:, 4:5]
        m2 = stat[:, 5:6]
        sq2 = stat[:, 6:7]
        negm2 = stat[:, 7:8]
        var2 = stat[:, 8:9]
        rstd2 = stat[:, 9:10]

        def ln_stats_and_norm(xtile, lntile, dcols):
            sc.op("vector", lambda e: e.reduce_sum(out=s_col, in_=xtile, axis=AX.X))
            sc.op("scalar", lambda e: e.activation(out=sq_scr[:, :dcols], in_=xtile,
                                                   func=AF.Square, accum_out=sq_col))
            sc.op("vector", lambda e: e.tensor_scalar(out=negm, in0=s_col,
                                                      scalar1=-1.0 / dcols, scalar2=None,
                                                      op0=ALU.mult))
            sc.op("vector", lambda e: e.tensor_scalar(out=sq2, in0=sq_col,
                                                      scalar1=1.0 / dcols, scalar2=None,
                                                      op0=ALU.mult))
            sc.op("vector", lambda e: e.scalar_tensor_tensor(
                out=var_c, in0=negm, scalar=negm, in1=sq2,
                op0=ALU.mult, op1=ALU.subtract))
            sc.op("vector", lambda e: e.tensor_scalar(out=var_c, in0=var_c,
                                                      scalar1=-1.0, scalar2=None,
                                                      op0=ALU.mult))
            sc.op("scalar", lambda e: e.activation(out=rstd, in_=var_c, func=AF.Sqrt,
                                                   bias=eps_ln_c))
            sc.op("vector", lambda e: e.reciprocal(out=rstd, in_=rstd))
            sc.op("vector", lambda e: e.tensor_scalar(out=lntile, in0=xtile,
                                                      scalar1=negm, scalar2=rstd,
                                                      op0=ALU.add, op1=ALU.mult))

        NBLK = NT // 512
        for blk in range([] if s3only else range(NBLK)) if False else (range(0) if s3only else range(NBLK)):
            t0 = blk * 512
            for tt in range(4):
                sc.dma(lambda e, t0=t0, tt=tt: e.dma_start(
                    out=xblk[:, tt * 1024:(tt + 1) * 1024],
                    in_=x_tm[t0 + tt * 128:t0 + (tt + 1) * 128, :]))
                ln_stats_and_norm(xblk[:, tt * 1024:(tt + 1) * 1024], lnblk, D)
                sc.dma(lambda e, t0=t0, tt=tt: e.dma_start(
                    out=ln_dram[t0 + tt * 128:t0 + (tt + 1) * 128, :], in_=lnblk))
            for dc in range(8):
                sc.dma(lambda e, t0=t0, dc=dc: e.dma_start(
                    out=lnfm[:, dc * 512:(dc + 1) * 512],
                    in_=ln_dram[t0:t0 + 512, dc * 128:(dc + 1) * 128],
                    transpose=True))
            for which, dst in (("Wq", qfm), ("Wk", kfm)):
                for do in range(8):
                    pb = ps[0]
                    for di in range(8):
                        sc.op("tensor", lambda e, which=which, do=do, di=di, pb=pb: e.matmul(
                            pb[:, 0:512], wslice(which, di, do * 128, 128),
                            lnfm[:, di * 512:(di + 1) * 512],
                            start=(di == 0), stop=(di == 7)))
                    sc.op("vector", lambda e, do=do, dst=dst, pb=pb: e.tensor_copy(
                        dst[:, do * 512:(do + 1) * 512], pb[:, 0:512]))
            for tt in range(4):
                for dvb in range(2):
                    pv, pk, pg = ps[1], ps[2], ps[3]
                    for di in range(8):
                        lhs = lnfm[:, di * 512 + tt * 128: di * 512 + tt * 128 + 128]
                        sc.op("tensor", lambda e, lhs=lhs, di=di, dvb=dvb, pv=pv: e.matmul(
                            pv[:, 0:512], lhs, wslice("Wv", di, dvb * 512, 512),
                            start=(di == 0), stop=(di == 7)))
                        sc.op("tensor", lambda e, lhs=lhs, di=di, dvb=dvb, pk=pk: e.matmul(
                            pk[:, 0:512], lhs, wslice("Wk", di, dvb * 512, 512),
                            start=(di == 0), stop=(di == 7)))
                        sc.op("tensor", lambda e, lhs=lhs, di=di, dvb=dvb, pg=pg: e.matmul(
                            pg[:, 0:512], lhs, wslice("Wg", di, dvb * 512, 512),
                            start=(di == 0), stop=(di == 7)))
                    vdst = vtm[:, dvb * 512:(dvb + 1) * 512]
                    sc.op("vector", lambda e, vdst=vdst, pv=pv: e.tensor_copy(vdst, pv[:, 0:512]))
                    for hh in range(4):
                        h = dvb * 4 + hh
                        kdst = kwtm[:, h * 128:(h + 1) * 128]
                        sc.op("vector", lambda e, kdst=kdst, pk=pk, hh=hh, h=h: e.tensor_scalar(
                            out=kdst, in0=pk[:, hh * 128:(hh + 1) * 128],
                            scalar1=gw_sb[:, h:h + 1], scalar2=None, op0=ALU.mult))
                    gdst = gstm[:, dvb * 512:(dvb + 1) * 512]
                    sc.op("scalar", lambda e, gdst=gdst, pg=pg: e.activation(
                        out=gdst, in_=pg[:, 0:512], func=AF.Silu))
                c = blk * 4 + tt
                batch_start = (c % (S // 128) == 0)
                for h in range(H):
                    qsl = qfm[:, h * 512 + tt * 128: h * 512 + tt * 128 + 128]
                    ksl = kfm[:, h * 512 + tt * 128: h * 512 + tt * 128 + 128]
                    vsl = vtm[:, h * 128:(h + 1) * 128]
                    kwsl = kwtm[:, h * 128:(h + 1) * 128]
                    Sf = S_f32[:, h * 128:(h + 1) * 128]
                    Sb = S_bf[:, h * 128:(h + 1) * 128]
                    p_sc, p_ret, p_upd = ps[4], ps[5], ps[6]
                    sc.op("tensor", lambda e, ksl=ksl, qsl=qsl, p_sc=p_sc: e.matmul(
                        p_sc[:, 0:128], ksl, qsl, start=True, stop=True))
                    sc.op("vector", lambda e, h=h, p_sc=p_sc: e.tensor_tensor(
                        out=sT_bf, in0=p_sc[:, 0:128],
                        in1=masks_sb[:, h * 128:(h + 1) * 128], op=ALU.mult))
                    if not batch_start:
                        sc.op("tensor", lambda e, qsl=qsl, Sb=Sb, p_ret=p_ret: e.matmul(
                            p_ret[:, 0:128], qsl, Sb, start=True, stop=False))
                    sc.op("tensor", lambda e, vsl=vsl, p_ret=p_ret, bs=batch_start: e.matmul(
                        p_ret[:, 0:128], sT_bf, vsl, start=bs, stop=True))
                    sc.op("vector", lambda e, p_ret=p_ret: e.reduce_sum(
                        out=m2, in_=p_ret[:, 0:128], axis=AX.X))
                    sc.op("scalar", lambda e, p_ret=p_ret: e.activation(
                        out=sq_scr[:, 0:128], in_=p_ret[:, 0:128], func=AF.Square,
                        accum_out=sq2))
                    sc.op("vector", lambda e: e.tensor_scalar(
                        out=negm2, in0=m2, scalar1=-1.0 / DH, scalar2=None, op0=ALU.mult))
                    sc.op("vector", lambda e: e.tensor_scalar(
                        out=var2, in0=sq2, scalar1=1.0 / DH, scalar2=None, op0=ALU.mult))
                    sc.op("vector", lambda e: e.scalar_tensor_tensor(
                        out=var2, in0=negm2, scalar=negm2, in1=var2,
                        op0=ALU.mult, op1=ALU.subtract))
                    sc.op("vector", lambda e: e.tensor_scalar(
                        out=var2, in0=var2, scalar1=-1.0, scalar2=None, op0=ALU.mult))
                    sc.op("scalar", lambda e: e.activation(out=rstd2, in_=var2,
                                                           func=AF.Sqrt, bias=eps_gn_c))
                    sc.op("vector", lambda e: e.reciprocal(out=rstd2, in_=rstd2))
                    rdst = retn[:, h * 128:(h + 1) * 128]
                    sc.op("vector", lambda e, rdst=rdst, p_ret=p_ret: e.tensor_scalar(
                        out=rdst, in0=p_ret[:, 0:128], scalar1=negm2, scalar2=rstd2,
                        op0=ALU.add, op1=ALU.mult))
                    sc.op("tensor", lambda e, kwsl=kwsl, vsl=vsl, p_upd=p_upd: e.matmul(
                        p_upd[:, 0:128], kwsl, vsl, start=True, stop=True))
                    g128 = GAMMAS[h] ** 128
                    if batch_start:
                        sc.op("vector", lambda e, Sf=Sf, p_upd=p_upd: e.tensor_copy(
                            Sf, p_upd[:, 0:128]))
                    else:
                        sc.op("vector", lambda e, Sf=Sf, p_upd=p_upd, g128=g128:
                              e.scalar_tensor_tensor(out=Sf, in0=Sf, scalar=g128,
                                                     in1=p_upd[:, 0:128],
                                                     op0=ALU.mult, op1=ALU.add))
                    sc.op("vector", lambda e, Sb=Sb, Sf=Sf: e.tensor_copy(Sb, Sf))
                sc.op("vector", lambda e: e.tensor_tensor(
                    out=gatedb, in0=retn, in1=gstm, op=ALU.mult))
                sc.dma(lambda e, tt=tt: e.dma_start(
                    out=gated_dram[tt * 128:(tt + 1) * 128, :], in_=gatedb))
            for dc in range(8):
                sc.dma(lambda e, dc=dc: e.dma_start(
                    out=gatedfm[:, dc * 512:(dc + 1) * 512],
                    in_=gated_dram[:, dc * 128:(dc + 1) * 128],
                    transpose=True))
            for tt in range(4):
                for dvb in range(2):
                    pb = ps[7]
                    for di in range(8):
                        lhs = gatedfm[:, di * 512 + tt * 128: di * 512 + tt * 128 + 128]
                        sc.op("tensor", lambda e, lhs=lhs, di=di, dvb=dvb, pb=pb: e.matmul(
                            pb[:, 0:512], lhs, wslice("Wo", di, dvb * 512, 512),
                            start=(di == 0), stop=(di == 7)))
                    dst = xblk[:, tt * 1024 + dvb * 512: tt * 1024 + (dvb + 1) * 512]
                    sc.op("vector", lambda e, dst=dst, pb=pb: e.tensor_tensor(
                        out=dst, in0=dst, in1=pb[:, 0:512], op=ALU.add))
            for tt in range(4):
                sc.dma(lambda e, t0=t0, tt=tt: e.dma_start(
                    out=y0_dram[t0 + tt * 128:t0 + (tt + 1) * 128, :],
                    in_=xblk[:, tt * 1024:(tt + 1) * 1024]))

        AF32.pop()
        ABF.pop()

        AF32.push()
        ABF.push()
        s2 = {}
        s2["y0blk"] = AF32.alloc(1024)
        s2["ln2blk"] = ABF.alloc(1024)
        s2["lnfm"] = ABF.alloc(8 * 512)
        s2["logits"] = AF32.alloc(4 * 64)
        s2["max8"] = AF32.alloc(8)
        s2["nm1"] = AF32.alloc(1)
        s2["expw"] = AF32.alloc(64)
        s2["den"] = AF32.alloc(1)
        s2["wv"] = AF32.alloc(64)
        s2["em"] = AF32.alloc(512)
        s2["selem"] = AF32.alloc(512)
        s2["scan"] = AF32.alloc(512)
        s2["pos"] = AF32.alloc(512)
        s2["slotm"] = AF32.alloc(512)
        s2["ones64"] = AF32.alloc(512)
        s2["big64"] = AF32.alloc(512)
        s2["kept"] = AF32.alloc(512)
        s2["kepti"] = stack.enter_context(nc.sbuf_tensor("kepti", [128, 512], I32))
        s2["e512"] = AF32.alloc(1)
        s2["cnt"] = AF32.alloc(1)
        s2["t8row"] = AF32.alloc(512)
        s2["ones1"] = AF32.alloc(64)
        s2["iotaf"] = AF32.alloc(1)
        s2["idxf"] = AF32.alloc(1)
        s2["idxi"] = stack.enter_context(nc.sbuf_tensor("idxi", [128, 8], I32))
        s2["lnown"] = ABF.alloc(1024)
        s2["lgown"] = AF32.alloc(64)
        s2["m8own"] = AF32.alloc(8)
        s2["wown"] = AF32.alloc(64)
        s2["smown"] = AF32.alloc(64)
        s2["oh"] = AF32.alloc(64)
        s2["ohs"] = AF32.alloc(64)
        s2["slot8"] = AF32.alloc(64 * 8)
        s2["w8"] = AF32.alloc(64 * 8)
        s2["zero8k"] = ABF.alloc(2048)
        s2["accv"] = AF32.alloc(1024)
        s2["ownb"] = AF32.alloc(1)
        s2["ygath"] = ABF.alloc(1024)
        s2["y0own"] = s2["y0blk"]
        s2["rw_sb"] = ABF.alloc(8 * 64)
        s2["wg"] = ABF.alloc(8 * F0)
        s2["wu"] = ABF.alloc(8 * F0)
        s2["wd"] = ABF.alloc(6 * 1024)
        s2["xfm"] = ABF.alloc(8 * 512)
        s2["hexp"] = ABF.alloc(4 * F0)
        s2["hfm"] = ABF.alloc(6 * 512)
        s2["ytile"] = ABF.alloc(1024)
        s2["siluscr"] = ABF.alloc(512)
        s2["wsgu"] = ABF.alloc(8 * 2 * SF)
        s2["wsd"] = ABF.alloc(6 * 1024)
        s2["gu"] = ABF.alloc(1536)
        s2["sgs"] = ABF.alloc(SF)
        s2["hs"] = ABF.alloc(4 * SF)
        s2["hsfm"] = ABF.alloc(6 * 512)

        if not s3only:
            emit_stage2(sc, ps, s2, dict(
                ln_dram=ln_dram, y0_dram=y0_dram, rwT=rwT, wgate=wgate, wup=wup,
                wdown=wdown, wsgu=wsgu, wsd=wsd, own_base_f=own_base_f,
                logits_dram=logits_dram, max8_dram=max8_dram, wtm_dram=wtm_dram,
                t8_dram=t8_dram, slotm_dram=slotm_dram, ln2own_dram=ln2own_dram,
                xbuf_dram=xbuf_dram, ybuf_dram=ybuf_dram, hexp_dram=hexp_dram,
                hs_dram=hs_dram, y1_dram=y1_dram, idf_sb=idf_sb,
            ), ln_stats_and_norm, stat, eps_ln_c)

        AF32.pop()
        ABF.pop()

        def xsb(name, shape, dt):
            return stack.enter_context(nc.sbuf_tensor(name, shape, dt))

        emit_stage3(sc, ps, AF32, ABF, xsb, dict(
            y1_dram=y1_dram, ln_dram=ln_dram, W_in=W_in, conv_w=conv_w,
            conv_b=conv_b, W_x=W_x, W_dt=W_dt, b_dt=b_dt, A_log=A_log,
            D_param=D_param, W_out=W_out, qoff_f=qoff_f,
            xz_dram=xz_dram, xc_dram=xc_dram, dt_dram=dt_dram, b_dram=b_dram,
            csum_dram=csum_dram, ysum_dram=ysum_dram, ybc_dram=ybc_dram,
            zq_dram=zq_dram, y1q_dram=y1q_dram, y2_dram=y2_dram,
            ybcq_dram=ybcq_dram,
        ), ln_stats_and_norm)

        emit_stage4(sc, ps, AF32, ABF, dict(
            y2_dram=y2_dram, ln_dram=ln_dram, Wg2=Wg2, bg2_b=bg2_b, W1=W1, b1=b1,
            W2=W2, b2=b2, w2bf_dram=w2bf_dram, acc_dram=acc_dram,
            out_final=out_final, idf_sb=idf_sb,
        ), ln_stats_and_norm)

        if taps:
            tap_scr_bf = ABF.alloc(8192)
            tap_scr_f = AF32.alloc(2048)
            for o_, t_ in taps:
                if len(t_.shape) == 3:
                    a_, b_, c_ = t_.shape
                    t_ = t_.rearrange("a b c -> (a b) c")
                    o_ = o_.rearrange("a b c -> (a b) c")
                rows, width = t_.shape
                scr = tap_scr_bf if t_.dtype == BF16 else tap_scr_f
                if rows % 128 == 0 and width <= scr.shape[1]:
                    for rr in range(rows // 128):
                        sc.dma(lambda e, t_=t_, rr=rr, width=width, scr=scr: e.dma_start(
                            out=scr[:, 0:width], in_=t_[rr * 128:(rr + 1) * 128, :]))
                        sc.dma(lambda e, o_=o_, rr=rr, width=width, scr=scr: e.dma_start(
                            out=o_[rr * 128:(rr + 1) * 128, :], in_=scr[:, 0:width]))
                else:
                    sc.dma(lambda e, o_=o_, t_=t_: e.dma_start(out=o_, in_=t_))

        if "y0_tm" in dbg or "y1_tm" in dbg:
            sc.dma(lambda e: e.dma_start(
                out=out_final[0:128, :], in_=s2["y0blk"][:, 0:1024]))

        sc.emit(block, ticks)

    return nc


def make_consts():
    t = np.arange(128, dtype=np.float64)
    masks = np.zeros((128, H * 128), dtype=np.float32)
    gw = np.zeros((128, H), dtype=np.float32)
    for h in range(H):
        g = GAMMAS[h]
        colv = np.power(g, -(t + 1.0)) / np.sqrt(DH)
        m = np.where(t[:, None] <= t[None, :], colv[:, None], 0.0)
        masks[:, h * 128:(h + 1) * 128] = m.astype(np.float32)
        gw[:, h] = (np.power(g, 127.0 - t) / np.sqrt(DH)).astype(np.float32)
    ident = np.eye(128, dtype=np.float32)
    return dict(masks_c=masks, gw_c=gw,
                ident_bf=ident.astype(ml_dtypes.bfloat16), ident_f32=ident)


def kernel(**inputs):
    z = inputs
    x_tm = np.ascontiguousarray(z["L0_attn_X"].reshape(-1, D).astype(np.float32))
    consts = make_consts()
    common = dict(
        x_tm=x_tm,
        Wq=z["L0_attn_Wq"], Wk=z["L0_attn_Wk"], Wv=z["L0_attn_Wv"],
        Wg=z["L0_attn_W_G"], Wo=z["L0_attn_W_O"],
        rwT=np.ascontiguousarray(z["L0_ffn_router_w"].T),
        wgate=z["L0_ffn_w_gate"], wup=z["L0_ffn_w_up"], wdown=z["L0_ffn_w_down"],
        wsgu=z["L0_ffn_ws_gate_up"], wsd=z["L0_ffn_ws_down"],
        W_in=z["L1_attn_W_in"], conv_w=z["L1_attn_conv_w"],
        conv_b=z["L1_attn_conv_b"], W_x=z["L1_attn_W_x"], W_dt=z["L1_attn_W_dt"],
        b_dt=z["L1_attn_b_dt"], A_log=z["L1_attn_A_log"],
        D_param=z["L1_attn_D_param"], W_out=z["L1_attn_W_out"],
        Wg2=z["L1_ffn_Wg"],
        bg2_b=np.ascontiguousarray(np.broadcast_to(z["L1_ffn_bg"], (128, E1))),
        W1=z["L1_ffn_W1"], b1=z["L1_ffn_b1"], W2=z["L1_ffn_W2"], b2=z["L1_ffn_b2"],
        **consts)
    in_maps = []
    for c in range(8):
        b, q = c // 4, c % 4
        m = dict(common)
        m["own_base_f"] = np.full((128, 1), float(b * S), np.float32)
        m["qoff_f"] = np.full((128, 1), float(q * QT), np.float32)
        in_maps.append(m)
    nc = build_program()
    res = run_bass_kernel_spmd(nc, in_maps, core_ids=list(range(8)))
    out = np.empty((B, S, D), np.float32)
    for c in range(8):
        b, q = c // 4, c % 4
        out[b, q * QT:(q + 1) * QT, :] = np.asarray(res.results[c]["out"])
    return out


if __name__ == "__main__":
    pass


# version 09
# speedup vs baseline: 1.2846x; 1.0199x over previous
import numpy as np
import ml_dtypes
import concourse.bass as bass
import concourse.mybir as mybir
from concourse.bass import IndirectOffsetOnAxis
from concourse.bass_utils import run_bass_kernel_spmd
from contextlib import ExitStack

F32 = mybir.dt.float32
BF16 = mybir.dt.bfloat16
I32 = mybir.dt.int32
U32 = mybir.dt.uint32
AF = mybir.ActivationFunctionType
ALU = mybir.AluOpType
AX = mybir.AxisListType

B, S, D = 2, 8192, 1024
NT = B * S
H, DH = 8, 128
EPS_LN = 1e-5
EPS_GN = 1e-6
E0, K0, F0, C0 = 64, 8, 768, 512
SF = 768
D_INNER, D_STATE, D_CONV = 2048, 16, 4
E1, K1, H1 = 8, 2, 4096
OWN = S
QT = 2048

GAMMAS = [1.0 - 2.0 ** (-5.0 - h) for h in range(H)]


class Script:
    def __init__(self):
        self.items = []
        self.setups = {}

    def setup(self, eng, fn):
        self.setups.setdefault(eng, []).append(fn)

    def op(self, eng, fn, inc=1, tag=None):
        self.items.append((eng, fn, inc, tag))

    def dma(self, fn, q="sync", tag=None):
        self.op(q, fn, 16, tag)

    EPOCH_LIMIT = 24000

    def plan(self):
        self.sched = []
        bounds = []
        ep, v = 0, 0
        for e, fn, inc, tag in self.items:
            if v + inc > self.EPOCH_LIMIT:
                bounds.append(v)
                ep += 1
                v = 0
            self.sched.append((ep, v))
            v += inc
        bounds.append(v)
        return bounds

    def emit(self, block, sems, coalesce=("tensor",)):
        bounds = self.plan()
        assert len(bounds) <= len(sems), f"need {len(bounds)} ticket sems"
        n = len(self.items)
        is_first = [True] * n
        run_inc = [0] * n
        i = 0
        while i < n:
            e0 = self.items[i][0]
            ep0 = self.sched[i][0]
            kind0 = self.items[i][2] == 16
            tag0 = self.items[i][3]
            j = i
            tot = 0
            while (j < n and self.items[j][0] == e0 and self.sched[j][0] == ep0
                   and (self.items[j][2] == 16) == kind0):
                ok = (e0 in coalesce) if not kind0 else (
                    tag0 is not None and self.items[j][3] == tag0)
                if j > i and not ok:
                    break
                tot += self.items[j][2]
                j += 1
                if not ok:
                    break
            for k2 in range(i, j):
                is_first[k2] = (k2 == i)
            run_inc[j - 1] = tot
            i = j
        engines = ["sync", "tensor", "vector", "scalar", "gpsimd"]
        for name in engines:
            def make(name):
                def body(eng):
                    for f in self.setups.get(name, []):
                        f(eng)
                    cur_ep = 0
                    for idx in range(n):
                        e, fn, inc, _tag = self.items[idx]
                        if e != name:
                            continue
                        ep, v = self.sched[idx]
                        while cur_ep < ep:
                            eng.wait_ge(sems[cur_ep], bounds[cur_ep])
                            cur_ep += 1
                        if is_first[idx]:
                            eng.wait_ge(sems[ep], v)
                        inst = fn(eng)
                        inst.then_inc(sems[ep], inc)
                    while cur_ep < len(bounds) - 1:
                        eng.wait_ge(sems[cur_ep], bounds[cur_ep])
                        cur_ep += 1
                    eng.wait_ge(sems[cur_ep], bounds[cur_ep])
                return body
            getattr(block, name)(make(name))


class Arena:

    def __init__(self, tile, cols):
        self.tile = tile
        self.cols = cols
        self.off = 0
        self.marks = []

    ALIGN = 32

    def alloc(self, n):
        self.off = (self.off + self.ALIGN - 1) // self.ALIGN * self.ALIGN
        assert self.off + n <= self.cols, f"arena overflow {self.off}+{n}>{self.cols}"
        v = self.tile[:, self.off:self.off + n]
        self.off += n
        return v

    def push(self):
        self.marks.append(self.off)

    def pop(self):
        self.off = self.marks.pop()


def ceil_div(a, b):
    return (a + b - 1) // b


def emit_stage3(sc, ps, AF32, ABF, xsb, d, ln_helper):
    IOA = IndirectOffsetOnAxis
    REG = sc.REG
    AF32.push()
    ABF.push()

    acol = AF32.alloc(256)
    cw = AF32.alloc(64)
    cb = AF32.alloc(16)
    bdt = AF32.alloc(16)
    dpar = AF32.alloc(16)
    qoff = AF32.alloc(1)
    iotaf2 = AF32.alloc(1)
    idxf2 = AF32.alloc(1)
    onesc = AF32.alloc(1)
    ones16 = AF32.alloc(1)
    states = AF32.alloc(256)
    wdt = ABF.alloc(2048)
    onesr = ABF.alloc(128)
    onescol = ABF.alloc(1)
    alog_s = AF32.alloc(256)
    idxi2 = xsb("s3idxi", [128, 1], I32)
    qoffi = xsb("s3qoffi", [128, 1], I32)

    sc.dma(lambda e: e.dma_start(out=wdt[0:1, :], in_=d["W_dt"]), q="gpsimd")
    sc.dma(lambda e: e.dma_start(
        out=cw.rearrange("p (c o) -> p c o", c=16),
        in_=d["conv_w"].rearrange("(c p) o -> p c o", p=128)))
    sc.dma(lambda e: e.dma_start(
        out=cb,
        in_=d["conv_b"].rearrange("(c p) -> p c", p=128),
        allow_slow_non_contiguous=True))
    sc.dma(lambda e: e.dma_start(
        out=bdt,
        in_=d["b_dt"].rearrange("(c p) -> p c", p=128),
        allow_slow_non_contiguous=True))
    sc.dma(lambda e: e.dma_start(
        out=alog_s.rearrange("p (c n) -> p c n", c=16),
        in_=d["A_log"].rearrange("(c p) n -> p c n", p=128)))
    sc.dma(lambda e: e.dma_start(
        out=dpar,
        in_=d["D_param"].rearrange("(c p) -> p c", p=128),
        allow_slow_non_contiguous=True))
    sc.dma(lambda e: e.dma_start(out=qoff[:, 0:1], in_=d["qoff_f"]))
    sc.op("scalar", lambda e: e.activation(out=acol, in_=alog_s, func=AF.Exp))
    sc.op("vector", lambda e: e.tensor_scalar(out=acol, in0=acol,
                                              scalar1=-1.0, scalar2=None, op0=ALU.mult))
    sc.op("vector", lambda e: e.memset(states, 0.0))
    sc.op("vector", lambda e: e.memset(onesc, 1.0))
    sc.op("vector", lambda e: e.memset(ones16, 1.0))
    sc.op("vector", lambda e: e.memset(onesr[0:1, :], 1.0))
    sc.op("vector", lambda e: e.memset(onescol, 1.0))
    sc.op("gpsimd", lambda e: e.iota(iotaf2, [[1, 1]], base=0, channel_multiplier=1,
                                     allow_small_or_imprecise_dtypes=True))

    AF32.push(); ABF.push()
    win = ABF.alloc(8 * 4096)
    lnfm = ABF.alloc(8 * 512)
    ln3blk = ABF.alloc(1024)
    xzt = ABF.alloc(4096)
    y1blk = AF32.alloc(1024)
    sc.dma(lambda e: e.dma_start(
        out=win.rearrange("p (c o) -> p c o", c=8),
        in_=d["W_in"].rearrange("(c p) o -> p c o", p=128)), q="gpsimd")
    for bo in range(OWN // 512):
        o0 = bo * 512
        for tt in range(4):
            sc.dma(lambda e, o0=o0, tt=tt: e.dma_start(
                out=y1blk, in_=d["y1_dram"][o0 + tt * 128:o0 + (tt + 1) * 128, :]))
            ln_helper(y1blk, ln3blk, D)
            sc.dma(lambda e, o0=o0, tt=tt: e.dma_start(
                out=d["ln_dram"][o0 + tt * 128:o0 + (tt + 1) * 128, :], in_=ln3blk))
        for dc in range(8):
            sc.dma(lambda e, o0=o0, dc=dc: e.dma_start(
                out=lnfm[:, dc * 512:(dc + 1) * 512],
                in_=d["ln_dram"][o0:o0 + 512, dc * 128:(dc + 1) * 128],
                transpose=True))
        for tt in range(4):
            for ob in range(8):
                p0 = ps[0]
                for di in range(8):
                    lhs = lnfm[:, di * 512 + tt * 128: di * 512 + tt * 128 + 128]
                    sc.op("tensor", lambda e, lhs=lhs, di=di, ob=ob, p0=p0, win=win: e.matmul(
                        p0[:, 0:512], lhs,
                        win[:, di * 4096 + ob * 512: di * 4096 + (ob + 1) * 512],
                        start=(di == 0), stop=(di == 7)))
                sc.op("vector", lambda e, ob=ob, p0=p0, xzt=xzt: e.tensor_copy(
                    xzt[:, ob * 512:(ob + 1) * 512], p0[:, 0:512]))
            sc.dma(lambda e, o0=o0, tt=tt, xzt=xzt: e.dma_start(
                out=d["xz_dram"][o0 + tt * 128:o0 + (tt + 1) * 128, :], in_=xzt))
    AF32.pop(); ABF.pop()

    AF32.push(); ABF.push()
    xcbuf = ABF.alloc(4 + OWN)
    xcs = ABF.alloc(OWN)
    xcv = AF32.alloc(0)
    for dc in range(16):
        sc.op("vector", lambda e, xcbuf=xcbuf: e.memset(xcbuf[:, 0:4], 0.0))
        for tb in range(16):
            sc.dma(lambda e, dc=dc, tb=tb, xcbuf=xcbuf: e.dma_start(
                out=xcbuf[:, 4 + tb * 512: 4 + (tb + 1) * 512],
                in_=d["xz_dram"][tb * 512:(tb + 1) * 512, dc * 128:(dc + 1) * 128],
                transpose=True))
        sc.op("vector", lambda e, dc=dc, xcbuf=xcbuf, xcs=xcs: e.tensor_scalar(
            out=xcs, in0=xcbuf[:, 4:4 + OWN],
            scalar1=cw[:, dc * 4 + 3:dc * 4 + 4],
            scalar2=cb[:, dc:dc + 1], op0=ALU.mult, op1=ALU.add))
        for k in range(3):
            sc.op("vector", lambda e, dc=dc, k=k, xcbuf=xcbuf, xcs=xcs: e.scalar_tensor_tensor(
                out=xcs, in0=xcbuf[:, 1 + k:1 + k + OWN],
                scalar=cw[:, dc * 4 + k:dc * 4 + k + 1],
                in1=xcs, op0=ALU.mult, op1=ALU.add))
        sc.op("scalar", lambda e, xcs=xcs: e.activation(out=xcs, in_=xcs, func=AF.Silu))
        sc.dma(lambda e, dc=dc, xcs=xcs: e.dma_start(
            out=d["xc_dram"][dc * 128:(dc + 1) * 128, :], in_=xcs))
    AF32.pop(); ABF.pop()

    AF32.push(); ABF.push()
    wx = ABF.alloc(16 * 33)
    xctile = ABF.alloc(16 * 512)
    bsb = ABF.alloc(512)
    csb = AF32.alloc(512)
    eu = ABF.alloc(512)
    dtt = ABF.alloc(512)
    csr = AF32.alloc(512)
    dtr = ABF.alloc(512)
    sc.dma(lambda e: e.dma_start(
        out=wx.rearrange("p (c o) -> p c o", c=16),
        in_=d["W_x"].rearrange("(c p) o -> p c o", p=128)), q="gpsimd")
    for tb in range(16):
        t0 = tb * 512
        for dc in range(16):
            sc.dma(lambda e, dc=dc, t0=t0, xctile=xctile: e.dma_start(
                out=xctile[:, dc * 512:(dc + 1) * 512],
                in_=d["xc_dram"][dc * 128:(dc + 1) * 128, t0:t0 + 512]),
                tag=("s3xc", t0))
        pB, pC, pD = ps[0], ps[1], ps[2]
        for dc in range(16):
            lhs = xctile[:, dc * 512:(dc + 1) * 512]
            sc.op("tensor", lambda e, lhs=lhs, dc=dc, pB=pB, wx=wx: e.matmul(
                pB[0:16, 0:512], wx[:, dc * 33: dc * 33 + 16], lhs,
                start=(dc == 0), stop=(dc == 15)))
            sc.op("tensor", lambda e, lhs=lhs, dc=dc, pC=pC, wx=wx: e.matmul(
                pC[0:16, 0:512], wx[:, dc * 33 + 16: dc * 33 + 32], lhs,
                start=(dc == 0), stop=(dc == 15)))
            sc.op("tensor", lambda e, lhs=lhs, dc=dc, pD=pD, wx=wx: e.matmul(
                pD[0:1, 0:512], wx[:, dc * 33 + 32: dc * 33 + 33], lhs,
                start=(dc == 0), stop=(dc == 15)))
        sc.op("vector", lambda e, pB=pB, bsb=bsb: e.tensor_copy(
            bsb[0:16, 0:512], pB[0:16, 0:512]))
        sc.dma(lambda e, t0=t0, bsb=bsb: e.dma_start(
            out=d["b_dram"][:, t0:t0 + 512], in_=bsb[0:16, 0:512]))
        sc.op("vector", lambda e, pC=pC, csb=csb: e.tensor_copy(
            csb[0:16, 0:512], pC[0:16, 0:512]))
        p3 = ps[3]
        sc.op("tensor", lambda e, p3=p3, csb=csb: e.matmul(
            p3[0:1, 0:512], ones16[0:16, 0:1], csb[0:16, 0:512],
            start=True, stop=True))
        sc.op("vector", lambda e, p3=p3, csr=csr: e.tensor_copy(
            csr[0:1, 0:512], p3[0:1, 0:512]))
        sc.dma(lambda e, t0=t0, csr=csr: e.dma_start(
            out=d["csum_dram"][0:1, t0:t0 + 512], in_=csr[0:1, 0:512]))
        sc.op("vector", lambda e, pD=pD, dtr=dtr: e.tensor_copy(
            dtr[0:1, 0:512], pD[0:1, 0:512]))
        for dc in range(16):
            p4 = ps[4]
            sc.op("tensor", lambda e, dc=dc, p4=p4, dtr=dtr: e.matmul(
                p4[:, 0:512], wdt[0:1, dc * 128:(dc + 1) * 128],
                dtr[0:1, 0:512], start=True, stop=True))
            sc.op("scalar", lambda e, dc=dc, p4=p4, eu=eu: e.activation(
                out=eu, in_=p4[:, 0:512], func=AF.Exp, bias=bdt[:, dc:dc + 1]))
            sc.op("scalar", lambda e, eu=eu, dtt=dtt: e.activation(
                out=dtt, in_=eu, func=AF.Ln, bias=onesc[:, 0:1]))
            sc.dma(lambda e, dc=dc, t0=t0, dtt=dtt: e.dma_start(
                out=d["dt_dram"][dc * 128:(dc + 1) * 128, t0:t0 + 512], in_=dtt))
    AF32.pop(); ABF.pop()

    AF32.push(); ABF.push()
    TQ = 1024
    dtall = ABF.alloc(16 * TQ)
    uball = ABF.alloc(16 * TQ)
    bb = ABF.alloc(TQ)
    bu = ABF.alloc(TQ)
    scano = ABF.alloc(TQ)
    brow = ABF.alloc(TQ)
    ab = AF32.alloc(TQ)
    ysr = AF32.alloc(512)
    for q in range(OWN // TQ):
        t0 = q * TQ
        for dc in range(16):
            sc.dma(lambda e, dc=dc, t0=t0: e.dma_start(
                out=dtall[:, dc * TQ:(dc + 1) * TQ],
                in_=d["dt_dram"][dc * 128:(dc + 1) * 128, t0:t0 + TQ]))
            sc.dma(lambda e, dc=dc, t0=t0: e.dma_start(
                out=uball[:, dc * TQ:(dc + 1) * TQ],
                in_=d["xc_dram"][dc * 128:(dc + 1) * 128, t0:t0 + TQ]))
            sc.op("vector", lambda e, dc=dc: e.tensor_tensor(
                out=uball[:, dc * TQ:(dc + 1) * TQ],
                in0=uball[:, dc * TQ:(dc + 1) * TQ],
                in1=dtall[:, dc * TQ:(dc + 1) * TQ], op=ALU.mult))
        for n in range(D_STATE):
            sc.dma(lambda e, n=n, t0=t0: e.dma_start(
                out=brow[0:1, 0:TQ], in_=d["b_dram"][n:n + 1, t0:t0 + TQ]))
            pb = ps[5]
            for j in range(TQ // 512):
                sc.op("tensor", lambda e, j=j, pb=pb: e.matmul(
                    pb[:, 0:512], onesr[0:1, 0:128],
                    brow[0:1, j * 512:(j + 1) * 512], start=True, stop=True))
                sc.op("vector", lambda e, j=j, pb=pb: e.tensor_copy(
                    bb[:, j * 512:(j + 1) * 512], pb[:, 0:512]))
            for dc in range(16):
                sci = dc * D_STATE + n
                last = (n == D_STATE - 1 and dc == 15)
                sc.op("scalar", lambda e, dc=dc, n=n: e.activation(
                    out=ab, in_=dtall[:, dc * TQ:(dc + 1) * TQ], func=AF.Exp,
                    scale=acol[:, dc * D_STATE + n: dc * D_STATE + n + 1]))
                sc.op("vector", lambda e, dc=dc: e.tensor_tensor(
                    out=bu, in0=uball[:, dc * TQ:(dc + 1) * TQ], in1=bb, op=ALU.mult))
                sc.op("vector", lambda e, sci=sci: e.tensor_tensor_scan(
                    out=scano, data0=ab, data1=bu,
                    initial=states[:, sci:sci + 1], op0=ALU.mult, op1=ALU.add))
                sc.op("vector", lambda e, sci=sci: e.tensor_copy(
                    states[:, sci:sci + 1], scano[:, TQ - 1:TQ]))
                first = (n == 0 and dc == 0)
                for j in range(TQ // 512):
                    pyj = ps[6] if j == 0 else ps[7]
                    sc.op("tensor", lambda e, j=j, first=first, last=last, pyj=pyj: e.matmul(
                        pyj[0:1, 0:512], onescol[:, 0:1],
                        scano[:, j * 512:(j + 1) * 512],
                        start=first, stop=last))
        for j in range(TQ // 512):
            pyj = ps[6] if j == 0 else ps[7]
            sc.op("vector", lambda e, j=j, pyj=pyj, ysr=ysr: e.tensor_copy(
                ysr[0:1, 0:512], pyj[0:1, 0:512]))
            sc.dma(lambda e, j=j, t0=t0, ysr=ysr: e.dma_start(
                out=d["ysum_dram"][0:1, t0 + j * 512:t0 + (j + 1) * 512],
                in_=ysr[0:1, 0:512]))
    AF32.pop(); ABF.pop()

    AF32.push(); ABF.push()
    yrow = AF32.alloc(64)
    crow = AF32.alloc(64)
    ybb = ABF.alloc(64)
    sc.dma(lambda e: e.dma_start(
        out=yrow, in_=d["ysum_dram"].rearrange("a (p c) -> (a p) c", p=128)))
    sc.dma(lambda e: e.dma_start(
        out=crow, in_=d["csum_dram"].rearrange("a (p c) -> (a p) c", p=128)))
    sc.op("vector", lambda e: e.tensor_tensor(out=yrow, in0=yrow, in1=crow, op=ALU.mult))
    sc.op("vector", lambda e: e.tensor_scalar(out=ybb, in0=yrow,
                                              scalar1=1.0 / D_INNER, scalar2=None,
                                              op0=ALU.mult))
    sc.dma(lambda e: e.dma_start(
        out=d["ybc_dram"].rearrange("a (p c) -> (a p) c", p=128), in_=ybb))
    AF32.pop(); ABF.pop()

    AF32.push(); ABF.push()
    wout = ABF.alloc(16 * 1024)
    xtq = ABF.alloc(16 * QT)
    xzgath = ABF.alloc(4096)
    zfm = ABF.alloc(128)
    zs = ABF.alloc(128)
    yq2 = ABF.alloc(128)
    ybcq = ABF.alloc(QT)
    ybcq16 = ABF.alloc(16)
    y1q = AF32.alloc(1024)
    sc.dma(lambda e: e.dma_start(
        out=wout.rearrange("p (c o) -> p c o", c=16),
        in_=d["W_out"].rearrange("(c p) o -> p c o", p=128)), q="gpsimd")
    sc.op("vector", lambda e: e.tensor_scalar(
        out=idxf2, in0=iotaf2, scalar1=16.0, scalar2=None, op0=ALU.mult))
    sc.op("vector", lambda e: e.tensor_tensor(
        out=idxf2, in0=idxf2, in1=qoff[:, 0:1], op=ALU.add))
    sc.op("vector", lambda e: e.tensor_copy(qoffi[:, 0:1], idxf2))
    sc.dma(lambda e: e.indirect_dma_start(
        out=ybcq16, out_offset=None,
        in_=d["ybc_dram"].rearrange("a (b c) -> (a b) c", c=1),
        in_offset=IOA(ap=qoffi[:, 0:1], axis=0),
        bounds_check=REG["b_own"], oob_is_err=False), q="gpsimd")
    sc.dma(lambda e: e.dma_start(
        out=d["ybcq_dram"].rearrange("a (p c) -> (a p) c", p=128), in_=ybcq16))
    sc.dma(lambda e: e.dma_start(out=ybcq[0:1, 0:QT], in_=d["ybcq_dram"]))
    for i in range(QT // 128):
        sc.op("vector", lambda e, i=i: e.tensor_scalar(
            out=idxf2, in0=iotaf2, scalar1=float(i * 128), scalar2=None, op0=ALU.add))
        sc.op("vector", lambda e: e.tensor_tensor(
            out=idxf2, in0=idxf2, in1=qoff[:, 0:1], op=ALU.add))
        sc.op("vector", lambda e: e.tensor_copy(idxi2[:, 0:1], idxf2))
        sc.dma(lambda e: e.indirect_dma_start(
            out=xzgath, out_offset=None, in_=d["xz_dram"],
            in_offset=IOA(ap=idxi2[:, 0:1], axis=0),
            bounds_check=REG["b_own"], oob_is_err=False), q="gpsimd")
        sc.dma(lambda e, i=i: e.dma_start(
            out=d["zq_dram"][i * 128:(i + 1) * 128, :], in_=xzgath[:, 2048:4096]))
        sc.dma(lambda e: e.indirect_dma_start(
            out=y1q, out_offset=None, in_=d["y1_dram"],
            in_offset=IOA(ap=idxi2[:, 0:1], axis=0),
            bounds_check=REG["b_own"], oob_is_err=False), q="gpsimd")
        sc.dma(lambda e, i=i: e.dma_start(
            out=d["y1q_dram"][i * 128:(i + 1) * 128, :], in_=y1q))
    for dc in range(16):
        sc.op("vector", lambda e, dc=dc: e.tensor_scalar(
            out=idxf2, in0=iotaf2, scalar1=float(dc * 128), scalar2=None, op0=ALU.add))
        sc.op("vector", lambda e: e.tensor_scalar(
            out=idxf2, in0=idxf2, scalar1=float(OWN), scalar2=None, op0=ALU.mult))
        sc.op("vector", lambda e: e.tensor_tensor(
            out=idxf2, in0=idxf2, in1=qoff[:, 0:1], op=ALU.add))
        sc.op("vector", lambda e: e.tensor_copy(idxi2[:, 0:1], idxf2))
        sc.dma(lambda e, dc=dc: e.indirect_dma_start(
            out=xtq[:, dc * QT:(dc + 1) * QT], out_offset=None,
            in_=d["xc_dram"].rearrange("a (b c) -> (a b) c", c=1),
            in_offset=IOA(ap=idxi2[:, 0:1], axis=0),
            bounds_check=REG["b_flat"], oob_is_err=False), q="gpsimd")
    for tq in range(QT // 128):
        pacc0, pacc1 = ps[0], ps[1]
        for dc in range(16):
            pbc = ps[2]
            sc.op("tensor", lambda e, tq=tq, pbc=pbc: e.matmul(
                pbc[:, 0:128], onesr[0:1, 0:128],
                ybcq[0:1, tq * 128:(tq + 1) * 128], start=True, stop=True))
            sc.dma(lambda e, tq=tq, dc=dc: e.dma_start(
                out=zfm, in_=d["zq_dram"][tq * 128:(tq + 1) * 128,
                                          dc * 128:(dc + 1) * 128],
                transpose=True))
            sc.op("scalar", lambda e: e.activation(out=zs, in_=zfm, func=AF.Silu))
            sc.op("vector", lambda e, dc=dc, tq=tq, pbc=pbc: e.scalar_tensor_tensor(
                out=yq2, in0=xtq[:, dc * QT + tq * 128: dc * QT + (tq + 1) * 128],
                scalar=dpar[:, dc:dc + 1], in1=pbc[:, 0:128],
                op0=ALU.mult, op1=ALU.add))
            sc.op("vector", lambda e: e.tensor_tensor(
                out=yq2, in0=yq2, in1=zs, op=ALU.mult))
            for ob in range(2):
                pacc = pacc0 if ob == 0 else pacc1
                sc.op("tensor", lambda e, dc=dc, ob=ob, pacc=pacc: e.matmul(
                    pacc[:, 0:512], yq2,
                    wout[:, dc * 1024 + ob * 512: dc * 1024 + (ob + 1) * 512],
                    start=(dc == 0), stop=(dc == 15)))
        sc.dma(lambda e, tq=tq: e.dma_start(
            out=y1q, in_=d["y1q_dram"][tq * 128:(tq + 1) * 128, :]))
        for ob in range(2):
            pacc = pacc0 if ob == 0 else pacc1
            ysl = y1q[:, ob * 512:(ob + 1) * 512]
            sc.op("vector", lambda e, ysl=ysl, pacc=pacc: e.tensor_tensor(
                out=ysl, in0=ysl, in1=pacc[:, 0:512], op=ALU.add))
        sc.dma(lambda e, tq=tq: e.dma_start(
            out=d["y2_dram"][tq * 128:(tq + 1) * 128, :], in_=y1q))
    AF32.pop(); ABF.pop()
    AF32.pop()
    ABF.pop()


def emit_stage4(sc, ps, AF32, ABF, d, ln_helper):
    AF32.push()
    ABF.push()
    wg2 = ABF.alloc(8 * 8)
    bg2b = AF32.alloc(8)
    b2sb = ABF.alloc(1024)
    lgt = AF32.alloc(8)
    m8 = AF32.alloc(8)
    e8 = AF32.alloc(8)
    oh0 = AF32.alloc(8)
    oh1 = AF32.alloc(8)
    nm1 = AF32.alloc(1)
    den = AF32.alloc(1)
    w1c = AF32.alloc(1)
    w2c = AF32.alloc(1)
    me_all = AF32.alloc(16 * 8)
    me_em = ABF.alloc(2048)
    ln4blk = ABF.alloc(1024)
    y2blk = AF32.alloc(1024)
    lnfm = ABF.alloc(8 * 512)
    w1sb = ABF.alloc(8 * 4096)
    w2ring = ABF.alloc(1024)
    ghall = ABF.alloc(32 * 512)
    b1col = AF32.alloc(32)
    accsl = AF32.alloc(512)
    fin = AF32.alloc(1024)

    sc.dma(lambda e: e.dma_start(
        out=wg2.rearrange("p (c o) -> p c o", c=8),
        in_=d["Wg2"].rearrange("(c p) o -> p c o", p=128)), q="gpsimd")
    sc.dma(lambda e: e.dma_start(out=bg2b, in_=d["bg2_b"]))
    sc.dma(lambda e: e.dma_start(out=b2sb[0:8, :], in_=d["b2"]), q="gpsimd")

    for ex in range(E1):
        sc.dma(lambda e, ex=ex: e.dma_start(
            out=w1sb.rearrange("p (a o) -> p a o", a=32),
            in_=d["W2"][ex].rearrange("(a p) o -> p a o", p=128)), q="gpsimd")
        sc.dma(lambda e, ex=ex: e.dma_start(
            out=d["w2bf_dram"][ex].rearrange("(a p) o -> p a o", p=128),
            in_=w1sb.rearrange("p (a o) -> p a o", a=32)))

    for i in range(QT // 128):
        sc.dma(lambda e, i=i: e.dma_start(
            out=y2blk, in_=d["y2_dram"][i * 128:(i + 1) * 128, :]))
        ln_helper(y2blk, ln4blk, D)
        sc.dma(lambda e, i=i: e.dma_start(
            out=d["ln_dram"][i * 128:(i + 1) * 128, :], in_=ln4blk))
    for bo in range(QT // 512):
        for dc in range(8):
            sc.dma(lambda e, bo=bo, dc=dc: e.dma_start(
                out=lnfm[:, dc * 512:(dc + 1) * 512],
                in_=d["ln_dram"][bo * 512:(bo + 1) * 512, dc * 128:(dc + 1) * 128],
                transpose=True))
        for tt in range(4):
            i = bo * 4 + tt
            p0 = ps[0]
            for di in range(8):
                lhs = lnfm[:, di * 512 + tt * 128: di * 512 + tt * 128 + 128]
                sc.op("tensor", lambda e, lhs=lhs, di=di, p0=p0: e.matmul(
                    p0[:, 0:8], lhs, wg2[:, di * 8:(di + 1) * 8],
                    start=(di == 0), stop=(di == 7)))
            sc.op("vector", lambda e, p0=p0: e.tensor_copy(lgt, p0[:, 0:8]))
            sc.op("vector", lambda e: e.tensor_tensor(out=lgt, in0=lgt, in1=bg2b,
                                                      op=ALU.add))
            sc.op("vector", lambda e: e.max(out=m8, in_=lgt))
            sc.op("vector", lambda e: e.tensor_scalar(
                out=nm1, in0=m8[:, 0:1], scalar1=-1.0, scalar2=None, op0=ALU.mult))
            sc.op("scalar", lambda e: e.activation(out=e8, in_=m8, func=AF.Exp,
                                                   bias=nm1[:, 0:1]))
            sc.op("vector", lambda e: e.reduce_sum(out=den, in_=e8, axis=AX.X))
            sc.op("vector", lambda e: e.reciprocal(out=den, in_=den))
            sc.op("vector", lambda e: e.tensor_scalar(
                out=w1c, in0=e8[:, 0:1], scalar1=den[:, 0:1], scalar2=None,
                op0=ALU.mult))
            sc.op("vector", lambda e: e.tensor_scalar(
                out=w2c, in0=e8[:, 1:2], scalar1=den[:, 0:1], scalar2=None,
                op0=ALU.mult))
            sc.op("vector", lambda e: e.tensor_scalar(
                out=oh0, in0=lgt, scalar1=m8[:, 0:1], scalar2=None, op0=ALU.is_equal))
            sc.op("vector", lambda e: e.tensor_scalar(
                out=oh1, in0=lgt, scalar1=m8[:, 1:2], scalar2=None, op0=ALU.is_equal))
            mei = me_all[:, i * 8:(i + 1) * 8]
            sc.op("vector", lambda e, mei=mei: e.tensor_scalar(
                out=mei, in0=oh0, scalar1=w1c[:, 0:1], scalar2=None, op0=ALU.mult))
            sc.op("vector", lambda e, mei=mei: e.scalar_tensor_tensor(
                out=mei, in0=oh1, scalar=w2c[:, 0:1], in1=mei,
                op0=ALU.mult, op1=ALU.add))
            p1 = ps[1]
            sc.op("tensor", lambda e, mei=mei, p1=p1: e.transpose(
                p1[0:8, 0:128], mei, d["idf_sb"]))
            sc.op("vector", lambda e, i=i, p1=p1: e.tensor_copy(
                me_em[0:8, i * 128:(i + 1) * 128], p1[0:8, 0:128]))

    for ex in range(E1):
        sc.dma(lambda e, ex=ex: e.dma_start(
            out=w1sb.rearrange("p (c o) -> p c o", c=8),
            in_=d["W1"][ex].rearrange("(c p) o -> p c o", p=128)), q="gpsimd")
        for hc in range(32):
            sc.dma(lambda e, ex=ex, hc=hc: e.dma_start(
                out=b1col[:, hc:hc + 1],
                in_=d["b1"][ex:ex + 1, hc * 128:(hc + 1) * 128]))
        for bo in range(QT // 512):
            for dc in range(8):
                sc.dma(lambda e, bo=bo, dc=dc: e.dma_start(
                    out=lnfm[:, dc * 512:(dc + 1) * 512],
                    in_=d["ln_dram"][bo * 512:(bo + 1) * 512, dc * 128:(dc + 1) * 128],
                    transpose=True))
            for hc in range(32):
                p0 = ps[0]
                for di in range(8):
                    lhs = w1sb[:, di * 4096 + hc * 128: di * 4096 + (hc + 1) * 128]
                    sc.op("tensor", lambda e, lhs=lhs, di=di, bo=bo, p0=p0: e.matmul(
                        p0[:, 0:512], lhs, lnfm[:, di * 512:(di + 1) * 512],
                        start=(di == 0), stop=(di == 7)))
                sc.op("scalar", lambda e, hc=hc, p0=p0: e.activation(
                    out=ghall[:, hc * 512:(hc + 1) * 512], in_=p0[:, 0:512],
                    func=AF.Gelu, bias=b1col[:, hc:hc + 1]))
            for hc in range(32):
                sc.dma(lambda e, ex=ex, hc=hc: e.dma_start(
                    out=w2ring, in_=d["w2bf_dram"][ex, hc * 128:(hc + 1) * 128, :]))
                for tt in range(4):
                    for obk in range(2):
                        pacc = ps[tt * 2 + obk]
                        lhs = ghall[:, hc * 512 + tt * 128: hc * 512 + tt * 128 + 128]
                        sc.op("tensor", lambda e, lhs=lhs, hc=hc, obk=obk, pacc=pacc: e.matmul(
                            pacc[:, 0:512], lhs, w2ring[:, obk * 512:(obk + 1) * 512],
                            start=(hc == 0), stop=(hc == 31)))
            for tt in range(4):
                i = bo * 4 + tt
                for obk in range(2):
                    pacc = ps[tt * 2 + obk]
                    if ex > 0:
                        sc.dma(lambda e, i=i, obk=obk: e.dma_start(
                            out=accsl,
                            in_=d["acc_dram"][i * 128:(i + 1) * 128,
                                              obk * 512:(obk + 1) * 512]))
                        sc.op("vector", lambda e, i=i, ex=ex, pacc=pacc: e.scalar_tensor_tensor(
                            out=accsl, in0=pacc[:, 0:512],
                            scalar=me_all[:, i * 8 + ex:i * 8 + ex + 1],
                            in1=accsl, op0=ALU.mult, op1=ALU.add))
                    else:
                        sc.op("vector", lambda e, i=i, ex=ex, pacc=pacc: e.tensor_scalar(
                            out=accsl, in0=pacc[:, 0:512],
                            scalar1=me_all[:, i * 8 + ex:i * 8 + ex + 1],
                            scalar2=None, op0=ALU.mult))
                    sc.dma(lambda e, i=i, obk=obk: e.dma_start(
                        out=d["acc_dram"][i * 128:(i + 1) * 128,
                                          obk * 512:(obk + 1) * 512], in_=accsl))

    for i in range(QT // 128):
        sc.dma(lambda e, i=i: e.dma_start(
            out=fin, in_=d["acc_dram"][i * 128:(i + 1) * 128, :]))
        sc.dma(lambda e, i=i: e.dma_start(
            out=y2blk, in_=d["y2_dram"][i * 128:(i + 1) * 128, :]))
        for obk in range(2):
            p2 = ps[0]
            sc.op("tensor", lambda e, i=i, obk=obk, p2=p2: e.matmul(
                p2[:, 0:512], me_em[0:8, i * 128:(i + 1) * 128],
                b2sb[0:8, obk * 512:(obk + 1) * 512], start=True, stop=True))
            fs = fin[:, obk * 512:(obk + 1) * 512]
            sc.op("vector", lambda e, fs=fs, p2=p2: e.tensor_tensor(
                out=fs, in0=fs, in1=p2[:, 0:512], op=ALU.add))
            ys = y2blk[:, obk * 512:(obk + 1) * 512]
            sc.op("vector", lambda e, fs=fs, ys=ys: e.tensor_tensor(
                out=fs, in0=fs, in1=ys, op=ALU.add))
        sc.dma(lambda e, i=i: e.dma_start(
            out=d["out_final"][i * 128:(i + 1) * 128, :], in_=fin))
    AF32.pop()
    ABF.pop()


def emit_stage2(sc, ps, s2, d, ln_helper, stat, eps_ln_c):
    NBLK = NT // 512
    IOA = IndirectOffsetOnAxis

    sc.op("vector", lambda e: e.memset(s2["ones64"], 1.0))
    sc.op("vector", lambda e: e.memset(s2["big64"], 1e9))
    sc.op("vector", lambda e: e.memset(s2["cnt"], 0.0))
    sc.op("vector", lambda e: e.memset(s2["ones1"][0:1, :], 1.0))
    sc.op("vector", lambda e: e.memset(s2["zero8k"], 0.0))
    sc.op("gpsimd", lambda e: e.iota(s2["e512"], [[1, 1]], base=0, channel_multiplier=512,
                                     allow_small_or_imprecise_dtypes=True))
    sc.op("gpsimd", lambda e: e.iota(s2["iotaf"], [[1, 1]], base=0, channel_multiplier=1,
                                     allow_small_or_imprecise_dtypes=True))
    ownb = s2["ownb"][:, 0:1]
    sc.dma(lambda e: e.dma_start(out=ownb, in_=d["own_base_f"]))
    sc.dma(lambda e: e.dma_start(
        out=s2["rw_sb"].rearrange("p (c o) -> p c o", c=8),
        in_=d["rwT"].rearrange("(c p) o -> p c o", p=128)), q="gpsimd")
    for a in range(128):
        sc.dma(lambda e, a=a: e.dma_start(
            out=d["xbuf_dram"][a * 256:(a + 1) * 256, :].rearrange(
                "(p q) d -> p q d", p=128),
            in_=s2["zero8k"].rearrange("p (q d) -> p q d", q=2)), tag=("xz0",))
    sc.dma(lambda e: e.dma_start(
        out=s2["wsgu"].rearrange("p (c o) -> p c o", c=8),
        in_=d["wsgu"].rearrange("(c p) o -> p c o", p=128)), q="gpsimd")
    sc.dma(lambda e: e.dma_start(
        out=s2["wsd"].rearrange("p (c o) -> p c o", c=6),
        in_=d["wsd"].rearrange("(c p) o -> p c o", p=128)), q="gpsimd")

    for blk in range(NBLK):
        t0 = blk * 512
        for tt in range(4):
            sc.dma(lambda e, t0=t0, tt=tt: e.dma_start(
                out=s2["y0blk"],
                in_=d["y0_dram"][t0 + tt * 128:t0 + (tt + 1) * 128, :]))
            ln_helper(s2["y0blk"], s2["ln2blk"], D)
            sc.dma(lambda e, t0=t0, tt=tt: e.dma_start(
                out=d["ln_dram"][t0 + tt * 128:t0 + (tt + 1) * 128, :],
                in_=s2["ln2blk"]))
        for dc in range(8):
            sc.dma(lambda e, t0=t0, dc=dc: e.dma_start(
                out=s2["lnfm"][:, dc * 512:(dc + 1) * 512],
                in_=d["ln_dram"][t0:t0 + 512, dc * 128:(dc + 1) * 128],
                transpose=True))
        for tt in range(4):
            p0 = ps[0]
            for di in range(8):
                lhs = s2["lnfm"][:, di * 512 + tt * 128: di * 512 + tt * 128 + 128]
                sc.op("tensor", lambda e, lhs=lhs, di=di, p0=p0: e.matmul(
                    p0[:, 0:64], lhs, s2["rw_sb"][:, di * 64:(di + 1) * 64],
                    start=(di == 0), stop=(di == 7)))
            lg = s2["logits"][:, tt * 64:(tt + 1) * 64]
            sc.op("vector", lambda e, lg=lg, p0=p0: e.tensor_copy(lg, p0[:, 0:64]))
            sc.dma(lambda e, lg=lg, t0=t0, tt=tt: e.dma_start(
                out=d["logits_dram"][t0 + tt * 128:t0 + (tt + 1) * 128, :], in_=lg))
            sc.op("vector", lambda e, lg=lg: e.max(out=s2["max8"], in_=lg))
            sc.dma(lambda e, t0=t0, tt=tt: e.dma_start(
                out=d["max8_dram"][t0 + tt * 128:t0 + (tt + 1) * 128, :], in_=s2["max8"]))
            sc.dma(lambda e, t0=t0, tt=tt: e.dma_start(
                out=d["t8_dram"][t0 + tt * 128:t0 + (tt + 1) * 128, :],
                in_=s2["max8"][:, 7:8]))
            sc.op("vector", lambda e: e.tensor_scalar(
                out=s2["nm1"], in0=s2["max8"][:, 0:1], scalar1=-1.0, scalar2=None,
                op0=ALU.mult))
            sc.op("scalar", lambda e, lg=lg: e.activation(
                out=s2["expw"], in_=lg, func=AF.Exp, bias=s2["nm1"][:, 0:1]))
            sc.op("vector", lambda e, lg=lg: e.tensor_scalar(
                out=s2["oh"], in0=lg, scalar1=s2["max8"][:, 7:8], scalar2=None,
                op0=ALU.is_ge))
            sc.op("vector", lambda e: e.tensor_tensor(
                out=s2["expw"], in0=s2["expw"], in1=s2["oh"], op=ALU.mult))
            sc.op("vector", lambda e: e.reduce_sum(out=s2["den"], in_=s2["expw"],
                                                   axis=AX.X))
            sc.op("vector", lambda e: e.reciprocal(out=s2["den"], in_=s2["den"]))
            sc.op("vector", lambda e: e.tensor_scalar(
                out=s2["wv"], in0=s2["expw"], scalar1=s2["den"][:, 0:1], scalar2=None,
                op0=ALU.mult))
            sc.dma(lambda e, t0=t0, tt=tt: e.dma_start(
                out=d["wtm_dram"][t0 + tt * 128:t0 + (tt + 1) * 128, :], in_=s2["wv"]))
            p1 = ps[1]
            sc.op("tensor", lambda e, lg=lg, p1=p1: e.transpose(
                p1[0:64, 0:128], lg, d["idf_sb"]))
            sc.op("vector", lambda e, tt=tt, p1=p1: e.tensor_copy(
                s2["em"][0:64, tt * 128:(tt + 1) * 128], p1[0:64, 0:128]))
        sc.dma(lambda e, t0=t0: e.dma_start(
            out=s2["t8row"][0:1, 0:512], in_=d["t8_dram"][t0:t0 + 512, :]))
        p2 = ps[2]
        sc.op("tensor", lambda e, p2=p2: e.matmul(
            p2[0:64, 0:512], s2["ones1"][0:1, 0:64], s2["t8row"][0:1, 0:512],
            start=True, stop=True))
        sc.op("vector", lambda e, p2=p2: e.tensor_tensor(
            out=s2["selem"][0:64, :], in0=s2["em"][0:64, :], in1=p2[0:64, 0:512],
            op=ALU.is_ge))
        sc.op("vector", lambda e: e.tensor_tensor_scan(
            out=s2["scan"][0:64, :], data0=s2["ones64"][0:64, :],
            data1=s2["selem"][0:64, :], initial=s2["cnt"][0:64, 0:1],
            op0=ALU.mult, op1=ALU.add))
        sc.op("vector", lambda e: e.tensor_copy(s2["cnt"][0:64, 0:1],
                                                s2["scan"][0:64, 511:512]))
        sc.op("vector", lambda e: e.tensor_tensor(
            out=s2["pos"][0:64, :], in0=s2["scan"][0:64, :], in1=s2["selem"][0:64, :],
            op=ALU.subtract))
        sc.op("vector", lambda e: e.scalar_tensor_tensor(
            out=s2["kept"][0:64, :], in0=s2["pos"][0:64, :], scalar=float(C0),
            in1=s2["selem"][0:64, :], op0=ALU.is_lt, op1=ALU.mult))
        sc.op("vector", lambda e: e.tensor_copy(s2["kepti"][0:64, :], s2["kept"][0:64, :]))
        sc.op("vector", lambda e: e.tensor_scalar(
            out=s2["scan"][0:64, :], in0=s2["pos"][0:64, :],
            scalar1=s2["e512"][0:64, 0:1], scalar2=None, op0=ALU.add))
        sc.op("vector", lambda e: e.select(
            out=s2["slotm"][0:64, :], mask=s2["kepti"][0:64, :],
            on_true=s2["scan"][0:64, :], on_false=s2["big64"][0:64, :]))
        for tt in range(4):
            p3 = ps[3]
            sc.op("tensor", lambda e, tt=tt, p3=p3: e.transpose(
                p3[:, 0:64], s2["slotm"][0:64, tt * 128:(tt + 1) * 128], d["idf_sb"][0:64, 0:64]))
            sc.op("vector", lambda e, p3=p3: e.tensor_copy(s2["lgown"], p3[:, 0:64]))
            sc.dma(lambda e, t0=t0, tt=tt: e.dma_start(
                out=d["slotm_dram"][t0 + tt * 128:t0 + (tt + 1) * 128, :],
                in_=s2["lgown"]))

    for i in range(OWN // 128):
        sc.op("vector", lambda e, i=i: e.tensor_scalar(
            out=s2["idxf"], in0=s2["iotaf"], scalar1=float(i * 128), scalar2=None,
            op0=ALU.add))
        sc.op("vector", lambda e: e.tensor_tensor(
            out=s2["idxf"], in0=s2["idxf"], in1=ownb, op=ALU.add))
        sc.op("vector", lambda e: e.tensor_copy(s2["idxi"][:, 0:1], s2["idxf"]))
        for name, dst, src in (
            ("ln", s2["lnown"], d["ln_dram"]),
            ("lg", s2["lgown"], d["logits_dram"]),
            ("m8", s2["m8own"], d["max8_dram"]),
            ("w", s2["wown"], d["wtm_dram"]),
            ("sm", s2["smown"], d["slotm_dram"]),
        ):
            sc.dma(lambda e, dst=dst, src=src: e.indirect_dma_start(
                out=dst, out_offset=None, in_=src,
                in_offset=IOA(ap=s2["idxi"][:, 0:1], axis=0),
                bounds_check=sc.REG["b_nt"], oob_is_err=False), q="gpsimd",
                tag=("s2g", i))
        sc.dma(lambda e, i=i: e.dma_start(
            out=d["ln2own_dram"][i * 128:(i + 1) * 128, :], in_=s2["lnown"]))
        for k in range(8):
            sc.op("vector", lambda e, k=k: e.tensor_scalar(
                out=s2["oh"], in0=s2["lgown"], scalar1=s2["m8own"][:, k:k + 1],
                scalar2=None, op0=ALU.is_equal))
            sc.op("vector", lambda e: e.tensor_tensor(
                out=s2["ohs"], in0=s2["oh"], in1=s2["smown"], op=ALU.mult))
            sc.op("vector", lambda e, i=i, k=k: e.reduce_sum(
                out=s2["slot8"][:, i * 8 + k:i * 8 + k + 1], in_=s2["ohs"], axis=AX.X))
            sc.op("vector", lambda e: e.tensor_tensor(
                out=s2["ohs"], in0=s2["oh"], in1=s2["wown"], op=ALU.mult))
            sc.op("vector", lambda e, i=i, k=k: e.reduce_sum(
                out=s2["w8"][:, i * 8 + k:i * 8 + k + 1], in_=s2["ohs"], axis=AX.X))
            sc.op("vector", lambda e, i=i, k=k: e.tensor_copy(
                s2["idxi"][:, 1:2], s2["slot8"][:, i * 8 + k:i * 8 + k + 1]))
            sc.dma(lambda e: e.indirect_dma_start(
                out=d["xbuf_dram"], out_offset=IOA(ap=s2["idxi"][:, 1:2], axis=0),
                in_=s2["lnown"], in_offset=None,
                bounds_check=sc.REG["b_slots"], oob_is_err=False), q="gpsimd")

    for ex in range(E0):
        sc.dma(lambda e, ex=ex: e.dma_start(
            out=s2["wg"].rearrange("p (c f) -> p c f", c=8),
            in_=d["wgate"][ex].rearrange("(c p) f -> p c f", p=128)), q="gpsimd",
            tag=("s2w", ex))
        sc.dma(lambda e, ex=ex: e.dma_start(
            out=s2["wu"].rearrange("p (c f) -> p c f", c=8),
            in_=d["wup"][ex].rearrange("(c p) f -> p c f", p=128)), q="gpsimd",
            tag=("s2w", ex))
        sc.dma(lambda e, ex=ex: e.dma_start(
            out=s2["wd"].rearrange("p (c o) -> p c o", c=6),
            in_=d["wdown"][ex].rearrange("(c p) o -> p c o", p=128)), q="gpsimd",
            tag=("s2w", ex))
        for dc in range(8):
            sc.dma(lambda e, ex=ex, dc=dc: e.dma_start(
                out=s2["xfm"][:, dc * 512:(dc + 1) * 512],
                in_=d["xbuf_dram"][ex * C0:(ex + 1) * C0, dc * 128:(dc + 1) * 128],
                transpose=True))
        for st in range(4):
            for fb0, fbn in ((0, 512), (512, 256)):
                pg, pu = ps[0], ps[1]
                for di in range(8):
                    lhs = s2["xfm"][:, di * 512 + st * 128: di * 512 + st * 128 + 128]
                    sc.op("tensor", lambda e, lhs=lhs, di=di, fb0=fb0, fbn=fbn, pg=pg: e.matmul(
                        pg[:, 0:fbn], lhs, s2["wg"][:, di * F0 + fb0: di * F0 + fb0 + fbn],
                        start=(di == 0), stop=(di == 7)))
                    sc.op("tensor", lambda e, lhs=lhs, di=di, fb0=fb0, fbn=fbn, pu=pu: e.matmul(
                        pu[:, 0:fbn], lhs, s2["wu"][:, di * F0 + fb0: di * F0 + fb0 + fbn],
                        start=(di == 0), stop=(di == 7)))
                sc.op("scalar", lambda e, fbn=fbn, pg=pg: e.activation(
                    out=s2["siluscr"][:, 0:fbn], in_=pg[:, 0:fbn], func=AF.Silu))
                sc.op("vector", lambda e, st=st, fb0=fb0, fbn=fbn, pu=pu: e.tensor_tensor(
                    out=s2["hexp"][:, st * F0 + fb0: st * F0 + fb0 + fbn],
                    in0=s2["siluscr"][:, 0:fbn], in1=pu[:, 0:fbn], op=ALU.mult))
        for st in range(4):
            sc.dma(lambda e, st=st: e.dma_start(
                out=d["hexp_dram"][st * 128:(st + 1) * 128, :],
                in_=s2["hexp"][:, st * F0:(st + 1) * F0]), tag=("s2he", ex))
        for fc in range(6):
            sc.dma(lambda e, fc=fc: e.dma_start(
                out=s2["hfm"][:, fc * 512:(fc + 1) * 512],
                in_=d["hexp_dram"][:, fc * 128:(fc + 1) * 128],
                transpose=True))
        for st in range(4):
            for obk in range(2):
                p2 = ps[2]
                for fc in range(6):
                    lhs = s2["hfm"][:, fc * 512 + st * 128: fc * 512 + st * 128 + 128]
                    sc.op("tensor", lambda e, lhs=lhs, fc=fc, obk=obk, p2=p2: e.matmul(
                        p2[:, 0:512], lhs, s2["wd"][:, fc * 1024 + obk * 512: fc * 1024 + (obk + 1) * 512],
                        start=(fc == 0), stop=(fc == 5)))
                sc.op("vector", lambda e, obk=obk, p2=p2: e.tensor_copy(
                    s2["ytile"][:, obk * 512:(obk + 1) * 512], p2[:, 0:512]))
            sc.dma(lambda e, ex=ex, st=st: e.dma_start(
                out=d["ybuf_dram"][ex * C0 + st * 128: ex * C0 + (st + 1) * 128, :],
                in_=s2["ytile"]))

    for bo in range(OWN // 512):
        o0 = bo * 512
        for dc in range(8):
            sc.dma(lambda e, o0=o0, dc=dc: e.dma_start(
                out=s2["lnfm"][:, dc * 512:(dc + 1) * 512],
                in_=d["ln2own_dram"][o0:o0 + 512, dc * 128:(dc + 1) * 128],
                transpose=True))
        for tt in range(4):
            for gb in range(3):
                p0 = ps[0]
                for di in range(8):
                    lhs = s2["lnfm"][:, di * 512 + tt * 128: di * 512 + tt * 128 + 128]
                    sc.op("tensor", lambda e, lhs=lhs, di=di, gb=gb, p0=p0: e.matmul(
                        p0[:, 0:512], lhs, s2["wsgu"][:, di * 1536 + gb * 512: di * 1536 + (gb + 1) * 512],
                        start=(di == 0), stop=(di == 7)))
                sc.op("vector", lambda e, gb=gb, p0=p0: e.tensor_copy(
                    s2["gu"][:, gb * 512:(gb + 1) * 512], p0[:, 0:512]))
            sc.op("scalar", lambda e: e.activation(
                out=s2["sgs"], in_=s2["gu"][:, 0:SF], func=AF.Silu))
            sc.op("vector", lambda e, tt=tt: e.tensor_tensor(
                out=s2["hs"][:, tt * SF:(tt + 1) * SF], in0=s2["sgs"],
                in1=s2["gu"][:, SF:2 * SF], op=ALU.mult))
            sc.dma(lambda e, o0=o0, tt=tt: e.dma_start(
                out=d["hs_dram"][o0 + tt * 128:o0 + (tt + 1) * 128, :],
                in_=s2["hs"][:, tt * SF:(tt + 1) * SF]))
        for fc in range(6):
            sc.dma(lambda e, o0=o0, fc=fc: e.dma_start(
                out=s2["hsfm"][:, fc * 512:(fc + 1) * 512],
                in_=d["hs_dram"][o0:o0 + 512, fc * 128:(fc + 1) * 128],
                transpose=True))
        for tt in range(4):
            i = bo * 4 + tt
            sc.op("vector", lambda e: e.memset(s2["accv"], 0.0))
            for k in range(8):
                sc.op("vector", lambda e, i=i, k=k: e.tensor_copy(
                    s2["idxi"][:, 1:2], s2["slot8"][:, i * 8 + k:i * 8 + k + 1]))
                sc.op("vector", lambda e: e.memset(s2["ygath"], 0.0))
                sc.dma(lambda e: e.indirect_dma_start(
                    out=s2["ygath"], out_offset=None, in_=d["ybuf_dram"],
                    in_offset=IOA(ap=s2["idxi"][:, 1:2], axis=0),
                    bounds_check=sc.REG["b_slots"], oob_is_err=False), q="gpsimd")
                sc.op("vector", lambda e, i=i, k=k: e.scalar_tensor_tensor(
                    out=s2["accv"], in0=s2["ygath"],
                    scalar=s2["w8"][:, i * 8 + k:i * 8 + k + 1],
                    in1=s2["accv"], op0=ALU.mult, op1=ALU.add))
            sc.op("vector", lambda e, i=i: e.tensor_scalar(
                out=s2["idxf"], in0=s2["iotaf"], scalar1=float(i * 128), scalar2=None,
                op0=ALU.add))
            sc.op("vector", lambda e: e.tensor_tensor(
                out=s2["idxf"], in0=s2["idxf"], in1=ownb, op=ALU.add))
            sc.op("vector", lambda e: e.tensor_copy(s2["idxi"][:, 0:1], s2["idxf"]))
            sc.dma(lambda e: e.indirect_dma_start(
                out=s2["y0own"], out_offset=None, in_=d["y0_dram"],
                in_offset=IOA(ap=s2["idxi"][:, 0:1], axis=0),
                bounds_check=sc.REG["b_nt"], oob_is_err=False), q="gpsimd")
            for obk in range(2):
                p1 = ps[1]
                for fc in range(6):
                    lhs = s2["hsfm"][:, fc * 512 + tt * 128: fc * 512 + tt * 128 + 128]
                    sc.op("tensor", lambda e, lhs=lhs, fc=fc, obk=obk, p1=p1: e.matmul(
                        p1[:, 0:512], lhs, s2["wsd"][:, fc * 1024 + obk * 512: fc * 1024 + (obk + 1) * 512],
                        start=(fc == 0), stop=(fc == 5)))
                asl = s2["accv"][:, obk * 512:(obk + 1) * 512]
                sc.op("vector", lambda e, asl=asl, p1=p1: e.tensor_tensor(
                    out=asl, in0=asl, in1=p1[:, 0:512], op=ALU.add))
                ysl = s2["y0own"][:, obk * 512:(obk + 1) * 512]
                sc.op("vector", lambda e, asl=asl, ysl=ysl: e.tensor_tensor(
                    out=ysl, in0=ysl, in1=asl, op=ALU.add))
            sc.dma(lambda e, i=i: e.dma_start(
                out=d["y1_dram"][i * 128:(i + 1) * 128, :], in_=s2["y0own"]))


def build_program(dbg=(), s3only=False):
    nc = bass.Bass("TRN2", target_bir_lowering=False, detect_race_conditions=False,
                   dynamic_dma_scratch_size=32768, num_swdge_queues=4)

    def dram_in(name, shape, dt):
        return nc.dram_tensor(name, shape, dt, kind="ExternalInput").ap()

    taps = []

    def dram_tmp(name, shape, dt):
        t = nc.dram_tensor(name, shape, dt, kind="Internal").ap()
        if name in dbg:
            o = nc.dram_tensor("tap_" + name, shape, dt, kind="ExternalOutput").ap()
            taps.append((o, t))
        return t

    x_tm = dram_in("x_tm", [NT, D], F32)
    Wq = dram_in("Wq", [D, D], F32)
    Wk = dram_in("Wk", [D, D], F32)
    Wv = dram_in("Wv", [D, D], F32)
    Wg = dram_in("Wg", [D, D], F32)
    Wo = dram_in("Wo", [D, D], F32)
    rwT = dram_in("rwT", [D, E0], F32)
    wgate = dram_in("wgate", [E0, D, F0], F32)
    wup = dram_in("wup", [E0, D, F0], F32)
    wdown = dram_in("wdown", [E0, F0, D], F32)
    wsgu = dram_in("wsgu", [D, 2 * SF], F32)
    wsd = dram_in("wsd", [SF, D], F32)
    own_base_f = dram_in("own_base_f", [128, 1], F32)
    W_in = dram_in("W_in", [D, 2 * D_INNER], F32)
    conv_w = dram_in("conv_w", [D_INNER, D_CONV], F32)
    conv_b = dram_in("conv_b", [D_INNER], F32)
    W_x = dram_in("W_x", [D_INNER, 2 * D_STATE + 1], F32)
    W_dt = dram_in("W_dt", [1, D_INNER], F32)
    b_dt = dram_in("b_dt", [D_INNER], F32)
    A_log = dram_in("A_log", [D_INNER, D_STATE], F32)
    D_param = dram_in("D_param", [D_INNER], F32)
    W_out = dram_in("W_out", [D_INNER, D], F32)
    qoff_f = dram_in("qoff_f", [128, 1], F32)
    Wg2 = dram_in("Wg2", [D, E1], F32)
    bg2_b = dram_in("bg2_b", [128, E1], F32)
    W1 = dram_in("W1", [E1, D, H1], F32)
    b1 = dram_in("b1", [E1, H1], F32)
    W2 = dram_in("W2", [E1, H1, D], F32)
    b2 = dram_in("b2", [E1, D], F32)
    masks_c = dram_in("masks_c", [128, H * 128], F32)
    gw_c = dram_in("gw_c", [128, H], F32)
    ident_bf = dram_in("ident_bf", [128, 128], BF16)
    ident_f32 = dram_in("ident_f32", [128, 128], F32)

    y0_dram = dram_tmp("y0_tm", [NT, D], F32)

    out_final = nc.dram_tensor("out", [QT, D], F32, kind="ExternalOutput").ap()

    ln_dram = dram_tmp("ln_dram", [NT, D], BF16)
    gated_dram = dram_tmp("gated_dram", [512, D], BF16)
    logits_dram = dram_tmp("logits_dram", [NT, E0], F32)
    max8_dram = dram_tmp("max8_dram", [NT, 8], F32)
    wtm_dram = dram_tmp("wtm_dram", [NT, E0], F32)
    t8_dram = dram_tmp("t8_dram", [NT, 1], F32)
    slotm_dram = dram_tmp("slotm_dram", [NT, E0], F32)
    ln2own_dram = dram_tmp("ln2own_dram", [OWN, D], BF16)
    xbuf_dram = dram_tmp("xbuf_dram", [E0 * C0, D], BF16)
    ybuf_dram = dram_tmp("ybuf_dram", [E0 * C0, D], BF16)
    hexp_dram = dram_tmp("hexp_dram", [C0, F0], BF16)
    hs_dram = dram_tmp("hs_dram", [OWN, SF], BF16)
    y1_dram = (dram_in("y1_in", [OWN, D], F32) if s3only
               else dram_tmp("y1_tm", [OWN, D], F32))
    xz_dram = dram_tmp("xz_dram", [OWN, 2 * D_INNER], BF16)
    xc_dram = dram_tmp("xc_dram", [D_INNER, OWN], BF16)
    dt_dram = dram_tmp("dt_dram", [D_INNER, OWN], BF16)
    b_dram = dram_tmp("b_dram", [D_STATE, OWN], BF16)
    csum_dram = dram_tmp("csum_dram", [1, OWN], F32)
    ysum_dram = dram_tmp("ysum_dram", [1, OWN], F32)
    ybc_dram = dram_tmp("ybc_dram", [1, OWN], BF16)
    ybcq_dram = dram_tmp("ybcq_dram", [1, QT], BF16)
    zq_dram = dram_tmp("zq_dram", [QT, D_INNER], BF16)
    y1q_dram = dram_tmp("y1q_dram", [QT, D], F32)
    y2_dram = dram_tmp("y2_tm", [QT, D], F32)
    w2bf_dram = dram_tmp("w2bf_dram", [E1, H1, D], BF16)
    acc_dram = dram_tmp("acc_dram", [QT, D], F32)

    sc = Script()
    REG = {}
    sc.REG = REG

    def reg_setup(eng):
        for name, val in (("b_nt", NT - 1), ("b_slots", E0 * C0 - 1),
                          ("b_own", OWN - 1), ("b_qt", QT - 1),
                          ("b_flat", D_INNER * OWN - 1)):
            r = eng.alloc_register(name)
            eng.reg_mov(r, val)
            REG[name] = r


    sc.setup("gpsimd", reg_setup)

    with ExitStack() as stack:
        def sb(name, shape, dt):
            return stack.enter_context(nc.sbuf_tensor(name, shape, dt))

        AF32 = Arena(sb("AF32", [128, 10000], F32), 10000)
        ABF = Arena(sb("ABF", [128, 68000], BF16), 68000)

        ps = [stack.enter_context(nc.psum_tensor(f"ps{i}", [128, 512], F32))
              for i in range(8)]

        ticks = [stack.enter_context(nc.semaphore(f"tick{i}")) for i in range(24)]
        block = stack.enter_context(nc.Block())

        masks_sb = AF32.alloc(H * 128)
        gw_sb = AF32.alloc(H)
        idbf_sb = ABF.alloc(128)
        idf_sb = AF32.alloc(128)

        sc.dma(lambda e: e.dma_start(out=masks_sb, in_=masks_c))
        sc.dma(lambda e: e.dma_start(out=gw_sb, in_=gw_c))
        sc.dma(lambda e: e.dma_start(out=idbf_sb, in_=ident_bf))
        sc.dma(lambda e: e.dma_start(out=idf_sb, in_=ident_f32))

        sq_scr = ABF.alloc(1024)
        stat = AF32.alloc(16)

        AF32.push()
        ABF.push()
        w_sb = {}
        for name, w in [("Wq", Wq), ("Wk", Wk), ("Wv", Wv), ("Wg", Wg), ("Wo", Wo)]:
            t = ABF.alloc(8 * 1024)
            w_sb[name] = t
            sc.dma(lambda e, t=t, w=w: e.dma_start(
                out=t.rearrange("p (c o) -> p c o", c=8),
                in_=w.rearrange("(c p) o -> p c o", p=128)), q="gpsimd",
                tag=("s1w",))

        def wslice(name, cin, lo, n):
            return w_sb[name][:, cin * 1024 + lo: cin * 1024 + lo + n]

        xblk = AF32.alloc(4 * 1024)
        lnblk = ABF.alloc(1024)
        lnfm = ABF.alloc(8 * 512)
        qfm = ABF.alloc(8 * 512)
        kfm = ABF.alloc(8 * 512)
        vtm = ABF.alloc(1024)
        kwtm = ABF.alloc(1024)
        gstm = ABF.alloc(1024)
        retn = ABF.alloc(1024)
        gatedb = retn
        gatedfm = lnfm
        sT_bf = ABF.alloc(128)
        S_f32 = AF32.alloc(8 * 128)
        S_bf = ABF.alloc(8 * 128)

        eps_ln_c = stat[:, 10:11]
        eps_gn_c = stat[:, 11:12]
        sc.op("vector", lambda e: e.memset(eps_ln_c, EPS_LN))
        sc.op("vector", lambda e: e.memset(eps_gn_c, EPS_GN))

        s_col = stat[:, 0:1]
        sq_col = stat[:, 1:2]
        negm = stat[:, 2:3]
        var_c = stat[:, 3:4]
        rstd = stat[:, 4:5]
        m2 = stat[:, 5:6]
        sq2 = stat[:, 6:7]
        negm2 = stat[:, 7:8]
        var2 = stat[:, 8:9]
        rstd2 = stat[:, 9:10]

        def ln_stats_and_norm(xtile, lntile, dcols):
            sc.op("vector", lambda e: e.reduce_sum(out=s_col, in_=xtile, axis=AX.X))
            sc.op("scalar", lambda e: e.activation(out=sq_scr[:, :dcols], in_=xtile,
                                                   func=AF.Square, accum_out=sq_col))
            sc.op("vector", lambda e: e.tensor_scalar(out=negm, in0=s_col,
                                                      scalar1=-1.0 / dcols, scalar2=None,
                                                      op0=ALU.mult))
            sc.op("vector", lambda e: e.tensor_scalar(out=sq2, in0=sq_col,
                                                      scalar1=1.0 / dcols, scalar2=None,
                                                      op0=ALU.mult))
            sc.op("vector", lambda e: e.scalar_tensor_tensor(
                out=var_c, in0=negm, scalar=negm, in1=sq2,
                op0=ALU.mult, op1=ALU.subtract))
            sc.op("vector", lambda e: e.tensor_scalar(out=var_c, in0=var_c,
                                                      scalar1=-1.0, scalar2=None,
                                                      op0=ALU.mult))
            sc.op("scalar", lambda e: e.activation(out=rstd, in_=var_c, func=AF.Sqrt,
                                                   bias=eps_ln_c))
            sc.op("vector", lambda e: e.reciprocal(out=rstd, in_=rstd))
            sc.op("vector", lambda e: e.tensor_scalar(out=lntile, in0=xtile,
                                                      scalar1=negm, scalar2=rstd,
                                                      op0=ALU.add, op1=ALU.mult))

        NBLK = NT // 512
        for blk in range([] if s3only else range(NBLK)) if False else (range(0) if s3only else range(NBLK)):
            t0 = blk * 512
            for tt in range(4):
                sc.dma(lambda e, t0=t0, tt=tt: e.dma_start(
                    out=xblk[:, tt * 1024:(tt + 1) * 1024],
                    in_=x_tm[t0 + tt * 128:t0 + (tt + 1) * 128, :]))
                ln_stats_and_norm(xblk[:, tt * 1024:(tt + 1) * 1024], lnblk, D)
                sc.dma(lambda e, t0=t0, tt=tt: e.dma_start(
                    out=ln_dram[t0 + tt * 128:t0 + (tt + 1) * 128, :], in_=lnblk))
            for dc in range(8):
                sc.dma(lambda e, t0=t0, dc=dc: e.dma_start(
                    out=lnfm[:, dc * 512:(dc + 1) * 512],
                    in_=ln_dram[t0:t0 + 512, dc * 128:(dc + 1) * 128],
                    transpose=True))
            for which, dst in (("Wq", qfm), ("Wk", kfm)):
                for do in range(8):
                    pb = ps[0]
                    for di in range(8):
                        sc.op("tensor", lambda e, which=which, do=do, di=di, pb=pb: e.matmul(
                            pb[:, 0:512], wslice(which, di, do * 128, 128),
                            lnfm[:, di * 512:(di + 1) * 512],
                            start=(di == 0), stop=(di == 7)))
                    sc.op("vector", lambda e, do=do, dst=dst, pb=pb: e.tensor_copy(
                        dst[:, do * 512:(do + 1) * 512], pb[:, 0:512]))
            for tt in range(4):
                for dvb in range(2):
                    pv, pk, pg = ps[1], ps[2], ps[3]
                    for di in range(8):
                        lhs = lnfm[:, di * 512 + tt * 128: di * 512 + tt * 128 + 128]
                        sc.op("tensor", lambda e, lhs=lhs, di=di, dvb=dvb, pv=pv: e.matmul(
                            pv[:, 0:512], lhs, wslice("Wv", di, dvb * 512, 512),
                            start=(di == 0), stop=(di == 7)))
                        sc.op("tensor", lambda e, lhs=lhs, di=di, dvb=dvb, pk=pk: e.matmul(
                            pk[:, 0:512], lhs, wslice("Wk", di, dvb * 512, 512),
                            start=(di == 0), stop=(di == 7)))
                        sc.op("tensor", lambda e, lhs=lhs, di=di, dvb=dvb, pg=pg: e.matmul(
                            pg[:, 0:512], lhs, wslice("Wg", di, dvb * 512, 512),
                            start=(di == 0), stop=(di == 7)))
                    vdst = vtm[:, dvb * 512:(dvb + 1) * 512]
                    sc.op("vector", lambda e, vdst=vdst, pv=pv: e.tensor_copy(vdst, pv[:, 0:512]))
                    for hh in range(4):
                        h = dvb * 4 + hh
                        kdst = kwtm[:, h * 128:(h + 1) * 128]
                        sc.op("vector", lambda e, kdst=kdst, pk=pk, hh=hh, h=h: e.tensor_scalar(
                            out=kdst, in0=pk[:, hh * 128:(hh + 1) * 128],
                            scalar1=gw_sb[:, h:h + 1], scalar2=None, op0=ALU.mult))
                    gdst = gstm[:, dvb * 512:(dvb + 1) * 512]
                    sc.op("scalar", lambda e, gdst=gdst, pg=pg: e.activation(
                        out=gdst, in_=pg[:, 0:512], func=AF.Silu))
                c = blk * 4 + tt
                batch_start = (c % (S // 128) == 0)
                for h in range(H):
                    qsl = qfm[:, h * 512 + tt * 128: h * 512 + tt * 128 + 128]
                    ksl = kfm[:, h * 512 + tt * 128: h * 512 + tt * 128 + 128]
                    vsl = vtm[:, h * 128:(h + 1) * 128]
                    kwsl = kwtm[:, h * 128:(h + 1) * 128]
                    Sf = S_f32[:, h * 128:(h + 1) * 128]
                    Sb = S_bf[:, h * 128:(h + 1) * 128]
                    p_sc, p_ret, p_upd = ps[4], ps[5], ps[6]
                    sc.op("tensor", lambda e, ksl=ksl, qsl=qsl, p_sc=p_sc: e.matmul(
                        p_sc[:, 0:128], ksl, qsl, start=True, stop=True))
                    sc.op("vector", lambda e, h=h, p_sc=p_sc: e.tensor_tensor(
                        out=sT_bf, in0=p_sc[:, 0:128],
                        in1=masks_sb[:, h * 128:(h + 1) * 128], op=ALU.mult))
                    if not batch_start:
                        sc.op("tensor", lambda e, qsl=qsl, Sb=Sb, p_ret=p_ret: e.matmul(
                            p_ret[:, 0:128], qsl, Sb, start=True, stop=False))
                    sc.op("tensor", lambda e, vsl=vsl, p_ret=p_ret, bs=batch_start: e.matmul(
                        p_ret[:, 0:128], sT_bf, vsl, start=bs, stop=True))
                    sc.op("vector", lambda e, p_ret=p_ret: e.reduce_sum(
                        out=m2, in_=p_ret[:, 0:128], axis=AX.X))
                    sc.op("scalar", lambda e, p_ret=p_ret: e.activation(
                        out=sq_scr[:, 0:128], in_=p_ret[:, 0:128], func=AF.Square,
                        accum_out=sq2))
                    sc.op("vector", lambda e: e.tensor_scalar(
                        out=negm2, in0=m2, scalar1=-1.0 / DH, scalar2=None, op0=ALU.mult))
                    sc.op("vector", lambda e: e.tensor_scalar(
                        out=var2, in0=sq2, scalar1=1.0 / DH, scalar2=None, op0=ALU.mult))
                    sc.op("vector", lambda e: e.scalar_tensor_tensor(
                        out=var2, in0=negm2, scalar=negm2, in1=var2,
                        op0=ALU.mult, op1=ALU.subtract))
                    sc.op("vector", lambda e: e.tensor_scalar(
                        out=var2, in0=var2, scalar1=-1.0, scalar2=None, op0=ALU.mult))
                    sc.op("scalar", lambda e: e.activation(out=rstd2, in_=var2,
                                                           func=AF.Sqrt, bias=eps_gn_c))
                    sc.op("vector", lambda e: e.reciprocal(out=rstd2, in_=rstd2))
                    rdst = retn[:, h * 128:(h + 1) * 128]
                    sc.op("vector", lambda e, rdst=rdst, p_ret=p_ret: e.tensor_scalar(
                        out=rdst, in0=p_ret[:, 0:128], scalar1=negm2, scalar2=rstd2,
                        op0=ALU.add, op1=ALU.mult))
                    sc.op("tensor", lambda e, kwsl=kwsl, vsl=vsl, p_upd=p_upd: e.matmul(
                        p_upd[:, 0:128], kwsl, vsl, start=True, stop=True))
                    g128 = GAMMAS[h] ** 128
                    if batch_start:
                        sc.op("vector", lambda e, Sf=Sf, p_upd=p_upd: e.tensor_copy(
                            Sf, p_upd[:, 0:128]))
                    else:
                        sc.op("vector", lambda e, Sf=Sf, p_upd=p_upd, g128=g128:
                              e.scalar_tensor_tensor(out=Sf, in0=Sf, scalar=g128,
                                                     in1=p_upd[:, 0:128],
                                                     op0=ALU.mult, op1=ALU.add))
                    sc.op("vector", lambda e, Sb=Sb, Sf=Sf: e.tensor_copy(Sb, Sf))
                sc.op("vector", lambda e: e.tensor_tensor(
                    out=gatedb, in0=retn, in1=gstm, op=ALU.mult))
                sc.dma(lambda e, tt=tt: e.dma_start(
                    out=gated_dram[tt * 128:(tt + 1) * 128, :], in_=gatedb))
            for dc in range(8):
                sc.dma(lambda e, dc=dc: e.dma_start(
                    out=gatedfm[:, dc * 512:(dc + 1) * 512],
                    in_=gated_dram[:, dc * 128:(dc + 1) * 128],
                    transpose=True))
            for tt in range(4):
                for dvb in range(2):
                    pb = ps[7]
                    for di in range(8):
                        lhs = gatedfm[:, di * 512 + tt * 128: di * 512 + tt * 128 + 128]
                        sc.op("tensor", lambda e, lhs=lhs, di=di, dvb=dvb, pb=pb: e.matmul(
                            pb[:, 0:512], lhs, wslice("Wo", di, dvb * 512, 512),
                            start=(di == 0), stop=(di == 7)))
                    dst = xblk[:, tt * 1024 + dvb * 512: tt * 1024 + (dvb + 1) * 512]
                    sc.op("vector", lambda e, dst=dst, pb=pb: e.tensor_tensor(
                        out=dst, in0=dst, in1=pb[:, 0:512], op=ALU.add))
            for tt in range(4):
                sc.dma(lambda e, t0=t0, tt=tt: e.dma_start(
                    out=y0_dram[t0 + tt * 128:t0 + (tt + 1) * 128, :],
                    in_=xblk[:, tt * 1024:(tt + 1) * 1024]))

        AF32.pop()
        ABF.pop()

        AF32.push()
        ABF.push()
        s2 = {}
        s2["y0blk"] = AF32.alloc(1024)
        s2["ln2blk"] = ABF.alloc(1024)
        s2["lnfm"] = ABF.alloc(8 * 512)
        s2["logits"] = AF32.alloc(4 * 64)
        s2["max8"] = AF32.alloc(8)
        s2["nm1"] = AF32.alloc(1)
        s2["expw"] = AF32.alloc(64)
        s2["den"] = AF32.alloc(1)
        s2["wv"] = AF32.alloc(64)
        s2["em"] = AF32.alloc(512)
        s2["selem"] = AF32.alloc(512)
        s2["scan"] = AF32.alloc(512)
        s2["pos"] = AF32.alloc(512)
        s2["slotm"] = AF32.alloc(512)
        s2["ones64"] = AF32.alloc(512)
        s2["big64"] = AF32.alloc(512)
        s2["kept"] = AF32.alloc(512)
        s2["kepti"] = stack.enter_context(nc.sbuf_tensor("kepti", [128, 512], I32))
        s2["e512"] = AF32.alloc(1)
        s2["cnt"] = AF32.alloc(1)
        s2["t8row"] = AF32.alloc(512)
        s2["ones1"] = AF32.alloc(64)
        s2["iotaf"] = AF32.alloc(1)
        s2["idxf"] = AF32.alloc(1)
        s2["idxi"] = stack.enter_context(nc.sbuf_tensor("idxi", [128, 8], I32))
        s2["lnown"] = ABF.alloc(1024)
        s2["lgown"] = AF32.alloc(64)
        s2["m8own"] = AF32.alloc(8)
        s2["wown"] = AF32.alloc(64)
        s2["smown"] = AF32.alloc(64)
        s2["oh"] = AF32.alloc(64)
        s2["ohs"] = AF32.alloc(64)
        s2["slot8"] = AF32.alloc(64 * 8)
        s2["w8"] = AF32.alloc(64 * 8)
        s2["zero8k"] = ABF.alloc(2048)
        s2["accv"] = AF32.alloc(1024)
        s2["ownb"] = AF32.alloc(1)
        s2["ygath"] = ABF.alloc(1024)
        s2["y0own"] = s2["y0blk"]
        s2["rw_sb"] = ABF.alloc(8 * 64)
        s2["wg"] = ABF.alloc(8 * F0)
        s2["wu"] = ABF.alloc(8 * F0)
        s2["wd"] = ABF.alloc(6 * 1024)
        s2["xfm"] = ABF.alloc(8 * 512)
        s2["hexp"] = ABF.alloc(4 * F0)
        s2["hfm"] = ABF.alloc(6 * 512)
        s2["ytile"] = ABF.alloc(1024)
        s2["siluscr"] = ABF.alloc(512)
        s2["wsgu"] = ABF.alloc(8 * 2 * SF)
        s2["wsd"] = ABF.alloc(6 * 1024)
        s2["gu"] = ABF.alloc(1536)
        s2["sgs"] = ABF.alloc(SF)
        s2["hs"] = ABF.alloc(4 * SF)
        s2["hsfm"] = ABF.alloc(6 * 512)

        if not s3only:
            emit_stage2(sc, ps, s2, dict(
                ln_dram=ln_dram, y0_dram=y0_dram, rwT=rwT, wgate=wgate, wup=wup,
                wdown=wdown, wsgu=wsgu, wsd=wsd, own_base_f=own_base_f,
                logits_dram=logits_dram, max8_dram=max8_dram, wtm_dram=wtm_dram,
                t8_dram=t8_dram, slotm_dram=slotm_dram, ln2own_dram=ln2own_dram,
                xbuf_dram=xbuf_dram, ybuf_dram=ybuf_dram, hexp_dram=hexp_dram,
                hs_dram=hs_dram, y1_dram=y1_dram, idf_sb=idf_sb,
            ), ln_stats_and_norm, stat, eps_ln_c)

        AF32.pop()
        ABF.pop()

        def xsb(name, shape, dt):
            return stack.enter_context(nc.sbuf_tensor(name, shape, dt))

        emit_stage3(sc, ps, AF32, ABF, xsb, dict(
            y1_dram=y1_dram, ln_dram=ln_dram, W_in=W_in, conv_w=conv_w,
            conv_b=conv_b, W_x=W_x, W_dt=W_dt, b_dt=b_dt, A_log=A_log,
            D_param=D_param, W_out=W_out, qoff_f=qoff_f,
            xz_dram=xz_dram, xc_dram=xc_dram, dt_dram=dt_dram, b_dram=b_dram,
            csum_dram=csum_dram, ysum_dram=ysum_dram, ybc_dram=ybc_dram,
            zq_dram=zq_dram, y1q_dram=y1q_dram, y2_dram=y2_dram,
            ybcq_dram=ybcq_dram,
        ), ln_stats_and_norm)

        emit_stage4(sc, ps, AF32, ABF, dict(
            y2_dram=y2_dram, ln_dram=ln_dram, Wg2=Wg2, bg2_b=bg2_b, W1=W1, b1=b1,
            W2=W2, b2=b2, w2bf_dram=w2bf_dram, acc_dram=acc_dram,
            out_final=out_final, idf_sb=idf_sb,
        ), ln_stats_and_norm)

        if taps:
            tap_scr_bf = ABF.alloc(8192)
            tap_scr_f = AF32.alloc(2048)
            for o_, t_ in taps:
                if len(t_.shape) == 3:
                    a_, b_, c_ = t_.shape
                    t_ = t_.rearrange("a b c -> (a b) c")
                    o_ = o_.rearrange("a b c -> (a b) c")
                rows, width = t_.shape
                scr = tap_scr_bf if t_.dtype == BF16 else tap_scr_f
                if rows % 128 == 0 and width <= scr.shape[1]:
                    for rr in range(rows // 128):
                        sc.dma(lambda e, t_=t_, rr=rr, width=width, scr=scr: e.dma_start(
                            out=scr[:, 0:width], in_=t_[rr * 128:(rr + 1) * 128, :]))
                        sc.dma(lambda e, o_=o_, rr=rr, width=width, scr=scr: e.dma_start(
                            out=o_[rr * 128:(rr + 1) * 128, :], in_=scr[:, 0:width]))
                else:
                    sc.dma(lambda e, o_=o_, t_=t_: e.dma_start(out=o_, in_=t_))

        if "y0_tm" in dbg or "y1_tm" in dbg:
            sc.dma(lambda e: e.dma_start(
                out=out_final[0:128, :], in_=s2["y0blk"][:, 0:1024]))

        sc.emit(block, ticks)

    return nc


def make_consts():
    t = np.arange(128, dtype=np.float64)
    masks = np.zeros((128, H * 128), dtype=np.float32)
    gw = np.zeros((128, H), dtype=np.float32)
    for h in range(H):
        g = GAMMAS[h]
        colv = np.power(g, -(t + 1.0)) / np.sqrt(DH)
        m = np.where(t[:, None] <= t[None, :], colv[:, None], 0.0)
        masks[:, h * 128:(h + 1) * 128] = m.astype(np.float32)
        gw[:, h] = (np.power(g, 127.0 - t) / np.sqrt(DH)).astype(np.float32)
    ident = np.eye(128, dtype=np.float32)
    return dict(masks_c=masks, gw_c=gw,
                ident_bf=ident.astype(ml_dtypes.bfloat16), ident_f32=ident)


def kernel(**inputs):
    z = inputs
    x_tm = np.ascontiguousarray(z["L0_attn_X"].reshape(-1, D).astype(np.float32))
    consts = make_consts()
    common = dict(
        x_tm=x_tm,
        Wq=z["L0_attn_Wq"], Wk=z["L0_attn_Wk"], Wv=z["L0_attn_Wv"],
        Wg=z["L0_attn_W_G"], Wo=z["L0_attn_W_O"],
        rwT=np.ascontiguousarray(z["L0_ffn_router_w"].T),
        wgate=z["L0_ffn_w_gate"], wup=z["L0_ffn_w_up"], wdown=z["L0_ffn_w_down"],
        wsgu=z["L0_ffn_ws_gate_up"], wsd=z["L0_ffn_ws_down"],
        W_in=z["L1_attn_W_in"], conv_w=z["L1_attn_conv_w"],
        conv_b=z["L1_attn_conv_b"], W_x=z["L1_attn_W_x"], W_dt=z["L1_attn_W_dt"],
        b_dt=z["L1_attn_b_dt"], A_log=z["L1_attn_A_log"],
        D_param=z["L1_attn_D_param"], W_out=z["L1_attn_W_out"],
        Wg2=z["L1_ffn_Wg"],
        bg2_b=np.ascontiguousarray(np.broadcast_to(z["L1_ffn_bg"], (128, E1))),
        W1=z["L1_ffn_W1"], b1=z["L1_ffn_b1"], W2=z["L1_ffn_W2"], b2=z["L1_ffn_b2"],
        **consts)
    in_maps = []
    for c in range(8):
        b, q = c // 4, c % 4
        m = dict(common)
        m["own_base_f"] = np.full((128, 1), float(b * S), np.float32)
        m["qoff_f"] = np.full((128, 1), float(q * QT), np.float32)
        in_maps.append(m)
    nc = build_program()
    res = run_bass_kernel_spmd(nc, in_maps, core_ids=list(range(8)))
    out = np.empty((B, S, D), np.float32)
    for c in range(8):
        b, q = c // 4, c % 4
        out[b, q * QT:(q + 1) * QT, :] = np.asarray(res.results[c]["out"])
    return out


if __name__ == "__main__":
    pass
